# version 07
# baseline (speedup 1.0000x reference)
import numpy as np
import ml_dtypes
from contextlib import ExitStack

import concourse.bass as bass
import concourse.mybir as mybir
from concourse.bass_utils import run_bass_kernel_spmd

F32 = mybir.dt.float32
BF16 = mybir.dt.bfloat16
AF = mybir.ActivationFunctionType
ALU = mybir.AluOpType
AX = mybir.AxisListType
BBF16 = ml_dtypes.bfloat16

NCORES = 8
TPC = 2048
D = 1024
H = 16
DKH = 64
E = 8
DF = 4096
FC = DF // 128
DC = D // 128
TT = 1024


ENGS = ("pe", "act", "dve", "pool", "sp")
DMA_QUEUES = {"sp": "sp", "act": "act", "pool": "pool"}
N_DMA_SEMS = {"sp": 6, "act": 4, "pool": 10}


def _akey(ap):
    try:
        return ap.name
    except Exception:
        return str(ap)


class Prog:
    def __init__(self, tag="k"):
        self.nc = bass.Bass("TRN2", target_bir_lowering=False, debug=False)
        self.ctx = ExitStack()
        self.tag = tag
        self.ops = []
        self.sems = {}
        self.count = {}
        for e in ENGS:
            self._mksem("e_" + e)
        self.dmapool = {}
        self.dmaidx = {q: 0 for q in DMA_QUEUES}
        for q, n in N_DMA_SEMS.items():
            self.dmapool[q] = [self._mksem(f"d_{q}{i}") for i in range(n)]
        self.writers = {}
        self.readers = {}
        self.waited = {e: {} for e in ENGS}
        self.sbufs = {}
        self.psums = {}
        self.drams = {}
        self._finalized = False
        for _nm, _shape, _dt in (("identbf", [128, 128], BF16),
                                 ("identf", [128, 128], F32),
                                 ("ones1", [1, 128], BF16)):
            self.sbufs[_nm] = self.ctx.enter_context(self.nc.sbuf_tensor(_nm, _shape, _dt))
        self.PERSIST = set(self.sbufs)
        self.arena_base = (self.nc.sbuf_base + 31) // 32 * 32
        self.arena_top = self.nc.sbuf_top
        self.arena_cur = 0
        self.arena_peak = 0

    def _mksem(self, name):
        h = self.ctx.enter_context(self.nc.semaphore(name))
        self.sems[name] = h
        self.count[name] = 0
        return name

    def sb(self, name, shape, dt):
        if name not in self.sbufs:
            per = 1
            for s in shape[1:]:
                per *= s
            per = (per * mybir.dt.size(dt) + 31) // 32 * 32
            off = self.arena_base + self.arena_cur
            assert off + per <= self.arena_top, (
                f"SBUF arena overflow: {name} at {off}+{per} > {self.arena_top}")
            self.sbufs[name] = self.nc.alloc_sbuf_tensor_at(name, list(shape), dt, offset=off)
            self.arena_cur += per
            self.arena_peak = max(self.arena_peak, self.arena_cur)
        return self.sbufs[name]

    def epoch(self):
        self.barrier()
        self.arena_cur = 0
        for k in list(self.sbufs):
            if k not in self.PERSIST:
                del self.sbufs[k]

    def ps(self, name, shape, dt=F32):
        if name not in self.psums:
            self.psums[name] = self.ctx.enter_context(self.nc.psum_tensor(name, list(shape), dt))
        return self.psums[name]

    def dram_in(self, name, shape, dt=F32):
        if name not in self.drams:
            self.drams[name] = self.nc.dram_tensor(name, list(shape), dt, kind="ExternalInput").ap()
        return self.drams[name]

    def dram_out(self, name, shape, dt=F32):
        if name not in self.drams:
            self.drams[name] = self.nc.dram_tensor(name, list(shape), dt, kind="ExternalOutput").ap()
        return self.drams[name]

    def dram_tmp(self, name, shape, dt=F32):
        if name not in self.drams:
            self.drams[name] = self.nc.dram_tensor(name, list(shape), dt, kind="Internal").ap()
        return self.drams[name]

    def _keys(self, ap):
        nm = _akey(ap)
        try:
            t = ap.tensor
            shape = list(t.shape)
            if len(shape) != 2:
                return [nm]
            rowbytes = shape[1] * mybir.dt.size(t.dtype)
            if rowbytes < 4096:
                return [nm]
        except Exception:
            return [nm]
        nb = (rowbytes + 2047) // 2048
        try:
            es = mybir.dt.size(ap.dtype)
            col0 = (ap.offset * es) % rowbytes
            span = es
            for stride, n in ap.ap:
                sb = stride * es
                if sb == rowbytes:
                    continue
                span += abs(sb) * (n - 1)
            b0 = col0 // 2048
            b1 = (col0 + span - 1) // 2048
            if b1 >= nb or (b1 - b0) > 31:
                b0, b1 = 0, nb - 1
        except Exception:
            b0, b1 = 0, nb - 1
        return [f"{nm}.{b}" for b in range(b0, b1 + 1)]

    def _record(self, eng, emit, reads, writes, opname="", dma_q=None):
        waits = {}

        def need(tok):
            if tok is None:
                return
            s, v = tok
            if eng == "pe" and s == "e_pe":
                return
            if self.waited[eng].get(s, 0) >= v:
                return
            waits[s] = max(waits.get(s, 0), v)

        rkeys = [k for a in reads if a is not None for k in self._keys(a)]
        wkeys = [k for a in writes if a is not None for k in self._keys(a)]
        psum_rkeys = [k for a in reads
                      if a is not None and _akey(a) in self.psums
                      for k in self._keys(a)]
        for k in rkeys:
            need(self.writers.get(k))
        own = "e_" + eng
        for k in psum_rkeys:
            for s, v in self.readers.get(k, {}).items():
                if s != own:
                    need((s, v))
        for k in wkeys:
            need(self.writers.get(k))
            for s, v in self.readers.get(k, {}).items():
                need((s, v))
        if dma_q is not None:
            pool = self.dmapool[dma_q]
            sname = pool[self.dmaidx[dma_q] % len(pool)]
            self.dmaidx[dma_q] += 1
            need((sname, self.count[sname]))
            self.count[sname] += 16
            tok = (sname, self.count[sname])
            inc = (sname, 16)
        else:
            sname = "e_" + eng
            self.count[sname] += 1
            tok = (sname, self.count[sname])
            inc = (sname, 1)
        for k in rkeys:
            self.readers.setdefault(k, {})[tok[0]] = tok[1]
        for k in wkeys:
            self.writers[k] = tok
            self.readers[k] = {tok[0]: tok[1]}
        for s, v in waits.items():
            self.waited[eng][s] = max(self.waited[eng].get(s, 0), v)
        self.ops.append(dict(eng=eng, waits=sorted(waits.items()), emit=emit, inc=inc, name=opname))
        return tok

    def barrier(self):
        cur = [(s, v) for s, v in self.count.items() if v > 0]
        for e in ENGS:
            ws = [(s, v) for s, v in cur if self.waited[e].get(s, 0) < v and s != "e_" + e]
            for s, v in ws:
                self.waited[e][s] = v
            if ws:
                self.ops.append(dict(eng=e, waits=sorted(ws), emit=None, inc=None, name="barrier"))

    def dma(self, dst, src, q="pool", name="dma"):
        eng = DMA_QUEUES[q]
        return self._record(eng, lambda E_: E_.dma_start(out=dst, in_=src),
                            reads=[src], writes=[dst], opname=name, dma_q=q)

    def mm(self, out, pairs, name="mm"):
        reads = [a for pr in pairs for a in pr]
        n = len(pairs)

        def emit(E_):
            last = None
            for i, (l, r) in enumerate(pairs):
                last = E_.matmul(out, l, r, start=(i == 0), stop=(i == n - 1))
            return last
        return self._record("pe", emit, reads=reads, writes=[out], opname=name)

    def mmx(self, groups, reads, writes, name="mmx"):
        def emit(E_):
            last = None
            for (o, l, r, st, sp) in groups:
                last = E_.matmul(o, l, r, start=st, stop=sp)
            return last
        return self._record("pe", emit, reads=reads, writes=writes, opname=name)

    def mm1(self, out, lhsT, rhs, start, stop, name="mm1"):
        return self._record("pe", lambda E_: E_.matmul(out, lhsT, rhs, start=start, stop=stop),
                            reads=[lhsT, rhs], writes=[out], opname=name)

    def trans(self, out, in_, ident, name="tr"):
        return self._record("pe", lambda E_: E_.transpose(out, in_, ident),
                            reads=[in_, ident], writes=[out], opname=name)

    def act(self, out, in_, func, bias=0.0, scale=1.0, accum_out=None, name="act"):
        reads = [in_] + [x for x in (bias, scale) if isinstance(x, bass.AP)]
        writes = [out] + ([accum_out] if accum_out is not None else [])

        def emit(E_):
            return E_.activation(out, in_, func, bias=bias, scale=scale, accum_out=accum_out)
        return self._record("act", emit, reads=reads, writes=writes, opname=name)

    def tt(self, out, a, b, op, eng="dve", name="tt"):
        return self._record(eng, lambda E_: E_.tensor_tensor(out, a, b, op=op),
                            reads=[a, b], writes=[out], opname=name)

    def ts(self, out, in_, s1, s2=None, op0=ALU.mult, op1=None, eng="dve", name="ts"):
        reads = [in_] + [s for s in (s1, s2) if isinstance(s, bass.AP)]

        def emit(E_):
            if op1 is None:
                return E_.tensor_single_scalar(out, in_, s1, op=op0)
            return E_.tensor_scalar(out, in_, s1, s2, op0=op0, op1=op1)
        return self._record(eng, emit, reads=reads, writes=[out], opname=name)

    def stt(self, out, in0, scalar, in1, op0, op1, eng="dve", name="stt"):
        reads = [in0, in1] + ([scalar] if isinstance(scalar, bass.AP) else [])
        return self._record(eng, lambda E_: E_.scalar_tensor_tensor(out, in0, scalar, in1, op0=op0, op1=op1),
                            reads=reads, writes=[out], opname=name)

    def red(self, out, in_, op=ALU.add, name="red"):
        if op == ALU.add:
            return self._record("dve", lambda E_: E_.reduce_sum(out, in_, axis=AX.X),
                                reads=[in_], writes=[out], opname=name)
        if op == ALU.max:
            return self._record("dve", lambda E_: E_.reduce_max(out, in_, axis=AX.X),
                                reads=[in_], writes=[out], opname=name)
        return self._record("dve", lambda E_: E_.tensor_reduce(out, in_, axis=AX.X, op=op),
                            reads=[in_], writes=[out], opname=name)

    def copy(self, out, in_, eng="dve", name="cp"):
        if eng == "act":
            return self._record("act", lambda E_: E_.copy(out=out, in_=in_),
                                reads=[in_], writes=[out], opname=name)
        return self._record(eng, lambda E_: E_.tensor_copy(out, in_),
                            reads=[in_], writes=[out], opname=name)

    def recip(self, out, in_, name="recip"):
        return self._record("dve", lambda E_: E_.reciprocal(out, in_),
                            reads=[in_], writes=[out], opname=name)

    def memset(self, ap, v, eng="dve", name="memset"):
        return self._record(eng, lambda E_: E_.memset(ap, v), reads=[], writes=[ap], opname=name)

    def dbg(self, name, ap, q="sp"):
        import os as _os
        if not _os.environ.get("KDBG"):
            return
        shp = list(ap.shape)
        d = self.dram_out("dbg_" + name, shp, ap.dtype)
        self.dma(d[tuple(slice(0, s) for s in shp)], ap, q=q, name="dbg_" + name)

    def collective(self, kind, op, ins, outs, replica_groups, name="cc"):
        def emit(E_):
            return E_.collective_compute(kind, op, ins=ins, outs=outs, replica_groups=replica_groups)
        return self._record("pool", emit, reads=ins, writes=outs, opname=name, dma_q="pool")

    def truncate(self, n):
        self.ops = self.ops[:n]
        cnt = {s: 0 for s in self.count}
        for op in self.ops:
            if op["inc"] is not None:
                s, amt = op["inc"]
                cnt[s] += amt
        self.count = cnt

    def finalize(self):
        assert not self._finalized
        self._finalized = True
        import os as _os
        if _os.environ.get("TRUNC_OPS"):
            self.truncate(int(_os.environ["TRUNC_OPS"]))
        final = [(s, v) for s, v in self.count.items() if v > 0]
        nc = self.nc
        byeng = {e: [] for e in ENGS}
        for op in self.ops:
            byeng[op["eng"]].append(op)
        block = self.ctx.enter_context(nc.Block())

        def run_section(eng_name, handle):
            for op in byeng[eng_name]:
                for s, v in op["waits"]:
                    handle.wait_ge(self.sems[s], v)
                if op["emit"] is not None:
                    inst = op["emit"](handle)
                    if op["inc"] is not None:
                        s, amt = op["inc"]
                        inst.then_inc(self.sems[s], amt)

        @block.sync
        def _(sp):
            run_section("sp", sp)

        @block.tensor
        def _(pe):
            run_section("pe", pe)

        @block.vector
        def _(dve):
            run_section("dve", dve)

        @block.scalar
        def _(act):
            run_section("act", act)

        @block.gpsimd
        def _(pool):
            run_section("pool", pool)
            for s, v in final:
                pool.wait_ge(self.sems[s], v)

        self.ctx.close()
        return nc



def pack_w1_from_df_d(w1_df_d, nfc=FC):
    w = w1_df_d.reshape(FC, 128, DC, 128)
    return np.ascontiguousarray(w.transpose(0, 3, 2, 1)[:nfc])


def pack_w1_from_d_df(w1_d_df, nfc=FC):
    w = w1_d_df.reshape(DC, 128, FC, 128)
    return np.ascontiguousarray(w.transpose(2, 1, 0, 3)[:nfc])


def pack_w2_from_d_df(w2_d_df, nfc=FC):
    w = w2_d_df.reshape(D, FC, 128)
    return np.ascontiguousarray(w.transpose(1, 2, 0)[:nfc])


def pack_w2_from_df_d(w2_df_d, nfc=FC):
    return np.ascontiguousarray(w2_df_d.reshape(FC, 128, D)[:nfc])


def pack_dvec(v):
    return np.ascontiguousarray(v.reshape(DC, 128).T)


def pack_proj(w_d_o):
    dlen, olen = w_d_o.shape
    w = w_d_o.reshape(dlen // 128, 128, olen)
    return np.ascontiguousarray(w.transpose(1, 0, 2))



def ln_stats_apply(p, x_ap, out_ap, scr, tag):
    st = scr["st"]
    mu, var, rs, m2 = st[:, 0:1], st[:, 1:2], st[:, 2:3], st[:, 3:4]
    p.red(mu, x_ap, ALU.add, name=f"{tag}_sum")
    p.ts(mu, mu, 1.0 / D, name=f"{tag}_mu")
    p.act(scr["junk"][:, :], x_ap, AF.Square, accum_out=var, name=f"{tag}_sq")
    p.ts(var, var, 1.0 / D, 1e-5, op0=ALU.mult, op1=ALU.add, name=f"{tag}_var")
    p.tt(m2, mu, mu, ALU.mult, name=f"{tag}_m2")
    p.tt(var, var, m2, ALU.subtract, name=f"{tag}_vc")
    p.recip(rs, var, name=f"{tag}_rcp")
    p.act(rs, rs, AF.Sqrt, name=f"{tag}_rstd")
    p.ts(out_ap, x_ap, mu, rs, op0=ALU.subtract, op1=ALU.mult, name=f"{tag}_app")


def build_ffn_layer(p, layer, x_dram_in, x_dram_out, n_tt=2, n_exp=E, nfc=FC):
    L = f"L{layer}f"
    has_shared = (layer == 1)
    eps_rt = 1e-8 if layer == 0 else 1e-9
    n_units = n_exp + (1 if has_shared else 0)

    w1b = p.dram_in(f"{L}_w1b", [n_units, nfc, 128, DC * 128], F32)
    w2b = p.dram_in(f"{L}_w2b", [n_units, nfc, 128, D], F32)
    b1b = p.dram_in(f"{L}_b1b", [128, n_units * nfc], F32)
    b2b = p.dram_in(f"{L}_b2b", [E, D], F32)
    b2sh = p.dram_in(f"{L}_b2sh", [1, D], F32)
    wgr = p.dram_in(f"{L}_wgr", [128, DC * E], F32)
    bgr = p.dram_in(f"{L}_bgr", [1, E], F32)
    if layer == 0:
        gb = p.dram_in(f"{L}_gb", [128, 2 * DC], F32)
    identc = p.dram_in("identc", [128, 128], F32)

    x_tm = p.sb(f"{L}_x", [128, 8 * D], F32)
    xnfm = p.sb(f"{L}_xnfm", [128, DC * TT], BF16)
    hbuf = p.sb(f"{L}_h", [128, nfc * TT], BF16)
    w1sl = [p.sb(f"{L}_w1s{i}", [128, DC * 128], BF16) for i in range(3)]
    SG = 8
    n_grp = (nfc + SG - 1) // SG
    w2sl = [p.sb(f"{L}_w2s{i}", [128, SG * D], BF16) for i in range(2)]
    b1sb = p.sb(f"{L}_b1sb", [128, n_units * nfc], F32)
    b2sb = p.sb(f"{L}_b2sb", [E, D], BF16)
    b2shs = p.sb(f"{L}_b2shs", [1, D], BF16)
    wgr_sb = p.sb(f"{L}_wgrs", [128, DC * E], BF16)
    bgr_sb = p.sb(f"{L}_bgrs", [1, E], BF16)
    w_tm = p.sb(f"{L}_wtm", [128, 8 * E], F32)
    wT_sb = p.sb(f"{L}_wT", [E, 8 * 128], BF16)
    ident_bf = p.sb("identbf", [128, 128], BF16)
    ident_f = p.sb("identf", [128, 128], F32)
    scr = dict(
        junk=p.sb(f"{L}_junk", [128, D], BF16),
        st=p.sb(f"{L}_st", [128, 16], F32),
        prob=p.sb(f"{L}_prob", [128, E], F32),
        prob2=p.sb(f"{L}_prob2", [128, E], F32),
        mask=p.sb(f"{L}_mask", [128, E], F32),
        mask2=p.sb(f"{L}_mask2", [128, E], F32),
        ones1=p.sb("ones1", [1, 128], BF16),
    )
    xn_scr = p.sb(f"{L}_xnscr", [128, D], BF16)
    st8 = p.sb(f"{L}_st8", [128, 80], F32)
    r64 = p.sb(f"{L}_r64", [128, 256], F32)
    if layer == 0:
        gb_sb = p.sb(f"{L}_gbsb", [128, 2 * DC], F32)

    pAB = p.ps("pAB", [128, 2048], F32)
    pCD = p.ps("pCD", [128, 2048], F32)
    pQ = [pAB[:, 0:1024], pAB[:, 1024:2048], pCD[:, 0:1024], pCD[:, 1024:2048]]
    pA, pB, pC, pD = pQ

    p.dma(ident_bf[:, :], identc[:, :], q="pool", name="ld_identbf")
    p.dma(ident_f[:, :], identc[:, :], q="sp", name="ld_identf")
    p.memset(scr["ones1"][0:1, :], 1.0, eng="dve")
    p.dma(wgr_sb[:, :], wgr[:, :], q="pool", name="ld_wgr")
    p.dma(bgr_sb[0:1, :], bgr[:, :], q="pool", name="ld_bgr")
    p.dma(b2sb[0:E, :], b2b[:, :], q="pool", name="ld_b2")
    p.dma(b2shs[0:1, :], b2sh[:, :], q="pool", name="ld_b2sh")
    p.dma(b1sb[:, :], b1b[:, :], q="pool", name="ld_b1")
    if layer == 0:
        p.dma(gb_sb[:, :], gb[:, :], q="sp", name="ld_gb")

    for tt in range(n_tt):
        Lt = f"{L}t{tt}"
        p.dma(x_tm[:, :].rearrange("p (tc d) -> p tc d", tc=8),
              x_dram_in[tt * TT:(tt + 1) * TT, :].rearrange("(tc p) d -> p tc d", p=128),
              q="sp", name=f"{Lt}_ldx")
        hbF = hbuf[:, :].bitcast(F32)
        x3v = x_tm[:, :].rearrange("p (tc d) -> p tc d", tc=8)
        mu8, var8, rs8, m28 = st8[:, 0:8], st8[:, 8:16], st8[:, 16:24], st8[:, 24:32]
        xnv = hbuf[:, 16384:24576].rearrange("p (tc d) -> p tc d", tc=8)
        if layer == 0:
            p.red(mu8, x3v, ALU.add, name=f"{Lt}lsum")
            p.ts(mu8, mu8, 1.0 / D, name=f"{Lt}lmu")
            p.act(hbF[:, 0:8192], x_tm[:, :], AF.Square, name=f"{Lt}lsq")
            p.red(var8, hbF[:, 0:8192].rearrange("p (tc d) -> p tc d", tc=8), ALU.add,
                  name=f"{Lt}lvs")
            p.ts(var8, var8, 1.0 / D, 1e-5, op0=ALU.mult, op1=ALU.add, name=f"{Lt}lvar")
            p.tt(m28, mu8, mu8, ALU.mult, name=f"{Lt}lm2")
            p.tt(var8, var8, m28, ALU.subtract, name=f"{Lt}lvc")
            p.recip(rs8, var8, name=f"{Lt}lrc")
            p.act(rs8, rs8, AF.Sqrt, name=f"{Lt}lrs")
            p.tt(xnv, x3v, mu8.unsqueeze(2).broadcast_to([128, 8, D]), ALU.subtract,
                 name=f"{Lt}lap")
            p.tt(xnv, xnv, rs8.unsqueeze(2).broadcast_to([128, 8, D]), ALU.mult,
                 name=f"{Lt}lam")
        else:
            p.copy(hbuf[:, 16384:24576], x_tm[:, :], name=f"{Lt}cast")
        pTb = pD[:, :].bitcast(BF16)
        for tc in range(8):
            for dc in range(DC):
                p.trans(pTb[:, dc * 128:(dc + 1) * 128],
                        hbuf[:, 16384 + tc * D + dc * 128: 16384 + tc * D + (dc + 1) * 128],
                        ident_bf[:, :], name=f"{Lt}tr{tc}_{dc}")
            dstv = xnfm[:, 0:8192].rearrange("p (dc t) -> p dc t", dc=8)[:, :, tc * 128:(tc + 1) * 128]
            srcv = pTb[:, 0:1024].rearrange("p (dc t) -> p dc t", dc=8)
            if layer == 0:
                p.tt(dstv, srcv,
                     gb_sb[:, 0:DC].unsqueeze(2).broadcast_to([128, DC, 128]),
                     ALU.mult, name=f"{Lt}gbm{tc}")
                p.tt(dstv, dstv,
                     gb_sb[:, DC:2 * DC].unsqueeze(2).broadcast_to([128, DC, 128]),
                     ALU.add, name=f"{Lt}gba{tc}")
            else:
                p.copy(dstv, srcv, name=f"{Lt}sc{tc}")
        for tc in range(8):
            ppr = pA[:, tc * E:(tc + 1) * E]
            pairs = [(xnfm[:, dc * TT + tc * 128: dc * TT + tc * 128 + 128],
                      wgr_sb[:, dc * E:(dc + 1) * E]) for dc in range(DC)]
            pairs.append((scr["ones1"][0:1, :], bgr_sb[0:1, :]))
            p.mm(ppr, pairs, name=f"{Lt}rlog{tc}")
        rm8, rsum8 = st8[:, 32:40], st8[:, 40:48]
        m18, m28r, den8 = st8[:, 48:56], st8[:, 56:64], st8[:, 64:72]
        pv, p2v = r64[:, 0:64], r64[:, 64:128]
        mkv, mk2v = r64[:, 128:192], r64[:, 192:256]
        r3 = lambda ap: ap.rearrange("p (tc e) -> p tc e", tc=8)
        bc8 = lambda ap: ap.unsqueeze(2).broadcast_to([128, 8, E])
        p.red(rm8, r3(pA[:, 0:64]), ALU.max, name=f"{Lt}rmax")
        p.tt(r3(pv), r3(pA[:, 0:64]), bc8(rm8), ALU.subtract, name=f"{Lt}rsub")
        p.act(pv, pv, AF.Exp, name=f"{Lt}rexp")
        p.red(rsum8, r3(pv), ALU.add, name=f"{Lt}rsum")
        p.recip(rsum8, rsum8, name=f"{Lt}rrcp")
        p.tt(r3(pv), r3(pv), bc8(rsum8), ALU.mult, name=f"{Lt}rnrm")
        p.red(m18, r3(pv), ALU.max, name=f"{Lt}m1")
        p.tt(r3(mkv), r3(pv), bc8(m18), ALU.is_ge, name=f"{Lt}mk1")
        p.tt(p2v, pv, mkv, ALU.mult, name=f"{Lt}pm")
        p.tt(p2v, pv, p2v, ALU.subtract, name=f"{Lt}p2")
        p.red(m28r, r3(p2v), ALU.max, name=f"{Lt}m2")
        p.tt(r3(mk2v), r3(p2v), bc8(m28r), ALU.is_ge, name=f"{Lt}mk2")
        p.tt(den8, m18, m28r, ALU.add, name=f"{Lt}dsum")
        p.ts(den8, den8, eps_rt, op0=ALU.add, name=f"{Lt}deps")
        p.recip(den8, den8, name=f"{Lt}drcp")
        p.tt(mkv, mkv, mk2v, ALU.add, name=f"{Lt}msum")
        p.tt(w_tm[:, 0:64], pv, mkv, ALU.mult, name=f"{Lt}pw")
        p.tt(r3(w_tm[:, 0:64]), r3(w_tm[:, 0:64]), bc8(den8), ALU.mult, name=f"{Lt}w")
        for tc in range(8):
            p.trans(pB[0:E, tc * 128:(tc + 1) * 128], w_tm[:, tc * E:(tc + 1) * E],
                    ident_f[:, :], name=f"{Lt}wT{tc}")
        p.copy(wT_sb[0:E, 0:1024], pB[0:E, 0:1024], eng="act", name=f"{Lt}wTe")
        if layer == 0:
            p.ts(x_tm[:, :], x_tm[:, :], 2.0, name=f"{Lt}x2")
        for tc in range(8):
            for dh in range(2):
                ps_ = pQ[tc % 2][:, dh * 512:(dh + 1) * 512]
                pairs = [(wT_sb[0:E, tc * 128:(tc + 1) * 128], b2sb[0:E, dh * 512:(dh + 1) * 512])]
                if has_shared:
                    pairs.append((scr["ones1"][0:1, :], b2shs[0:1, dh * 512:(dh + 1) * 512]))
                p.mm(ps_, pairs, name=f"{Lt}b2mm{tc}_{dh}")
                xa = x_tm[:, tc * D + dh * 512: tc * D + (dh + 1) * 512]
                p.tt(xa, xa, ps_, ALU.add, name=f"{Lt}b2add{tc}_{dh}")
        for u in range(n_units):
            Ue = f"{Lt}e{u}"
            shared = has_shared and (u == n_exp)
            for fc in range(nfc):
                sl = w1sl[fc % 3]
                p.dma(sl[:, :], w1b[u, fc, :, :], q="pool", name=f"{Ue}w1d{fc}")
                ph = pQ[fc % 2]
                for t5 in range(2):
                    pairs = [(sl[:, dc * 128:(dc + 1) * 128],
                              xnfm[:, dc * TT + t5 * 512: dc * TT + t5 * 512 + 512]) for dc in range(DC)]
                    p.mm(ph[:, t5 * 512:(t5 + 1) * 512], pairs, name=f"{Ue}p1_{fc}_{t5}")
                bidx = u * nfc + fc
                p.act(hbuf[:, fc * TT:(fc + 1) * TT], ph[:, :], AF.Gelu,
                      bias=b1sb[:, bidx:bidx + 1], name=f"{Ue}gelu{fc}")
            for g in range(n_grp):
                gfc = list(range(g * SG, min((g + 1) * SG, nfc)))
                sl2 = w2sl[g % 2]
                p.dma(sl2[:, 0:len(gfc) * D].rearrange("p (f d) -> p f d", f=len(gfc)),
                      w2b[u, gfc[0]:gfc[-1] + 1, :, :].rearrange("f p d -> p f d"),
                      q="pool", name=f"{Ue}w2d{g}")
                for tq in range(2):
                    for tci in range(4):
                        tc = tq * 4 + tci
                        groups = []
                        for i, fc in enumerate(gfc):
                            lhsT = hbuf[:, fc * TT + tc * 128: fc * TT + tc * 128 + 128]
                            for dh in range(2):
                                groups.append((pQ[tci][:, dh * 512:(dh + 1) * 512], lhsT,
                                               sl2[:, i * D + dh * 512: i * D + (dh + 1) * 512],
                                               i == 0, i == len(gfc) - 1))
                        p.mmx(groups, reads=[hbuf[:, :], sl2[:, :]], writes=[pQ[tci][:, :]],
                              name=f"{Ue}p2_{g}_{tc}")
                    for tci in range(4):
                        tc = tq * 4 + tci
                        xa = x_tm[:, tc * D:(tc + 1) * D]
                        if shared:
                            p.tt(xa, xa, pQ[tci][:, :], ALU.add, name=f"{Ue}fl{g}_{tc}")
                        else:
                            w_ap = w_tm[:, tc * E + u: tc * E + u + 1]
                            p.stt(xa, pQ[tci][:, :], w_ap, xa, op0=ALU.mult, op1=ALU.add,
                                  name=f"{Ue}fl{g}_{tc}")
        p.red(mu8, x3v, ALU.add, name=f"{Lt}psum")
        p.ts(mu8, mu8, 1.0 / D, name=f"{Lt}pmu")
        p.act(hbF[:, 0:8192], x_tm[:, :], AF.Square, name=f"{Lt}psq")
        p.red(var8, hbF[:, 0:8192].rearrange("p (tc d) -> p tc d", tc=8), ALU.add,
              name=f"{Lt}pvs")
        p.ts(var8, var8, 1.0 / D, 1e-5, op0=ALU.mult, op1=ALU.add, name=f"{Lt}pvar")
        p.tt(m28, mu8, mu8, ALU.mult, name=f"{Lt}pm2")
        p.tt(var8, var8, m28, ALU.subtract, name=f"{Lt}pvc")
        p.recip(rs8, var8, name=f"{Lt}prc")
        p.act(rs8, rs8, AF.Sqrt, name=f"{Lt}prs")
        p.tt(x3v, x3v, mu8.unsqueeze(2).broadcast_to([128, 8, D]), ALU.subtract,
             name=f"{Lt}pap")
        p.tt(x3v, x3v, rs8.unsqueeze(2).broadcast_to([128, 8, D]), ALU.mult,
             name=f"{Lt}pam")
        p.dma(x_dram_out[tt * TT:(tt + 1) * TT, :].rearrange("(tc p) d -> p tc d", p=128),
              x_tm[:, :].rearrange("p (tc d) -> p tc d", tc=8), q="sp", name=f"{Lt}_stx")



def host_pack_ffn(layer, inputs, n_exp=E, nfc=FC):
    d = {}
    L = f"L{layer}f"
    if layer == 0:
        W1 = np.asarray(inputs["L0_ffn_W1"], np.float32)
        W2 = np.asarray(inputs["L0_ffn_W2"], np.float32)
        d[f"{L}_w1b"] = np.stack([pack_w1_from_df_d(W1[e], nfc) for e in range(n_exp)])
        d[f"{L}_w2b"] = np.stack([pack_w2_from_d_df(W2[e], nfc) for e in range(n_exp)])
        b1r = np.asarray(inputs["L0_ffn_b1"], np.float32)[:n_exp].reshape(n_exp, FC, 128)[:, :nfc]
        d[f"{L}_b1b"] = np.ascontiguousarray(b1r.transpose(2, 0, 1).reshape(128, n_exp * nfc))
        d[f"{L}_b2b"] = np.asarray(inputs["L0_ffn_b2"], np.float32)
        d[f"{L}_b2sh"] = np.zeros((1, D), np.float32)
        d[f"{L}_wgr"] = np.ascontiguousarray(
            np.asarray(inputs["L0_ffn_Wg"], np.float32).T.reshape(DC, 128, E)
            .transpose(1, 0, 2).reshape(128, DC * E))
        d[f"{L}_bgr"] = np.asarray(inputs["L0_ffn_bg"], np.float32).reshape(1, E)
        gbp = np.empty((128, 2 * DC), np.float32)
        gbp[:, 0:DC] = pack_dvec(np.asarray(inputs["L0_ffn_gamma"], np.float32))
        gbp[:, DC:] = pack_dvec(np.asarray(inputs["L0_ffn_beta"], np.float32))
        d[f"{L}_gb"] = gbp
    else:
        W1 = np.asarray(inputs["L1_ffn_exp_w1"], np.float32)
        W2 = np.asarray(inputs["L1_ffn_exp_w2"], np.float32)
        w1l = [pack_w1_from_d_df(W1[e], nfc) for e in range(n_exp)]
        w1l.append(pack_w1_from_d_df(np.asarray(inputs["L1_ffn_shared_w1"], np.float32), nfc))
        d[f"{L}_w1b"] = np.stack(w1l)
        w2l = [pack_w2_from_df_d(W2[e], nfc) for e in range(n_exp)]
        w2l.append(pack_w2_from_df_d(np.asarray(inputs["L1_ffn_shared_w2"], np.float32), nfc))
        d[f"{L}_w2b"] = np.stack(w2l)
        b1l = np.asarray(inputs["L1_ffn_exp_b1"], np.float32)[:n_exp].reshape(n_exp, FC, 128)
        b1sh = np.asarray(inputs["L1_ffn_shared_b1"], np.float32).reshape(1, FC, 128)
        b1r = np.concatenate([b1l, b1sh], 0)[:, :nfc]
        d[f"{L}_b1b"] = np.ascontiguousarray(b1r.transpose(2, 0, 1).reshape(128, (n_exp + 1) * nfc))
        d[f"{L}_b2b"] = np.asarray(inputs["L1_ffn_exp_b2"], np.float32)
        d[f"{L}_b2sh"] = np.asarray(inputs["L1_ffn_shared_w2"], np.float32)[0:0]
        d[f"{L}_b2sh"] = np.asarray(inputs["L1_ffn_shared_b2"], np.float32).reshape(1, D)
        d[f"{L}_wgr"] = np.ascontiguousarray(
            np.asarray(inputs["L1_ffn_router_w"], np.float32).reshape(DC, 128, E)
            .transpose(1, 0, 2).reshape(128, DC * E))
        d[f"{L}_bgr"] = np.asarray(inputs["L1_ffn_router_b"], np.float32).reshape(1, E)
    d["identc"] = np.eye(128, dtype=np.float32)
    return d


def _attn_core_meta(c):
    half = c % 2
    return dict(half=half, pos0=half * TPC, pscale=float(half))


def kernel(**inputs):
    x = np.ascontiguousarray(np.asarray(inputs["L0_attn_x"], np.float32).reshape(16384, D))
    zeros_prev = np.zeros((TPC, D), np.float32)

    p = Prog()
    xin = p.dram_in("xin", [TPC, D], F32)
    xprev = p.dram_in("xprev", [TPC, D], F32)
    xmid = p.dram_tmp("xmid", [2 * TPC, D], F32)
    x2f = p.dram_tmp("x2f", [2 * TPC, D], F32)
    x3d = p.dram_tmp("x3d", [TPC, D], F32)
    xout = p.dram_out("xout", [TPC, D], F32)

    build_attn0(p, xin, xprev, xmid[TPC:2 * TPC, :], xprev_out=xmid[0:TPC, :])
    p.epoch()
    build_ffn_layer(p, 0, xmid[:, :], x2f[:, :], n_tt=4)
    p.epoch()
    build_attn1(p, x2f[TPC:2 * TPC, :], x2f[0:TPC, :], x3d[:, :])
    p.epoch()
    build_ffn_layer(p, 1, x3d[:, :], xout[:, :], n_tt=2)
    nc = p.finalize()

    pk_f0 = host_pack_ffn(0, inputs)
    pk_f1 = host_pack_ffn(1, inputs)
    pk_a1 = host_pack_attn1(inputs)
    in_maps = []
    for c in range(NCORES):
        m = _attn_core_meta(c)
        xin_c = x[c * TPC:(c + 1) * TPC]
        xprev_c = x[(c - 1) * TPC:c * TPC] if m["half"] == 1 else zeros_prev
        halo = x[c * TPC - 3:c * TPC] if m["half"] == 1 else None
        pk_a0 = host_pack_attn0(inputs, core_pos0=m["pos0"], halo_main=halo,
                                pscale=m["pscale"])
        mp = dict(pk_f0)
        mp.update(pk_f1)
        mp.update(pk_a1)
        mp.update(pk_a0)
        mp["L1a_ps"] = np.full((128, 1), m["pscale"], np.float32)
        mp["xin"] = xin_c
        mp["xprev"] = xprev_c
        in_maps.append(mp)

    res = run_bass_kernel_spmd(nc, in_maps, core_ids=list(range(NCORES)))
    out = np.concatenate([res.results[c]["xout"] for c in range(NCORES)], axis=0)
    return out.reshape(4, 4096, D)



def pack_proj_oc(w_d_o):
    dlen, olen = w_d_o.shape
    w = w_d_o.reshape(dlen // 128, 128, olen // 128, 128)
    return np.ascontiguousarray(w.transpose(2, 1, 0, 3).reshape(olen // 128, 128, dlen // 128 * 128))


def pack_proj_dc(w_d_o):
    dlen, olen = w_d_o.shape
    return np.ascontiguousarray(w_d_o.reshape(dlen // 128, 128, olen))


def emit_cast_transpose_fm(p, src_tm_ap, dst_fm, tcs, Ttot, scr_bf, pT_ps, ident_bf, tag,
                           src_is_f32=True):
    pTb = pT_ps[:, :].bitcast(BF16)
    for tc in tcs:
        s = src_tm_ap(tc)
        if src_is_f32:
            p.copy(scr_bf[:, :], s, name=f"{tag}c{tc}")
            s = scr_bf[:, :]
        for dc in range(DC):
            p.trans(pTb[:, dc * 128:(dc + 1) * 128], s[:, dc * 128:(dc + 1) * 128],
                    ident_bf[:, :], name=f"{tag}t{tc}_{dc}")
        p.copy(dst_fm[:, 0:DC * Ttot].rearrange("p (dc t) -> p dc t", dc=DC)
               [:, :, tc * 128:(tc + 1) * 128],
               pTb[:, 0:1024].rearrange("p (dc t) -> p dc t", dc=DC),
               name=f"{tag}e{tc}")


def emit_softplus(p, out_ap, in_ap, scr1, scr2, tag):
    p.act(scr1[:, :], in_ap, AF.Abs, name=f"{tag}a")
    p.ts(out_ap, in_ap, 0.0, op0=ALU.max, name=f"{tag}r")
    p.act(scr1[:, :], scr1[:, :], AF.Exp, scale=-1.0, name=f"{tag}e")
    p.act(scr1[:, :], scr1[:, :], AF.Ln, bias=1.0, name=f"{tag}l")
    p.tt(out_ap, out_ap, scr1[:, :], ALU.add, name=f"{tag}s")


def build_attn1(p, x_in, xprev_in, x_out, n_tc=16):
    L = "L1a"
    Ttot = n_tc * 128

    wq = p.dram_in(f"{L}_wq", [8, 128, DC * 128], F32)
    wk = p.dram_in(f"{L}_wk", [8, 128, DC * 128], F32)
    wg = p.dram_in(f"{L}_wg", [8, 128, DC * 128], F32)
    wv = p.dram_in(f"{L}_wv", [DC, 128, D], F32)
    wo = p.dram_in(f"{L}_wo", [8, 128, D], F32)
    bgc = p.dram_in(f"{L}_bg", [128, 8], F32)
    pscale_d = p.dram_in(f"{L}_ps", [128, 1], F32)
    maskst_d = p.dram_in("maskst", [128, 128], F32)
    identc = p.dram_in("identc", [128, 128], F32)

    x_scr = p.sb(f"{L}_xscr", [128, D], F32)
    x_fm = p.sb(f"{L}_xfm", [128, DC * Ttot], BF16)
    qf_fm = p.sb(f"{L}_qf", [128, 8 * Ttot], BF16)
    kf_fm = p.sb(f"{L}_kf", [128, 8 * Ttot], BF16)
    kf_tm = p.sb(f"{L}_kftm", [128, n_tc * D], BF16)
    v65 = p.sb(f"{L}_v65", [128, n_tc * H * 65], BF16)
    S_acc = p.sb(f"{L}_S", [128, 8 * 65], F32)
    S_bf = p.sb(f"{L}_Sbf", [128, 8 * 65], BF16)
    sc_sb = p.sb(f"{L}_sc", [128, H * 128], BF16)
    o_scr = p.sb(f"{L}_oscr", [128, D], BF16)
    ofm_scr = p.sb(f"{L}_ofm", [128, D], BF16)
    den_sb = p.sb(f"{L}_den", [128, H], F32)
    wsl = [p.sb(f"{L}_wsl{i}", [128, DC * 128], BF16) for i in range(3)]
    wsl2 = [p.sb(f"{L}_wsl2{i}", [128, DC * 128], BF16) for i in range(2)]
    wv_sb = p.sb(f"{L}_wvsb", [128, DC * D], BF16)
    wo_sb = x_fm
    bg_sb = p.sb(f"{L}_bgsb", [128, 8], F32)
    ps_sb = p.sb(f"{L}_pssb", [128, 1], F32)
    mask_sb = p.sb("mask_sb", [128, 128], BF16)
    ident_bf = p.sb("identbf", [128, 128], BF16)
    s1 = p.sb(f"{L}_s1", [128, 512], F32)
    s2 = None
    cast_scr = p.sb(f"{L}_cscr", [128, D], BF16)
    lnscr = dict(junk=cast_scr, st=p.sb(f"{L}_st", [128, 16], F32))

    pAB = p.ps("pAB", [128, 2048], F32)
    pCD = p.ps("pCD", [128, 2048], F32)
    pQ = [pAB[:, 0:1024], pAB[:, 1024:2048], pCD[:, 0:1024], pCD[:, 1024:2048]]
    pA, pB, pC, pD = pQ

    p.dma(ident_bf[:, :], identc[:, :], q="pool", name="ld_identbf")
    p.dma(mask_sb[:, :], maskst_d[:, :], q="pool", name="ld_mask")
    p.dma(wv_sb[:, :].rearrange("p (a b) -> p a b", a=DC), wv[:, :, :].rearrange("a p b -> p a b"),
          q="pool", name="ld_wv")
    p.dma(bg_sb[:, :], bgc[:, :], q="pool", name="ld_bg")
    p.dma(ps_sb[:, :], pscale_d[:, :], q="pool", name="ld_ps")

    def proj_fm_pass(xfm, wblob, slabs, evac, tag):
        nt5 = Ttot // 512
        for oc in range(8):
            sl = slabs[oc % len(slabs)]
            p.dma(sl[:, :], wblob[oc, :, :], q="pool", name=f"{tag}d{oc}")
            for t5 in range(nt5):
                ph = pQ[(oc * nt5 + t5) % 2]
                half = ((oc * nt5 + t5) // 2) % 2
                ps_ = ph[:, half * 512:(half + 1) * 512]
                pairs = [(sl[:, dc * 128:(dc + 1) * 128],
                          xfm[:, dc * Ttot + t5 * 512: dc * Ttot + t5 * 512 + 512])
                         for dc in range(DC)]
                p.mm(ps_, pairs, name=f"{tag}mm{oc}_{t5}")
                evac(oc, t5, ps_)

    def state_accum(tc, tag):
        for h in range(H):
            px = pQ[h // 8]
            slot = (h % 8) * 128
            p.mm(px[0:64, slot:slot + 65],
                 [(kf_tm[:, tc * D + h * 64: tc * D + h * 64 + 64],
                   v65[:, tc * H * 65 + h * 65: tc * H * 65 + (h + 1) * 65])],
                 name=f"{tag}st{h}")
        for h in range(H):
            px = pQ[h // 8]
            slot = (h % 8) * 128
            r0 = (h % 2) * 64
            p.tt(S_acc[r0:r0 + 64, (h // 2) * 65:(h // 2 + 1) * 65],
                 S_acc[r0:r0 + 64, (h // 2) * 65:(h // 2 + 1) * 65],
                 px[0:64, slot:slot + 65], ALU.add, name=f"{tag}sa{h}")

    def l1_half(x_dram, do_out, tag):
        def src(tc):
            p.dma(x_scr[:, :], x_dram[tc * 128:(tc + 1) * 128, :], q="sp", name=f"{tag}lx{tc}")
            return x_scr[:, :]
        emit_cast_transpose_fm(p, src, x_fm, range(n_tc), Ttot, cast_scr, pD, ident_bf, f"{tag}xt")
        def evk(oc, t5, ps_):
            emit_softplus(p, kf_fm[:, oc * Ttot + t5 * 512: oc * Ttot + t5 * 512 + 512],
                          ps_, s1, s2, f"{tag}spk{oc}_{t5}")
        proj_fm_pass(x_fm, wk, wsl, evk, f"{tag}pk")
        pTb = pD[:, :].bitcast(BF16)
        for tc in range(n_tc):
            for oc in range(8):
                p.trans(pTb[:, oc * 128:(oc + 1) * 128],
                        kf_fm[:, oc * Ttot + tc * 128: oc * Ttot + tc * 128 + 128],
                        ident_bf[:, :], name=f"{tag}ktt{tc}_{oc}")
            p.copy(kf_tm[:, tc * D:(tc + 1) * D], pTb[:, 0:1024], name=f"{tag}kte{tc}")
        for tc in range(n_tc):
            for oh in range(2):
                ph = pQ[oh]
                ps_ = ph[:, 0:512]
                pairs = [(x_fm[:, dc * Ttot + tc * 128: dc * Ttot + tc * 128 + 128],
                          wv_sb[:, dc * D + oh * 512: dc * D + oh * 512 + 512])
                         for dc in range(DC)]
                p.mm(ps_, pairs, name=f"{tag}pv{tc}_{oh}")
                dst = v65[:, :].rearrange("p (tc h j) -> p tc h j", tc=n_tc, h=H)[
                    :, tc, oh * 8:(oh + 1) * 8, 0:64]
                p.copy(dst, ps_.rearrange("p (h j) -> p h j", h=8), name=f"{tag}ve{tc}_{oh}")
            p.memset(v65[:, :].rearrange("p (tc h j) -> p tc h j", tc=n_tc, h=H)[
                :, tc, :, 64:65], 1.0, name=f"{tag}vo{tc}")
        if do_out:
            nt5 = Ttot // 512
            for oc in range(8):
                slq = wsl[oc % 3]
                slg = wsl2[oc % 2]
                p.dma(slq[:, :], wq[oc, :, :], q="pool", name=f"{tag}qd{oc}")
                p.dma(slg[:, :], wg[oc, :, :], q="pool", name=f"{tag}gd{oc}")
                for t5 in range(nt5):
                    psq = pA[:, (t5 % 2) * 512:(t5 % 2) * 512 + 512]
                    psg = pB[:, (t5 % 2) * 512:(t5 % 2) * 512 + 512]
                    p.mm(psq, [(slq[:, dc * 128:(dc + 1) * 128],
                                x_fm[:, dc * Ttot + t5 * 512: dc * Ttot + t5 * 512 + 512])
                               for dc in range(DC)], name=f"{tag}qmm{oc}_{t5}")
                    p.mm(psg, [(slg[:, dc * 128:(dc + 1) * 128],
                                x_fm[:, dc * Ttot + t5 * 512: dc * Ttot + t5 * 512 + 512])
                               for dc in range(DC)], name=f"{tag}gmm{oc}_{t5}")
                    emit_softplus(p, o_scr[:, 0:512], psq, s1, s2, f"{tag}spq{oc}_{t5}")
                    p.act(s1[:, :], psg, AF.Sigmoid, bias=bg_sb[:, oc:oc + 1], name=f"{tag}sig{oc}_{t5}")
                    p.stt(qf_fm[:, oc * Ttot + t5 * 512: oc * Ttot + t5 * 512 + 512],
                          s1[:, :], 1e-6, o_scr[:, 0:512], op0=ALU.add, op1=ALU.mult, name=f"{tag}qf{oc}_{t5}")

    p.memset(S_acc[:, :], 0.0, name="S0")
    l1_half(xprev_in, False, f"{L}p")
    for tc in range(n_tc):
        state_accum(tc, f"{L}ps{tc}")
    p.ts(S_acc[:, :], S_acc[:, :], ps_sb[:, 0:1], op0=ALU.mult, name=f"{L}psc")
    l1_half(x_in, True, f"{L}m")
    p.dma(wo_sb[:, 0:8 * D].rearrange("p (a b) -> p a b", a=8),
          wo[:, :, :].rearrange("a p b -> p a b"), q="pool", name="ld_wo")
    p.copy(S_bf[:, :], S_acc[:, :], name=f"{L}sb0")
    for tc in range(n_tc):
        T = f"{L}c{tc}"
        for h in range(H):
            ph = pQ[h % 2]
            slot = (h // 2) * 128
            r0 = (h % 2) * 64
            cs = (h // 2) * Ttot + tc * 128
            p.mm(ph[:, slot:slot + 128],
                 [(kf_fm[r0:r0 + 64, cs:cs + 128], qf_fm[r0:r0 + 64, cs:cs + 128])],
                 name=f"{T}sc{h}")
        for half in range(2):
            ph = pQ[half]
            p.tt(sc_sb[:, half * 1024:(half + 1) * 1024].rearrange("p (h t) -> p h t", h=8),
                 ph[:, 0:1024].rearrange("p (h t) -> p h t", h=8),
                 mask_sb[:, :].unsqueeze(1).broadcast_to([128, 8, 128]),
                 ALU.mult, name=f"{T}mk{half}")
        for h in range(H):
            pn = pQ[2 + h % 2]
            slot = (h // 2) * 128
            scc = ((h % 2) * 8 + h // 2) * 128
            r0 = (h % 2) * 64
            cs = (h // 2) * Ttot + tc * 128
            p.mmx([(pn[:, slot:slot + 65], sc_sb[:, scc:scc + 128],
                    v65[:, tc * H * 65 + h * 65: tc * H * 65 + (h + 1) * 65], True, False),
                   (pn[:, slot:slot + 65], qf_fm[r0:r0 + 64, cs:cs + 128],
                    S_bf[r0:r0 + 64, (h // 2) * 65:(h // 2 + 1) * 65], False, True)],
                  reads=[sc_sb[:, :], v65[:, :], qf_fm[:, :], S_bf[:, :]],
                  writes=[pn[:, slot:slot + 65]], name=f"{T}nm{h}")
        state_accum(tc, T)
        p.copy(S_bf[:, :], S_acc[:, :], name=f"{T}sbf")
        for half in range(2):
            pn = pQ[2 + half]
            p.copy(den_sb[:, half * 8:(half + 1) * 8],
                   pn[:, 0:1024].rearrange("p (h t) -> p h t", h=8)[:, :, 64], name=f"{T}dg{half}")
        p.ts(den_sb[:, :], den_sb[:, :], 1e-6, op0=ALU.add, name=f"{T}de")
        p.recip(den_sb[:, :], den_sb[:, :], name=f"{T}dr")
        for h in range(H):
            pn = pQ[2 + h % 2]
            slot = (h // 2) * 128
            p.ts(o_scr[:, h * 64:(h + 1) * 64], pn[:, slot:slot + 64],
                 den_sb[:, (h % 2) * 8 + h // 2:(h % 2) * 8 + h // 2 + 1],
                 op0=ALU.mult, name=f"{T}oe{h}")
        pTb2 = pA[:, :].bitcast(BF16)
        for oc in range(8):
            p.trans(pTb2[:, oc * 128:(oc + 1) * 128], o_scr[:, oc * 128:(oc + 1) * 128],
                    ident_bf[:, :], name=f"{T}ot{oc}")
        for oc in range(8):
            p.copy(ofm_scr[:, oc * 128:(oc + 1) * 128], pTb2[:, oc * 128:(oc + 1) * 128],
                   name=f"{T}oc{oc}")
        p.dma(x_scr[:, :], x_in[tc * 128:(tc + 1) * 128, :], q="sp", name=f"{T}rx")
        for dh in range(2):
            pso = pB[:, dh * 512:(dh + 1) * 512]
            p.mm(pso, [(ofm_scr[:, oc * 128:(oc + 1) * 128],
                        wo_sb[:, oc * D + dh * 512: oc * D + dh * 512 + 512])
                       for oc in range(8)], name=f"{T}op{dh}")
            p.tt(x_scr[:, dh * 512:(dh + 1) * 512], x_scr[:, dh * 512:(dh + 1) * 512],
                 pso, ALU.add, name=f"{T}ra{dh}")
        ln_stats_apply(p, x_scr[:, :], x_scr[:, :], lnscr, f"{T}ln")
        p.dma(x_out[tc * 128:(tc + 1) * 128, :], x_scr[:, :], q="sp", name=f"{T}sx")


def host_pack_attn1(inputs):
    d = {}
    Wq = np.asarray(inputs["L1_attn_Wq"], np.float32)
    Wk = np.asarray(inputs["L1_attn_Wk"], np.float32)
    Wv = np.asarray(inputs["L1_attn_Wv"], np.float32)
    Wg = np.asarray(inputs["L1_attn_Wg"], np.float32)
    Wo = np.asarray(inputs["L1_attn_Wo"], np.float32)
    d["L1a_wq"] = pack_proj_oc(Wq.T)
    d["L1a_wk"] = pack_proj_oc(Wk.T)
    d["L1a_wg"] = pack_proj_oc(Wg.T)
    d["L1a_wv"] = pack_proj_dc(Wv.T)
    d["L1a_wo"] = pack_proj_dc(Wo.T)
    d["L1a_bg"] = np.ascontiguousarray(
        np.asarray(inputs["L1_attn_bg"], np.float32).reshape(8, 128).T)
    d["maskst"] = np.triu(np.ones((128, 128), np.float32))
    d["identc"] = np.eye(128, dtype=np.float32)
    return d



def build_attn0(p, x_in, xprev_in, x_out, xprev_out=None, n_tc=16):
    L = "L0a"
    Ttot = n_tc * 128
    W3 = Ttot + 4
    HOF = 1
    DOF = HOF + 3

    wq = p.dram_in(f"{L}_wq", [8, 128, DC * 128], F32)
    wk = p.dram_in(f"{L}_wk", [8, 128, DC * 128], F32)
    wv = p.dram_in(f"{L}_wv", [8, 128, DC * 128], F32)
    wg = p.dram_in(f"{L}_wg", [DC, 128, D], F32)
    wo = p.dram_in(f"{L}_wo", [8, 128, D], F32)
    wgkb = p.dram_in(f"{L}_wgk", [128, DC * H], F32)
    wbb = p.dram_in(f"{L}_wb", [128, DC * H], F32)
    convw = p.dram_in(f"{L}_cw", [128, 3 * 8 * 4], F32)
    dtb_d = p.dram_in(f"{L}_dtb", [H, 1], F32)
    alog_d = p.dram_in(f"{L}_alog", [H, 1], F32)
    bb_d = p.dram_in(f"{L}_bb", [1, H], F32)
    gnorm_d = p.dram_in(f"{L}_gn", [128, 1], F32)
    ropec_d = p.dram_in(f"{L}_ropec", [128, n_tc * 16], F32)
    ropes_d = p.dram_in(f"{L}_ropes", [128, n_tc * 16], F32)
    ropecp_d = p.dram_in(f"{L}_ropecp", [128, n_tc * 16], F32)
    ropesp_d = p.dram_in(f"{L}_ropesp", [128, n_tc * 16], F32)
    xhalo_p = p.dram_in(f"{L}_xhp", [128, DC * 3], F32)
    xhalo_m = p.dram_in(f"{L}_xhm", [128, DC * 3], F32)
    pscale_d = p.dram_in(f"{L}_ps", [128, 1], F32)
    identc = p.dram_in("identc", [128, 128], F32)
    maskI_d = p.dram_in(f"{L}_mI", [128, 128], F32)
    maskSn_d = p.dram_in(f"{L}_mSn", [128, 128], F32)
    maskLn_d = p.dram_in(f"{L}_mLn", [128, 128], F32)
    pairsel_d = p.dram_in(f"{L}_psel", [H, 128], F32)
    pairdg_d = p.dram_in(f"{L}_pdg", [H, 8], F32)
    sel16_d = p.dram_in(f"{L}_sel", [H, H * 128], F32)
    identE_d = p.dram_in(f"{L}_idE", [H, H], F32)

    x_fm = p.sb(f"{L}_xfm", [128, max(DC * Ttot, 15360)], BF16)
    pre = p.sb(f"{L}_pre", [128, max(8 * W3, 16384)], BF16)
    q_tm = p.sb(f"{L}_qtm", [128, n_tc * D], BF16)
    k_tm = p.sb(f"{L}_ktm", [128, n_tc * D], BF16)
    v_tm = p.sb(f"{L}_vtm", [128, n_tc * D], BF16)
    gk_tm = p.sb(f"{L}_gktm", [128, n_tc * H], F32)
    beta_tm = p.sb(f"{L}_btm", [128, n_tc * H], F32)
    S_acc = p.sb(f"{L}_S", [128, 8 * DKH], F32)
    S_bf = p.sb(f"{L}_Sbf", [128, 8 * DKH], BF16)
    wsl = [p.sb(f"{L}_wsl{i}", [128, DC * 128], BF16) for i in range(2)]
    wosb = p.sb(f"{L}_wosb", [128, 8 * D], BF16)
    wgsl = None
    wgk_sb = p.sb(f"{L}_wgks", [128, DC * H], BF16)
    wb_sb = p.sb(f"{L}_wbs", [128, DC * H], BF16)
    cw_sb = p.sb(f"{L}_cws", [128, 96], F32)
    ropec = p.sb(f"{L}_rc", [128, n_tc * 16], F32)
    ropes = p.sb(f"{L}_rs", [128, n_tc * 16], F32)
    xhalo_sb = p.sb(f"{L}_xh", [128, DC * 3], BF16)
    gn_sb = p.sb(f"{L}_gns", [128, 1], F32)
    ps_sb = p.sb(f"{L}_pss", [128, 1], F32)
    dtb_sb = p.sb(f"{L}_dtbs", [H, 1], F32)
    nega_sb = p.sb(f"{L}_negas", [H, 1], F32)
    bb_sb = p.sb(f"{L}_bbs", [1, H], BF16)
    ident_bf = p.sb("identbf", [128, 128], BF16)
    ident_f = p.sb("identf", [128, 128], F32)
    maskI = p.sb(f"{L}_mIs", [128, 128], BF16)
    maskSn = p.sb(f"{L}_mSns", [128, 128], BF16)
    maskLn = p.sb(f"{L}_mLns", [128, 128], BF16)
    pairsel = p.sb(f"{L}_psels", [H, 128], F32)
    pairdg = p.sb(f"{L}_pdgs", [H, 8], F32)
    sel16 = p.sb(f"{L}_sels", [H, H * 128], F32)
    identE = p.sb(f"{L}_idEs", [H, H], F32)
    ones1 = p.sb("ones1", [1, 128], BF16)
    xfm_c = p.sb(f"{L}_xfmc", [128, D], BF16)
    mIf_sb = p.sb(f"{L}_mIfs", [128, 128], F32)
    Gfm_c = p.sb(f"{L}_Gfm", [H, 128], F32)
    bfm_c = p.sb(f"{L}_bfm", [H, 128], F32)
    st16 = p.sb(f"{L}_st16", [128, 16], F32)
    st16b = p.sb(f"{L}_st16b", [128, 16], F32)
    st16c = p.sb(f"{L}_st16c", [128, 16], F32)
    hcsc = p.sb(f"{L}_hcsc", [128, 32], F32)
    ebcp = p.sb(f"{L}_ebcp", [128, 8], F32)
    eda = p.sb(f"{L}_eda", [H, 16], F32)
    tsc = p.sb(f"{L}_tsc", [128, 512], BF16)
    x_scr = p.sb(f"{L}_xscr", [128, D], F32)
    s1 = x_scr[:, 512:1024]
    cscr = xfm_c[:, :].bitcast(F32)
    lnscr = dict(junk=xfm_c, st=p.sb(f"{L}_lst", [128, 16], F32))

    AR = x_fm[:, :]
    sc_sb = AR[:, 0:2048]
    MB_sb = AR[:, 2048:4096]
    Dst_sb = AR[:, 4096:6144]
    DmI_sb = AR[:, 6144:8192]
    DmS_sb = AR[:, 8192:10240]
    B_sb = AR[:, 10240:11264]
    U_sb = AR[:, 11264:12288]
    o_tmc = AR[:, 12288:13312]
    ofm_c = AR[:, 13312:14336]
    qgfm_c = AR[:, 14336:15360]
    AR2 = pre[:, :]
    Mst = AR2[:, 0:2048]
    Mmath = AR2[:, 2048:4096]
    M2st = AR2[:, 4096:6144]
    M2math = AR2[:, 6144:8192]
    bbc_sb = AR2[:, 8192:10240]
    kgfm_c = AR2[:, 10240:11264]
    tmp_tm = AR2[:, 11264:12288]
    ktb_c = AR2[:, 12288:13312]
    kfm_c = AR2[:, 13312:14336]
    qfm_c = AR2[:, 14336:15360]
    gs_c = AR2[:, 15360:16384]

    wgsl = wsl
    pAB = p.ps("pAB", [128, 2048], F32)
    pCD = p.ps("pCD", [128, 2048], F32)
    pQ = [pAB[:, 0:1024], pAB[:, 1024:2048], pCD[:, 0:1024], pCD[:, 1024:2048]]
    pA, pB, pC, pD = pQ

    p.dma(ident_bf[:, :], identc[:, :], q="pool", name="ld_ib")
    p.dma(ident_f[:, :], identc[:, :], q="sp", name="ld_if")
    p.dma(maskI[:, :], maskI_d[:, :], q="pool", name="ld_mI")
    p.dma(mIf_sb[:, :], maskI_d[:, :], q="sp", name="ld_mIf")
    p.dma(maskSn[:, :], maskSn_d[:, :], q="pool", name="ld_mSn")
    p.dma(maskLn[:, :], maskLn_d[:, :], q="pool", name="ld_mLn")
    p.dma(pairsel[:, :], pairsel_d[:, :], q="pool", name="ld_psel")
    p.dma(pairdg[:, :], pairdg_d[:, :], q="sp", name="ld_pdg")
    p.dma(sel16[:, :], sel16_d[:, :], q="pool", name="ld_sel")
    p.dma(identE[:, :], identE_d[:, :], q="sp", name="ld_idE")
    p.dma(wgk_sb[:, :], wgkb[:, :], q="pool", name="ld_wgk")
    p.dma(wb_sb[:, :], wbb[:, :], q="pool", name="ld_wb")
    p.dma(cw_sb[:, :], convw[:, :], q="sp", name="ld_cw")
    p.dma(gn_sb[:, :], gnorm_d[:, :], q="sp", name="ld_gn")
    p.dma(ps_sb[:, :], pscale_d[:, :], q="sp", name="ld_ps")
    p.dma(dtb_sb[:, :], dtb_d[:, :], q="sp", name="ld_dtb")
    p.dma(nega_sb[:, :], alog_d[:, :], q="sp", name="ld_alog")
    p.dma(bb_sb[:, :], bb_d[:, :], q="pool", name="ld_bb")
    p.memset(ones1[0:1, :], 1.0, name="ones")
    p.act(nega_sb[:, :], nega_sb[:, :], AF.Exp, name="negA_e")
    p.ts(nega_sb[:, :], nega_sb[:, :], -1.0, name="negA_n")
    p.memset(S_acc[:, :], 0.0, name="S0")
    p.memset(S_bf[:, :], 0.0, name="Sbf0")

    def cast_transpose_half(x_dram, tag):
        def src(tc):
            p.dma(x_scr[:, :], x_dram[tc * 128:(tc + 1) * 128, :], q="sp", name=f"{tag}lx{tc}")
            return x_scr[:, :]
        emit_cast_transpose_fm(p, src, x_fm, range(n_tc), Ttot, xfm_c,
                               pD, ident_bf, f"{tag}xt")

    def proj_conv_tensor(which, wblob, cw_off, dst_tm, halo_sb, tag):
        nt5 = Ttot // 512
        for oc in range(8):
            sl = wsl[oc % 2]
            p.dma(sl[:, :], wblob[oc, :, :], q="pool", name=f"{tag}d{oc}")
            phh = pA[:, 0:3]
            p.mm(phh, [(sl[:, dc * 128:(dc + 1) * 128], halo_sb[:, dc * 3:(dc + 1) * 3])
                       for dc in range(DC)], name=f"{tag}h{oc}")
            p.copy(pre[:, oc * W3 + HOF: oc * W3 + HOF + 3], phh, name=f"{tag}he{oc}")
            for t5 in range(nt5):
                ph = pQ[1 + t5 % 2]
                ps_ = ph[:, 0:512]
                p.mm(ps_, [(sl[:, dc * 128:(dc + 1) * 128],
                            x_fm[:, dc * Ttot + t5 * 512: dc * Ttot + t5 * 512 + 512])
                           for dc in range(DC)], name=f"{tag}mm{oc}_{t5}")
                p.copy(pre[:, oc * W3 + DOF + t5 * 512: oc * W3 + DOF + t5 * 512 + 512],
                       ps_, name=f"{tag}pe{oc}_{t5}")
            for t5 in range(nt5):
                base = oc * W3 + DOF + t5 * 512
                p.ts(cscr[:, :], pre[:, base - 3: base - 3 + 512],
                     cw_sb[:, cw_off + oc * 4: cw_off + oc * 4 + 1], op0=ALU.mult,
                     name=f"{tag}c0_{oc}_{t5}")
                for kk in range(1, 4):
                    p.stt(cscr[:, :], pre[:, base - 3 + kk: base - 3 + kk + 512],
                          cw_sb[:, cw_off + oc * 4 + kk: cw_off + oc * 4 + kk + 1],
                          cscr[:, :], op0=ALU.mult, op1=ALU.add, name=f"{tag}c{kk}_{oc}_{t5}")
                p.act(tsc[:, :], cscr[:, :], AF.Silu, name=f"{tag}si{oc}_{t5}")
                pTb = pD[:, :].bitcast(BF16)
                for i in range(4):
                    p.trans(pTb[:, i * 128:(i + 1) * 128], tsc[:, i * 128:(i + 1) * 128],
                            ident_bf[:, :], name=f"{tag}tr{oc}_{t5}_{i}")
                p.copy(dst_tm[:, 0:n_tc * D].rearrange("p (tc d) -> p tc d", tc=n_tc)
                       [:, t5 * 4:(t5 + 1) * 4, oc * 128:(oc + 1) * 128],
                       pTb[:, 0:512].rearrange("p (i t) -> p i t", i=4),
                       name=f"{tag}te{oc}_{t5}")

    def gkbeta_pass(tag):
        nt5 = Ttot // 512
        for t5 in range(nt5):
            psg = pA[0:H, 0:512]
            pairs = [(wgk_sb[:, dc * H:(dc + 1) * H],
                      x_fm[:, dc * Ttot + t5 * 512: dc * Ttot + t5 * 512 + 512])
                     for dc in range(DC)]
            p.mm(psg, pairs, name=f"{tag}gkmm{t5}")
            p.act(s1[0:H, :], psg, AF.Abs, bias=dtb_sb[0:H, 0:1], name=f"{tag}gka{t5}")
            p.act(s1[0:H, :], s1[0:H, :], AF.Exp, scale=-1.0, name=f"{tag}gke{t5}")
            p.act(s1[0:H, :], s1[0:H, :], AF.Ln, bias=1.0, name=f"{tag}gkl{t5}")
            p.act(cscr[0:H, :], psg, AF.Relu, bias=dtb_sb[0:H, 0:1], name=f"{tag}gkr{t5}")
            p.tt(cscr[0:H, :], cscr[0:H, :], s1[0:H, :], ALU.add, name=f"{tag}gks{t5}")
            p.ts(cscr[0:H, :], cscr[0:H, :], nega_sb[0:H, 0:1], op0=ALU.mult, name=f"{tag}gkn{t5}")
            for i in range(4):
                ptt = pB[:, 0:16]
                p.trans(ptt, cscr[0:H, i * 128:(i + 1) * 128], identE[:, :], name=f"{tag}gkt{t5}_{i}")
                tc = t5 * 4 + i
                p.copy(gk_tm[:, tc * H:(tc + 1) * H], ptt, name=f"{tag}gkte{t5}_{i}")
        for tc in range(n_tc):
            psb_ = pC[:, 0:H]
            pairs = [(x_fm[:, dc * Ttot + tc * 128: dc * Ttot + tc * 128 + 128],
                      wb_sb[:, dc * H:(dc + 1) * H]) for dc in range(DC)]
            pairs.append((ones1[0:1, :], bb_sb[0:1, :]))
            p.mm(psb_, pairs, name=f"{tag}bmm{tc}")
            p.act(beta_tm[:, tc * H:(tc + 1) * H], psb_, AF.Sigmoid, name=f"{tag}bsig{tc}")

    def l2_rope(dst_tm, tag, rc=None, rs=None):
        rc = ropec if rc is None else rc
        rs = ropes if rs is None else rs
        """l2norm (per head) + rope(32) in TM, in place, per tc."""
        for tc in range(n_tc):
            t3 = dst_tm[:, tc * D:(tc + 1) * D].rearrange("p (h d) -> p h d", h=H)
            p.act(xfm_c[:, :], dst_tm[:, tc * D:(tc + 1) * D], AF.Square, name=f"{tag}sq{tc}")
            p.red(st16[:, :], xfm_c[:, :].rearrange("p (h d) -> p h d", h=H), ALU.add,
                  name=f"{tag}nrm{tc}")
            p.ts(st16[:, :], st16[:, :], 1e-6, op0=ALU.add, name=f"{tag}ne{tc}")
            p.recip(st16[:, :], st16[:, :], name=f"{tag}nr{tc}")
            p.act(st16[:, :], st16[:, :], AF.Sqrt, name=f"{tag}ns{tc}")
            p.tt(t3, t3, st16[:, :].unsqueeze(2).broadcast_to([128, H, 64]), ALU.mult,
                 name=f"{tag}nm{tc}")
            x1 = t3[:, :, 0:16]
            x2 = t3[:, :, 16:32]
            cosb = rc[:, tc * 16:(tc + 1) * 16].unsqueeze(1).broadcast_to([128, H, 16])
            sinb = rs[:, tc * 16:(tc + 1) * 16].unsqueeze(1).broadcast_to([128, H, 16])
            sA = s1[:, 0:256].rearrange("p (h d) -> p h d", h=H)
            sB = cscr[:, 0:256].rearrange("p (h d) -> p h d", h=H)
            p.tt(sA, x1, sinb, ALU.mult, name=f"{tag}r1{tc}")
            p.tt(sB, x1, cosb, ALU.mult, name=f"{tag}r2{tc}")
            p.tt(x1, x2, sinb, ALU.mult, name=f"{tag}r3{tc}")
            p.tt(x1, sB, x1, ALU.subtract, name=f"{tag}r4{tc}")
            p.tt(x2, x2, cosb, ALU.mult, name=f"{tag}r5{tc}")
            p.tt(x2, x2, sA, ALU.add, name=f"{tag}r6{tc}")

    def chunk_pass(tc, do_out, tag, x_dram=None, x_outd=None):
        T = tag
        pG3 = pB[:, 0:16]
        p.mm(pG3, [(mIf_sb[:, :], gk_tm[:, tc * H:(tc + 1) * H])], name=f"{T}Gmm")
        p.copy(st16[:, :], pG3, name=f"{T}Gtm")
        pG2 = pA[0:H, 0:128]
        p.trans(pG2, st16[:, :], ident_f[:, :], name=f"{T}Gt")
        p.copy(Gfm_c[:, :], pG2, name=f"{T}Gf")
        p.ts(eda[:, 0:1], Gfm_c[:, 127:128], 1.0, op0=ALU.mult, name=f"{T}GL")
        p.ts(bfm_c[:, :], Gfm_c[:, :], eda[:, 0:1], op0=ALU.subtract, name=f"{T}E2a")
        p.act(bfm_c[:, :], bfm_c[:, :], AF.Exp, scale=-1.0, name=f"{T}E2e")
        pG4 = pC[:, 0:16]
        p.trans(pG4, bfm_c[:, :], identE[:, :], name=f"{T}E2t")
        p.tt(lnscr["st"][:, 0:16], beta_tm[:, tc * H:(tc + 1) * H], pG4, ALU.mult,
             name=f"{T}fb")
        if (not do_out) and tc == 0:
            p.dbg("pfb0", lnscr["st"][:, 0:16])
            p.dbg("pGtm0", st16[:, :])
        p.act(st16c[:, :], st16[:, :], AF.Exp, name=f"{T}eG")
        p.act(eda[:, 0:1], eda[:, 0:1], AF.Exp, name=f"{T}eL")
        p.tt(eda[:, 8:16], pairdg[:, :], eda[:, 0:1].broadcast_to([H, 8]), ALU.mult,
             name=f"{T}eD2")
        pEB = pD[:, 0:8]
        p.mm(pEB, [(pairsel[:, :], eda[:, 8:16])], name=f"{T}ebm")
        p.copy(ebcp[:, :], pEB, name=f"{T}ebe")
        pTb = pD[:, :].bitcast(BF16)
        for oc in range(8):
            p.trans(pTb[:, oc * 128:(oc + 1) * 128],
                    k_tm[:, tc * D + oc * 128: tc * D + oc * 128 + 128], ident_bf[:, :],
                    name=f"{T}ktr{oc}")
        p.copy(kfm_c[:, 0:1024], pTb[:, 0:1024], eng="act", name=f"{T}kte")
        eGb = st16c[:, :].unsqueeze(2).broadcast_to([128, H, 64])
        fbb = lnscr["st"][:, 0:16].unsqueeze(2).broadcast_to([128, H, 64])
        k3 = k_tm[:, tc * D:(tc + 1) * D].rearrange("p (h d) -> p h d", h=H)
        p.tt(tmp_tm[:, :].rearrange("p (h d) -> p h d", h=H), k3, eGb, ALU.mult,
             name=f"{T}kg")
        p.ts(tmp_tm[:, :], tmp_tm[:, :], -1.0, name=f"{T}kgn")
        p.tt(ktb_c[:, :].rearrange("p (h d) -> p h d", h=H), k3, fbb, ALU.mult,
             name=f"{T}ktb")
        if (not do_out) and tc == 0:
            p.dbg("pkg0", tmp_tm[:, :])
            p.dbg("peG0", st16c[:, :])
            p.dbg("pktmc0", k_tm[:, 0:D])
        for oc in range(8):
            p.trans(pTb[:, oc * 128:(oc + 1) * 128],
                    tmp_tm[:, oc * 128:(oc + 1) * 128], ident_bf[:, :], name=f"{T}kgt{oc}")
        p.copy(kgfm_c[:, 0:1024], pTb[:, 0:1024], eng="act", name=f"{T}kge")
        if do_out:
            for oc in range(8):
                p.trans(pTb[:, oc * 128:(oc + 1) * 128],
                        q_tm[:, tc * D + oc * 128: tc * D + oc * 128 + 128], ident_bf[:, :],
                        name=f"{T}qtr{oc}")
            p.copy(qfm_c[:, 0:1024], pTb[:, 0:1024], eng="act", name=f"{T}qte")
            q3 = q_tm[:, tc * D:(tc + 1) * D].rearrange("p (h d) -> p h d", h=H)
            p.tt(tmp_tm[:, :].rearrange("p (h d) -> p h d", h=H), q3, eGb, ALU.mult,
                 name=f"{T}qg")
            for oc in range(8):
                p.trans(pTb[:, oc * 128:(oc + 1) * 128],
                        tmp_tm[:, oc * 128:(oc + 1) * 128], ident_bf[:, :], name=f"{T}qgt{oc}")
            p.copy(qgfm_c[:, 0:1024], pTb[:, 0:1024], eng="act", name=f"{T}qge")
        pG5 = pB[0:H, 0:128]
        p.trans(pG5, beta_tm[:, tc * H:(tc + 1) * H], ident_f[:, :], name=f"{T}bt")
        p.copy(bfm_c[:, :], pG5, name=f"{T}bte")
        for h in range(H):
            px = pQ[h % 2]
            slot = (h // 2) * 128
            r0 = (h % 2) * 64
            cs = (h // 2) * 128
            p.mm(px[:, slot:slot + 128],
                 [(kfm_c[r0:r0 + 64, cs:cs + 128], kfm_c[r0:r0 + 64, cs:cs + 128])],
                 name=f"{T}skk{h}")
        p.copy(sc_sb[:, 0:2048], pAB[:, 0:2048], eng="act", name=f"{T}skke")
        if do_out:
            for h in range(H):
                px = pQ[2 + h % 2]
                slot = (h // 2) * 128
                r0 = (h % 2) * 64
                cs = (h // 2) * 128
                p.mm(px[:, slot:slot + 128],
                     [(kfm_c[r0:r0 + 64, cs:cs + 128], qfm_c[r0:r0 + 64, cs:cs + 128])],
                     name=f"{T}skq{h}")
        p.copy(hcsc[:, 0:16].rearrange("p (q a) -> p q a", q=2),
               st16[:, :].rearrange("p (a q) -> p q a", q=2), name=f"{T}ghc")
        p.copy(hcsc[:, 16:32].rearrange("p (q a) -> p q a", q=2),
               beta_tm[:, tc * H:(tc + 1) * H].rearrange("p (a q) -> p q a", q=2),
               name=f"{T}bhc")
        g_bc = hcsc[:, 0:16].unsqueeze(2).broadcast_to([128, 16, 128])
        b_bc = hcsc[:, 16:32].unsqueeze(2).broadcast_to([128, 16, 128])
        for h in range(H):
            px = pQ[h % 2]
            slot = (h // 2) * 128
            p.mm(px[:, slot:slot + 128], [(sel16[:, h * 128:(h + 1) * 128], Gfm_c[:, :])],
                 name=f"{T}cbG{h}")
        p.tt(Dst_sb[:, 0:2048].rearrange("p (hc t) -> p hc t", hc=16),
             pAB[:, 0:2048].rearrange("p (hc t) -> p hc t", hc=16), g_bc,
             ALU.subtract, name=f"{T}dsc")
        p.ts(Dst_sb[:, 0:2048], Dst_sb[:, 0:2048], 0.0, op0=ALU.min, name=f"{T}dmn")
        p.act(Dst_sb[:, 0:2048], Dst_sb[:, 0:2048], AF.Exp, name=f"{T}dst")
        mi3 = maskI[:, :].unsqueeze(1).broadcast_to([128, 16, 128])
        ms3 = maskSn[:, :].unsqueeze(1).broadcast_to([128, 16, 128])
        dsv = Dst_sb[:, 0:2048].rearrange("p (hc t) -> p hc t", hc=16)
        p.tt(DmI_sb[:, 0:2048].rearrange("p (hc t) -> p hc t", hc=16), dsv, mi3,
             ALU.mult, name=f"{T}dmi")
        p.tt(DmS_sb[:, 0:2048].rearrange("p (hc t) -> p hc t", hc=16), dsv, ms3,
             ALU.mult, name=f"{T}dms")
        p.tt(Mst[:, 0:2048], sc_sb[:, 0:2048], DmS_sb[:, 0:2048], ALU.mult,
             name=f"{T}msa")
        p.tt(Mst[:, 0:2048].rearrange("p (hc t) -> p hc t", hc=16),
             Mst[:, 0:2048].rearrange("p (hc t) -> p hc t", hc=16), b_bc,
             ALU.mult, name=f"{T}msb")
        if do_out:
            p.tt(MB_sb[:, 0:2048], pCD[:, 0:2048], DmI_sb[:, 0:2048],
                 ALU.mult, name=f"{T}mba")
            p.tt(MB_sb[:, 0:2048].rearrange("p (hc t) -> p hc t", hc=16),
                 MB_sb[:, 0:2048].rearrange("p (hc t) -> p hc t", hc=16), b_bc,
                 ALU.mult, name=f"{T}mbb")
        for h in range(H):
            px = pQ[2 + h % 2]
            slot = (h // 2) * 128
            p.mm(px[:, slot:slot + 128], [(sel16[:, h * 128:(h + 1) * 128], Gfm_c[:, :])],
                 name=f"{T}cbG2{h}")
        p.tt(Dst_sb[:, 0:2048].rearrange("p (hc t) -> p hc t", hc=16),
             pCD[:, 0:2048].rearrange("p (hc t) -> p hc t", hc=16), g_bc,
             ALU.subtract, name=f"{T}dtc")
        p.ts(Dst_sb[:, 0:2048], Dst_sb[:, 0:2048], 0.0, op0=ALU.max, name=f"{T}dmx")
        p.act(Dst_sb[:, 0:2048], Dst_sb[:, 0:2048], AF.Exp, scale=-1.0, name=f"{T}dts")
        for h in range(H):
            px = pQ[h % 2]
            slot = (h // 2) * 128
            p.mm(px[:, slot:slot + 128],
                 [(sel16[:, h * 128:(h + 1) * 128], bfm_c[:, :])], name=f"{T}cbB{h}")
        p.copy(bbc_sb[:, 0:2048], pAB[:, 0:2048], eng="act", name=f"{T}cbBe")
        ml3 = maskLn[:, :].unsqueeze(1).broadcast_to([128, 16, 128])
        p.tt(Mmath[:, 0:2048].rearrange("p (hc t) -> p hc t", hc=16),
             Dst_sb[:, 0:2048].rearrange("p (hc t) -> p hc t", hc=16), ml3,
             ALU.mult, name=f"{T}mm1")
        p.tt(Mmath[:, 0:2048], Mmath[:, 0:2048], sc_sb[:, 0:2048], ALU.mult, name=f"{T}mm2")
        p.tt(Mmath[:, 0:2048], Mmath[:, 0:2048], bbc_sb[:, 0:2048], ALU.mult, name=f"{T}mm3")
        for h in range(H):
            px = pQ[h % 2]
            r0 = (h % 2) * 64
            sl_ = px[:, (h // 2) * 64:(h // 2) * 64 + 64]
            p.mmx([(sl_, ident_bf[:, :], v_tm[:, tc * D + h * 64: tc * D + h * 64 + 64],
                    True, False),
                   (sl_, kgfm_c[r0:r0 + 64, (h // 2) * 128:(h // 2) * 128 + 128],
                    S_bf[r0:r0 + 64, (h // 2) * 64:(h // 2) * 64 + 64], False, True)],
                  reads=[ident_bf[:, :], v_tm[:, :], kgfm_c[:, :], S_bf[:, :]],
                  writes=[sl_], name=f"{T}B{h}")
        p.copy(B_sb[:, 0:1024].rearrange("p (q w) -> p q w", q=2),
               pAB[:, 0:2048].rearrange("p (q w) -> p q w", q=2)[:, :, 0:512],
               eng="act", name=f"{T}Be")
        cm, cs_, nm_, ns_ = Mmath, Mst, M2math, M2st
        src_x, dst_x = B_sb, U_sb
        for lev in range(7):
            for h in range(H):
                px = pQ[h % 2]
                hc = ((h % 2) * 8 + h // 2)
                p.mm1(px[:, (h // 2) * 64:(h // 2) * 64 + 64],
                      cs_[:, hc * 128:hc * 128 + 128],
                      src_x[:, hc * 64:hc * 64 + 64], True, True, name=f"{T}ap{lev}_{h}")
            p.tt(dst_x[:, 0:1024].rearrange("p (q w) -> p q w", q=2),
                 src_x[:, 0:1024].rearrange("p (q w) -> p q w", q=2),
                 pAB[:, 0:2048].rearrange("p (q w) -> p q w", q=2)[:, :, 0:512],
                 ALU.add, name=f"{T}ape{lev}")
            src_x, dst_x = dst_x, src_x
            if lev < 6:
                for h in range(H):
                    hc = ((h % 2) * 8 + h // 2)
                    px = pQ[2 + h % 2]
                    p.mm(px[:, (h // 2) * 128:(h // 2) * 128 + 128],
                         [(cs_[:, hc * 128:hc * 128 + 128], cm[:, hc * 128:hc * 128 + 128])],
                         name=f"{T}sqm{lev}_{h}")
                p.copy(nm_[:, 0:2048], pCD[:, 0:2048], eng="act", name=f"{T}sqme{lev}")
                for h in range(H):
                    hc = ((h % 2) * 8 + h // 2)
                    px = pQ[2 + h % 2]
                    p.mm(px[:, (h // 2) * 128:(h // 2) * 128 + 128],
                         [(cm[:, hc * 128:hc * 128 + 128], cs_[:, hc * 128:hc * 128 + 128])],
                         name=f"{T}sqs{lev}_{h}")
                p.copy(ns_[:, 0:2048], pCD[:, 0:2048], eng="act", name=f"{T}sqse{lev}")
                cm, cs_, nm_, ns_ = nm_, ns_, cm, cs_
        U_fin = src_x
        if (not do_out) and tc == 0:
            p.dbg("pU0", U_fin[:, 0:1024])
            p.dbg("pMst0", Mst[:, 0:2048])
            p.dbg("pMmath0", Mmath[:, 0:2048])
            p.dbg("pB0", B_sb[:, 0:1024])
            p.dbg("pktb0", ktb_c[:, :])
        if do_out and tc == 0:
            p.dbg("U0", U_fin[:, 0:1024])
            p.dbg("G0", Gfm_c[:, :])
            p.dbg("Mst0", Mst[:, 0:2048])
            p.dbg("MB0", MB_sb[:, 0:2048])
            p.dbg("sc0", sc_sb[:, 0:2048])
            p.dbg("B0", B_sb[:, 0:1024])
        if do_out:
            for h in range(H):
                px = pQ[2 + h % 2]
                hc = ((h % 2) * 8 + h // 2)
                r0 = (h % 2) * 64
                p.mmx([(px[:, (h // 2) * 64:(h // 2) * 64 + 64],
                        MB_sb[:, hc * 128:hc * 128 + 128], U_fin[:, hc * 64:hc * 64 + 64],
                        True, False),
                       (px[:, (h // 2) * 64:(h // 2) * 64 + 64],
                        qgfm_c[r0:r0 + 64, (h // 2) * 128:(h // 2) * 128 + 128],
                        S_bf[r0:r0 + 64, (h // 2) * 64:(h // 2) * 64 + 64], False, True)],
                      reads=[MB_sb, U_fin, qgfm_c[:, :], S_bf[:, :]],
                      writes=[px[:, (h // 2) * 64:(h // 2) * 64 + 64]], name=f"{T}O{h}")
        for h in range(H):
            px = pQ[h % 2]
            p.mm1(px[0:64, (h // 2) * 64:(h // 2) * 64 + 64],
                  ktb_c[:, h * 64:(h + 1) * 64],
                  U_fin[:, ((h % 2) * 8 + h // 2) * 64:((h % 2) * 8 + h // 2) * 64 + 64],
                  True, True, name=f"{T}su{h}")
        for q in range(2):
            r0 = q * 64
            Sg = S_acc[r0:r0 + 64, :].rearrange("p (a d) -> p a d", a=8)
            eb = ebcp[r0:r0 + 64, :].unsqueeze(2).broadcast_to([64, 8, 64])
            p.tt(Sg, Sg, eb, ALU.mult, name=f"{T}ssm{q}")
            p.tt(S_acc[r0:r0 + 64, :], S_acc[r0:r0 + 64, :], pQ[q][0:64, 0:512],
                 ALU.add, name=f"{T}ssa{q}")
        p.copy(S_bf[:, :], S_acc[:, :], name=f"{T}sbf")
        if (not do_out) and tc == 0:
            p.dbg("pS1", S_acc[:, :])
        if do_out:
            for hh in range(2):
                px = pQ[2 + hh]
                dst3 = o_tmc[:, :].rearrange("p (oc two d) -> p oc two d", oc=8, two=2)[:, :, hh, :]
                p.copy(dst3, px[:, 0:512].rearrange("p (a b) -> p a b", a=8),
                       name=f"{T}oe{hh}")
            p.act(xfm_c[:, :], o_tmc[:, :], AF.Square, name=f"{T}osq")
            p.red(st16b[:, :], xfm_c[:, :].rearrange("p (h d) -> p h d", h=H), ALU.add,
                  name=f"{T}orn")
            p.ts(st16b[:, :], st16b[:, :], 1.0 / 64.0, 1e-5, op0=ALU.mult, op1=ALU.add,
                 name=f"{T}ore")
            p.recip(st16b[:, :], st16b[:, :], name=f"{T}orr")
            p.act(st16b[:, :], st16b[:, :], AF.Sqrt, name=f"{T}ors")
            p.tt(o_tmc[:, :].rearrange("p (h d) -> p h d", h=H),
                 o_tmc[:, :].rearrange("p (h d) -> p h d", h=H),
                 st16b[:, :].unsqueeze(2).broadcast_to([128, H, 64]), ALU.mult,
                 name=f"{T}orm")
            p.dma(x_scr[:, :], x_dram[tc * 128:(tc + 1) * 128, :], q="sp", name=f"{T}gx")
            p.copy(xfm_c[:, :], x_scr[:, :], name=f"{T}gc")
            pTb2 = pD[:, :].bitcast(BF16)
            for dc in range(DC):
                p.trans(pTb2[:, dc * 128:(dc + 1) * 128], xfm_c[:, dc * 128:(dc + 1) * 128],
                        ident_bf[:, :], name=f"{T}gt{dc}")
            p.copy(tmp_tm[:, 0:1024], pTb2[:, 0:1024], eng="act", name=f"{T}gte")
            for dc in range(DC):
                sl = wgsl[dc % 2]
                p.dma(sl[:, :], wg[dc, :, :], q="pool", name=f"{T}gd{dc}")
                for oh in range(2):
                    psg = pC[:, oh * 512:(oh + 1) * 512]
                    p.mm1(psg, tmp_tm[:, dc * 128:(dc + 1) * 128],
                          sl[:, oh * 512:(oh + 1) * 512], start=(dc == 0), stop=(dc == DC - 1),
                          name=f"{T}gp{oh}_{dc}")
            p.act(gs_c[:, 0:512], pC[:, 0:512], AF.Silu, name=f"{T}gs0")
            p.act(gs_c[:, 512:1024], pC[:, 512:1024], AF.Silu, name=f"{T}gs1")
            if tc == 0:
                p.dbg("gs0", gs_c[:, :])
                p.dbg("orms0", o_tmc[:, :])
            p.tt(o_tmc[:, :], o_tmc[:, :], gs_c[:, :], ALU.mult, name=f"{T}og")
            if tc == 0:
                p.dbg("o0", o_tmc[:, :])
            for oc in range(8):
                p.trans(pTb2[:, oc * 128:(oc + 1) * 128], o_tmc[:, oc * 128:(oc + 1) * 128],
                        ident_bf[:, :], name=f"{T}ot{oc}")
            p.copy(ofm_c[:, 0:1024], pTb2[:, 0:1024], eng="act", name=f"{T}ote")
            for dh in range(2):
                pso = pB[:, dh * 512:(dh + 1) * 512]
                p.mm(pso, [(ofm_c[:, oc * 128:(oc + 1) * 128],
                            wosb[:, oc * D + dh * 512: oc * D + dh * 512 + 512])
                           for oc in range(8)], name=f"{T}op{dh}")
                p.tt(x_scr[:, dh * 512:(dh + 1) * 512], x_scr[:, dh * 512:(dh + 1) * 512],
                     pso, ALU.add, name=f"{T}ra{dh}")
            ln_stats_apply(p, x_scr[:, :], x_scr[:, :], lnscr, f"{T}ln")
            p.dma(x_outd[tc * 128:(tc + 1) * 128, :], x_scr[:, :], q="sp", name=f"{T}sx")

    def half_pass(x_dram, halo_sb_src, do_out, tag, is_prefix=False, x_outd=None):
        p.dma(ropec[:, :], ropecp_d[:, :] if is_prefix else ropec_d[:, :], q="sp", name=f"{tag}lrc")
        p.dma(ropes[:, :], ropesp_d[:, :] if is_prefix else ropes_d[:, :], q="sp", name=f"{tag}lrs")
        p.dma(xhalo_sb[:, :], halo_sb_src[:, :], q="pool", name=f"{tag}lh")
        cast_transpose_half(x_dram, tag)
        proj_conv_tensor("q", wq, 0, q_tm, xhalo_sb, f"{tag}q") if do_out else None
        proj_conv_tensor("k", wk, 32, k_tm, xhalo_sb, f"{tag}k")
        proj_conv_tensor("v", wv, 64, v_tm, xhalo_sb, f"{tag}v")
        gkbeta_pass(tag)
        if do_out:
            l2_rope(q_tm, f"{tag}lrq")
        l2_rope(k_tm, f"{tag}lrk")
        if do_out:
            p.dbg("qtm", q_tm[:, :])
            p.dbg("ktm", k_tm[:, :])
            p.dbg("vtm", v_tm[:, :])
            p.dbg("gktm", gk_tm[:, :])
            p.dbg("btm", beta_tm[:, :])
        if do_out:
            p.dma(wosb[:, :].rearrange("p (a b) -> p a b", a=8),
                  wo[:, :, :].rearrange("a p b -> p a b"), q="pool", name=f"{tag}lwo")
            p.ts(wosb[:, :], wosb[:, :], gn_sb[:, 0:1], op0=ALU.mult, name=f"{tag}gn")
        for tc in range(n_tc):
            chunk_pass(tc, do_out, f"{tag}c{tc}", x_dram=x_dram, x_outd=x_outd)

    half_pass(xprev_in, xhalo_p, xprev_out is not None, f"{L}p",
              is_prefix=True, x_outd=xprev_out)
    p.ts(S_acc[:, :], S_acc[:, :], ps_sb[:, 0:1], op0=ALU.mult, name=f"{L}psc")
    p.copy(S_bf[:, :], S_acc[:, :], name=f"{L}psbf")
    p.dbg("Spre", S_acc[:, :])
    half_pass(x_in, xhalo_m, True, f"{L}m", is_prefix=False, x_outd=x_out)


def host_pack_attn0(inputs, core_pos0=0, halo_main=None, pscale=0.0, n_tc=16):
    d = {}
    Wq = np.asarray(inputs["L0_attn_Wq"], np.float32)
    Wk = np.asarray(inputs["L0_attn_Wk"], np.float32)
    Wv = np.asarray(inputs["L0_attn_Wv"], np.float32)
    Wg = np.asarray(inputs["L0_attn_Wg"], np.float32)
    Wo = np.asarray(inputs["L0_attn_Wo"], np.float32)
    d["L0a_wq"] = pack_proj_oc(Wq)
    d["L0a_wk"] = pack_proj_oc(Wk)
    d["L0a_wv"] = pack_proj_oc(Wv)
    d["L0a_wg"] = pack_proj_dc(Wg)
    d["L0a_wo"] = pack_proj_dc(Wo)
    d["L0a_wgk"] = np.ascontiguousarray(
        np.asarray(inputs["L0_attn_Wgk"], np.float32).reshape(DC, 128, H)
        .transpose(1, 0, 2).reshape(128, DC * H))
    d["L0a_wb"] = np.ascontiguousarray(
        np.asarray(inputs["L0_attn_Wb"], np.float32).reshape(DC, 128, H)
        .transpose(1, 0, 2).reshape(128, DC * H))
    cw = np.zeros((128, 96), np.float32)
    for i, nm in enumerate(("conv_q_w", "conv_k_w", "conv_v_w")):
        w = np.asarray(inputs[f"L0_attn_{nm}"], np.float32)
        cw[:, i * 32:(i + 1) * 32] = w.reshape(8, 128, 4).transpose(1, 0, 2).reshape(128, 32)
    d["L0a_cw"] = cw
    d["L0a_dtb"] = np.asarray(inputs["L0_attn_dt_bias"], np.float32).reshape(H, 1)
    d["L0a_alog"] = np.asarray(inputs["L0_attn_A_log"], np.float32).reshape(H, 1)
    d["L0a_bb"] = np.asarray(inputs["L0_attn_bb"], np.float32).reshape(1, H)
    d["L0a_gn"] = np.tile(np.asarray(inputs["L0_attn_gnorm_w"], np.float32), 2).reshape(128, 1)
    Ttot = n_tc * 128
    inv = 10000.0 ** (-np.arange(16) / 16.0)

    def _tab(p0):
        ang = (p0 + np.arange(Ttot))[:, None] * inv[None, :]
        c = np.ascontiguousarray(np.cos(ang).astype(np.float32)
                                 .reshape(n_tc, 128, 16).transpose(1, 0, 2).reshape(128, n_tc * 16))
        s_ = np.ascontiguousarray(np.sin(ang).astype(np.float32)
                                  .reshape(n_tc, 128, 16).transpose(1, 0, 2).reshape(128, n_tc * 16))
        return c, s_
    d["L0a_ropec"], d["L0a_ropes"] = _tab(core_pos0)
    d["L0a_ropecp"], d["L0a_ropesp"] = _tab(max(core_pos0 - Ttot, 0))
    d["L0a_xhp"] = np.zeros((128, DC * 3), np.float32)
    if halo_main is None:
        d["L0a_xhm"] = np.zeros((128, DC * 3), np.float32)
    else:
        d["L0a_xhm"] = np.ascontiguousarray(
            halo_main.T.reshape(DC, 128, 3).transpose(1, 0, 2).reshape(128, DC * 3)).astype(np.float32)
    d["L0a_ps"] = np.full((128, 1), pscale, np.float32)
    s = np.arange(128)
    d["L0a_mI"] = (s[:, None] <= s[None, :]).astype(np.float32)
    d["L0a_mSn"] = -(s[:, None] < s[None, :]).astype(np.float32)
    d["L0a_mLn"] = -(s[:, None] > s[None, :]).astype(np.float32)
    ps_ = np.zeros((H, 128), np.float32)
    for h in range(H):
        ps_[h, :] = ((np.arange(128) >= 64) == (h % 2 == 1)).astype(np.float32)
    d["L0a_psel"] = ps_
    pd_ = np.zeros((H, 8), np.float32)
    for h in range(H):
        pd_[h, h // 2] = 1.0
    d["L0a_pdg"] = pd_
    sel = np.zeros((H, H * 128), np.float32)
    for h in range(H):
        sel[h, h * 128:(h + 1) * 128] = 1.0
    d["L0a_sel"] = sel
    d["L0a_idE"] = np.eye(H, dtype=np.float32)
    d["identc"] = np.eye(128, dtype=np.float32)
    return d


# version 09
# speedup vs baseline: 1.0725x; 1.0725x over previous
import numpy as np
import ml_dtypes
from contextlib import ExitStack

import concourse.bass as bass
import concourse.mybir as mybir
from concourse.bass_utils import run_bass_kernel_spmd

F32 = mybir.dt.float32
BF16 = mybir.dt.bfloat16
AF = mybir.ActivationFunctionType
ALU = mybir.AluOpType
AX = mybir.AxisListType
BBF16 = ml_dtypes.bfloat16

NCORES = 8
TPC = 2048
D = 1024
H = 16
DKH = 64
E = 8
DF = 4096
FC = DF // 128
DC = D // 128
TT = 1024


ENGS = ("pe", "act", "dve", "pool", "sp")
DMA_QUEUES = {"sp": "sp", "act": "act", "pool": "pool"}
N_DMA_SEMS = {"sp": 6, "act": 4, "pool": 10}


def _akey(ap):
    try:
        return ap.name
    except Exception:
        return str(ap)


class Prog:
    def __init__(self, tag="k"):
        self.nc = bass.Bass("TRN2", target_bir_lowering=False, debug=False)
        self.ctx = ExitStack()
        self.tag = tag
        self.ops = []
        self.sems = {}
        self.count = {}
        for e in ENGS:
            self._mksem("e_" + e)
        self.dmapool = {}
        self.dmaidx = {q: 0 for q in DMA_QUEUES}
        for q, n in N_DMA_SEMS.items():
            self.dmapool[q] = [self._mksem(f"d_{q}{i}") for i in range(n)]
        self.writers = {}
        self.readers = {}
        self.waited = {e: {} for e in ENGS}
        self.sbufs = {}
        self.psums = {}
        self.drams = {}
        self._finalized = False
        for _nm, _shape, _dt in (("identbf", [128, 128], BF16),
                                 ("identf", [128, 128], F32),
                                 ("ones1", [1, 128], BF16)):
            self.sbufs[_nm] = self.ctx.enter_context(self.nc.sbuf_tensor(_nm, _shape, _dt))
        self.PERSIST = set(self.sbufs)
        self.arena_base = (self.nc.sbuf_base + 31) // 32 * 32
        self.arena_top = self.nc.sbuf_top
        self.arena_cur = 0
        self.arena_peak = 0

    def _mksem(self, name):
        h = self.ctx.enter_context(self.nc.semaphore(name))
        self.sems[name] = h
        self.count[name] = 0
        return name

    def sb(self, name, shape, dt):
        if name not in self.sbufs:
            per = 1
            for s in shape[1:]:
                per *= s
            per = (per * mybir.dt.size(dt) + 31) // 32 * 32
            off = self.arena_base + self.arena_cur
            assert off + per <= self.arena_top, (
                f"SBUF arena overflow: {name} at {off}+{per} > {self.arena_top}")
            self.sbufs[name] = self.nc.alloc_sbuf_tensor_at(name, list(shape), dt, offset=off)
            self.arena_cur += per
            self.arena_peak = max(self.arena_peak, self.arena_cur)
        return self.sbufs[name]

    def epoch(self):
        self.barrier()
        self.arena_cur = 0
        for k in list(self.sbufs):
            if k not in self.PERSIST:
                del self.sbufs[k]

    def ps(self, name, shape, dt=F32):
        if name not in self.psums:
            self.psums[name] = self.ctx.enter_context(self.nc.psum_tensor(name, list(shape), dt))
        return self.psums[name]

    def dram_in(self, name, shape, dt=F32):
        if name not in self.drams:
            self.drams[name] = self.nc.dram_tensor(name, list(shape), dt, kind="ExternalInput").ap()
        return self.drams[name]

    def dram_out(self, name, shape, dt=F32):
        if name not in self.drams:
            self.drams[name] = self.nc.dram_tensor(name, list(shape), dt, kind="ExternalOutput").ap()
        return self.drams[name]

    def dram_tmp(self, name, shape, dt=F32):
        if name not in self.drams:
            self.drams[name] = self.nc.dram_tensor(name, list(shape), dt, kind="Internal").ap()
        return self.drams[name]

    def _keys(self, ap):
        nm = _akey(ap)
        try:
            t = ap.tensor
            shape = list(t.shape)
            if len(shape) != 2:
                return [nm]
            rowbytes = shape[1] * mybir.dt.size(t.dtype)
            if rowbytes < 4096:
                return [nm]
        except Exception:
            return [nm]
        nb = (rowbytes + 2047) // 2048
        try:
            es = mybir.dt.size(ap.dtype)
            col0 = (ap.offset * es) % rowbytes
            span = es
            for stride, n in ap.ap:
                sb = stride * es
                if sb == rowbytes:
                    continue
                span += abs(sb) * (n - 1)
            b0 = col0 // 2048
            b1 = (col0 + span - 1) // 2048
            if b1 >= nb or (b1 - b0) > 31:
                b0, b1 = 0, nb - 1
        except Exception:
            b0, b1 = 0, nb - 1
        return [f"{nm}.{b}" for b in range(b0, b1 + 1)]

    def _record(self, eng, emit, reads, writes, opname="", dma_q=None):
        waits = {}

        def need(tok):
            if tok is None:
                return
            s, v = tok
            if eng == "pe" and s == "e_pe":
                return
            if self.waited[eng].get(s, 0) >= v:
                return
            waits[s] = max(waits.get(s, 0), v)

        rkeys = [k for a in reads if a is not None for k in self._keys(a)]
        wkeys = [k for a in writes if a is not None for k in self._keys(a)]
        psum_rkeys = [k for a in reads
                      if a is not None and _akey(a) in self.psums
                      for k in self._keys(a)]
        for k in rkeys:
            need(self.writers.get(k))
        own = "e_" + eng
        for k in psum_rkeys:
            for s, v in self.readers.get(k, {}).items():
                if s != own:
                    need((s, v))
        for k in wkeys:
            need(self.writers.get(k))
            for s, v in self.readers.get(k, {}).items():
                need((s, v))
        if dma_q is not None:
            pool = self.dmapool[dma_q]
            sname = pool[self.dmaidx[dma_q] % len(pool)]
            self.dmaidx[dma_q] += 1
            need((sname, self.count[sname]))
            self.count[sname] += 16
            tok = (sname, self.count[sname])
            inc = (sname, 16)
        else:
            sname = "e_" + eng
            self.count[sname] += 1
            tok = (sname, self.count[sname])
            inc = (sname, 1)
        for k in rkeys:
            self.readers.setdefault(k, {})[tok[0]] = tok[1]
        for k in wkeys:
            self.writers[k] = tok
            self.readers[k] = {tok[0]: tok[1]}
        for s, v in waits.items():
            self.waited[eng][s] = max(self.waited[eng].get(s, 0), v)
        self.ops.append(dict(eng=eng, waits=sorted(waits.items()), emit=emit, inc=inc, name=opname))
        return tok

    def barrier(self):
        cur = [(s, v) for s, v in self.count.items() if v > 0]
        for e in ENGS:
            ws = [(s, v) for s, v in cur if self.waited[e].get(s, 0) < v and s != "e_" + e]
            for s, v in ws:
                self.waited[e][s] = v
            if ws:
                self.ops.append(dict(eng=e, waits=sorted(ws), emit=None, inc=None, name="barrier"))

    def dma(self, dst, src, q="pool", name="dma"):
        eng = DMA_QUEUES[q]
        return self._record(eng, lambda E_: E_.dma_start(out=dst, in_=src),
                            reads=[src], writes=[dst], opname=name, dma_q=q)

    def mm(self, out, pairs, name="mm"):
        reads = [a for pr in pairs for a in pr]
        n = len(pairs)

        def emit(E_):
            last = None
            for i, (l, r) in enumerate(pairs):
                last = E_.matmul(out, l, r, start=(i == 0), stop=(i == n - 1))
            return last
        return self._record("pe", emit, reads=reads, writes=[out], opname=name)

    def mmx(self, groups, reads, writes, name="mmx"):
        def emit(E_):
            last = None
            for (o, l, r, st, sp) in groups:
                last = E_.matmul(o, l, r, start=st, stop=sp)
            return last
        return self._record("pe", emit, reads=reads, writes=writes, opname=name)

    def mm1(self, out, lhsT, rhs, start, stop, name="mm1"):
        return self._record("pe", lambda E_: E_.matmul(out, lhsT, rhs, start=start, stop=stop),
                            reads=[lhsT, rhs], writes=[out], opname=name)

    def trans(self, out, in_, ident, name="tr"):
        return self._record("pe", lambda E_: E_.transpose(out, in_, ident),
                            reads=[in_, ident], writes=[out], opname=name)

    def act(self, out, in_, func, bias=0.0, scale=1.0, accum_out=None, name="act"):
        reads = [in_] + [x for x in (bias, scale) if isinstance(x, bass.AP)]
        writes = [out] + ([accum_out] if accum_out is not None else [])

        def emit(E_):
            return E_.activation(out, in_, func, bias=bias, scale=scale, accum_out=accum_out)
        return self._record("act", emit, reads=reads, writes=writes, opname=name)

    def tt(self, out, a, b, op, eng="dve", name="tt"):
        return self._record(eng, lambda E_: E_.tensor_tensor(out, a, b, op=op),
                            reads=[a, b], writes=[out], opname=name)

    def ts(self, out, in_, s1, s2=None, op0=ALU.mult, op1=None, eng="dve", name="ts"):
        reads = [in_] + [s for s in (s1, s2) if isinstance(s, bass.AP)]

        def emit(E_):
            if op1 is None:
                return E_.tensor_single_scalar(out, in_, s1, op=op0)
            return E_.tensor_scalar(out, in_, s1, s2, op0=op0, op1=op1)
        return self._record(eng, emit, reads=reads, writes=[out], opname=name)

    def stt(self, out, in0, scalar, in1, op0, op1, eng="dve", name="stt"):
        reads = [in0, in1] + ([scalar] if isinstance(scalar, bass.AP) else [])
        return self._record(eng, lambda E_: E_.scalar_tensor_tensor(out, in0, scalar, in1, op0=op0, op1=op1),
                            reads=reads, writes=[out], opname=name)

    def red(self, out, in_, op=ALU.add, name="red"):
        if op == ALU.add:
            return self._record("dve", lambda E_: E_.reduce_sum(out, in_, axis=AX.X),
                                reads=[in_], writes=[out], opname=name)
        if op == ALU.max:
            return self._record("dve", lambda E_: E_.reduce_max(out, in_, axis=AX.X),
                                reads=[in_], writes=[out], opname=name)
        return self._record("dve", lambda E_: E_.tensor_reduce(out, in_, axis=AX.X, op=op),
                            reads=[in_], writes=[out], opname=name)

    def copy(self, out, in_, eng="dve", name="cp"):
        if eng == "act":
            return self._record("act", lambda E_: E_.copy(out=out, in_=in_),
                                reads=[in_], writes=[out], opname=name)
        return self._record(eng, lambda E_: E_.tensor_copy(out, in_),
                            reads=[in_], writes=[out], opname=name)

    def recip(self, out, in_, name="recip"):
        return self._record("dve", lambda E_: E_.reciprocal(out, in_),
                            reads=[in_], writes=[out], opname=name)

    def memset(self, ap, v, eng="dve", name="memset"):
        return self._record(eng, lambda E_: E_.memset(ap, v), reads=[], writes=[ap], opname=name)

    def dbg(self, name, ap, q="sp"):
        import os as _os
        if not _os.environ.get("KDBG"):
            return
        shp = list(ap.shape)
        d = self.dram_out("dbg_" + name, shp, ap.dtype)
        self.dma(d[tuple(slice(0, s) for s in shp)], ap, q=q, name="dbg_" + name)

    def collective(self, kind, op, ins, outs, replica_groups, name="cc"):
        def emit(E_):
            return E_.collective_compute(kind, op, ins=ins, outs=outs, replica_groups=replica_groups)
        return self._record("pool", emit, reads=ins, writes=outs, opname=name, dma_q="pool")

    def truncate(self, n):
        self.ops = self.ops[:n]
        cnt = {s: 0 for s in self.count}
        for op in self.ops:
            if op["inc"] is not None:
                s, amt = op["inc"]
                cnt[s] += amt
        self.count = cnt

    def finalize(self):
        assert not self._finalized
        self._finalized = True
        import os as _os
        if _os.environ.get("TRUNC_OPS"):
            self.truncate(int(_os.environ["TRUNC_OPS"]))
        final = [(s, v) for s, v in self.count.items() if v > 0]
        nc = self.nc
        byeng = {e: [] for e in ENGS}
        for op in self.ops:
            byeng[op["eng"]].append(op)
        block = self.ctx.enter_context(nc.Block())

        def run_section(eng_name, handle):
            for op in byeng[eng_name]:
                for s, v in op["waits"]:
                    handle.wait_ge(self.sems[s], v)
                if op["emit"] is not None:
                    inst = op["emit"](handle)
                    if op["inc"] is not None:
                        s, amt = op["inc"]
                        inst.then_inc(self.sems[s], amt)

        @block.sync
        def _(sp):
            run_section("sp", sp)

        @block.tensor
        def _(pe):
            run_section("pe", pe)

        @block.vector
        def _(dve):
            run_section("dve", dve)

        @block.scalar
        def _(act):
            run_section("act", act)

        @block.gpsimd
        def _(pool):
            run_section("pool", pool)
            for s, v in final:
                pool.wait_ge(self.sems[s], v)

        self.ctx.close()
        return nc



def pack_w1_from_df_d(w1_df_d, nfc=FC):
    w = w1_df_d.reshape(FC, 128, DC, 128)
    return np.ascontiguousarray(w.transpose(0, 3, 2, 1)[:nfc])


def pack_w1_from_d_df(w1_d_df, nfc=FC):
    w = w1_d_df.reshape(DC, 128, FC, 128)
    return np.ascontiguousarray(w.transpose(2, 1, 0, 3)[:nfc])


def pack_w2_from_d_df(w2_d_df, nfc=FC):
    w = w2_d_df.reshape(D, FC, 128)
    return np.ascontiguousarray(w.transpose(1, 2, 0)[:nfc])


def pack_w2_from_df_d(w2_df_d, nfc=FC):
    return np.ascontiguousarray(w2_df_d.reshape(FC, 128, D)[:nfc])


def pack_dvec(v):
    return np.ascontiguousarray(v.reshape(DC, 128).T)


def pack_proj(w_d_o):
    dlen, olen = w_d_o.shape
    w = w_d_o.reshape(dlen // 128, 128, olen)
    return np.ascontiguousarray(w.transpose(1, 0, 2))



def ln_stats_apply(p, x_ap, out_ap, scr, tag):
    st = scr["st"]
    mu, var, rs, m2 = st[:, 0:1], st[:, 1:2], st[:, 2:3], st[:, 3:4]
    p.red(mu, x_ap, ALU.add, name=f"{tag}_sum")
    p.ts(mu, mu, 1.0 / D, name=f"{tag}_mu")
    p.act(scr["junk"][:, :], x_ap, AF.Square, accum_out=var, name=f"{tag}_sq")
    p.ts(var, var, 1.0 / D, 1e-5, op0=ALU.mult, op1=ALU.add, name=f"{tag}_var")
    p.tt(m2, mu, mu, ALU.mult, name=f"{tag}_m2")
    p.tt(var, var, m2, ALU.subtract, name=f"{tag}_vc")
    p.recip(rs, var, name=f"{tag}_rcp")
    p.act(rs, rs, AF.Sqrt, name=f"{tag}_rstd")
    p.ts(out_ap, x_ap, mu, rs, op0=ALU.subtract, op1=ALU.mult, name=f"{tag}_app")


def build_ffn_layer(p, layer, x_dram_in, x_dram_out, n_tt=2, n_exp=E, nfc=FC):
    L = f"L{layer}f"
    has_shared = (layer == 1)
    eps_rt = 1e-8 if layer == 0 else 1e-9
    n_units = n_exp + (1 if has_shared else 0)

    w1b = p.dram_in(f"{L}_w1b", [n_units, nfc, 128, DC * 128], F32)
    w2b = p.dram_in(f"{L}_w2b", [n_units, nfc, 128, D], F32)
    b1b = p.dram_in(f"{L}_b1b", [128, n_units * nfc], F32)
    b2b = p.dram_in(f"{L}_b2b", [E, D], F32)
    b2sh = p.dram_in(f"{L}_b2sh", [1, D], F32)
    wgr = p.dram_in(f"{L}_wgr", [128, DC * E], F32)
    bgr = p.dram_in(f"{L}_bgr", [1, E], F32)
    if layer == 0:
        gb = p.dram_in(f"{L}_gb", [128, 2 * DC], F32)
    identc = p.dram_in("identc", [128, 128], F32)

    x_tm = p.sb(f"{L}_x", [128, 8 * D], F32)
    xnfm = p.sb(f"{L}_xnfm", [128, DC * TT], BF16)
    hbuf = p.sb(f"{L}_h", [128, nfc * TT], BF16)
    w1sl = [p.sb(f"{L}_w1s{i}", [128, DC * 128], BF16) for i in range(3)]
    SG = 8
    n_grp = (nfc + SG - 1) // SG
    w2sl = [p.sb(f"{L}_w2s{i}", [128, SG * D], BF16) for i in range(2)]
    b1sb = p.sb(f"{L}_b1sb", [128, n_units * nfc], F32)
    b2sb = p.sb(f"{L}_b2sb", [E, D], BF16)
    b2shs = p.sb(f"{L}_b2shs", [1, D], BF16)
    wgr_sb = p.sb(f"{L}_wgrs", [128, DC * E], BF16)
    bgr_sb = p.sb(f"{L}_bgrs", [1, E], BF16)
    w_tm = p.sb(f"{L}_wtm", [128, 8 * E], F32)
    wT_sb = p.sb(f"{L}_wT", [E, 8 * 128], BF16)
    ident_bf = p.sb("identbf", [128, 128], BF16)
    ident_f = p.sb("identf", [128, 128], F32)
    scr = dict(
        junk=p.sb(f"{L}_junk", [128, D], BF16),
        st=p.sb(f"{L}_st", [128, 16], F32),
        prob=p.sb(f"{L}_prob", [128, E], F32),
        prob2=p.sb(f"{L}_prob2", [128, E], F32),
        mask=p.sb(f"{L}_mask", [128, E], F32),
        mask2=p.sb(f"{L}_mask2", [128, E], F32),
        ones1=p.sb("ones1", [1, 128], BF16),
    )
    xn_scr = p.sb(f"{L}_xnscr", [128, D], BF16)
    st8 = p.sb(f"{L}_st8", [128, 80], F32)
    r64 = p.sb(f"{L}_r64", [128, 256], F32)
    if layer == 0:
        gb_sb = p.sb(f"{L}_gbsb", [128, 2 * DC], F32)

    pAB = p.ps("pAB", [128, 2048], F32)
    pCD = p.ps("pCD", [128, 2048], F32)
    pQ = [pAB[:, 0:1024], pAB[:, 1024:2048], pCD[:, 0:1024], pCD[:, 1024:2048]]
    pA, pB, pC, pD = pQ

    p.dma(ident_bf[:, :], identc[:, :], q="pool", name="ld_identbf")
    p.dma(ident_f[:, :], identc[:, :], q="sp", name="ld_identf")
    p.memset(scr["ones1"][0:1, :], 1.0, eng="dve")
    p.dma(wgr_sb[:, :], wgr[:, :], q="pool", name="ld_wgr")
    p.dma(bgr_sb[0:1, :], bgr[:, :], q="pool", name="ld_bgr")
    p.dma(b2sb[0:E, :], b2b[:, :], q="pool", name="ld_b2")
    p.dma(b2shs[0:1, :], b2sh[:, :], q="pool", name="ld_b2sh")
    p.dma(b1sb[:, :], b1b[:, :], q="pool", name="ld_b1")
    if layer == 0:
        p.dma(gb_sb[:, :], gb[:, :], q="sp", name="ld_gb")

    for tt in range(n_tt):
        Lt = f"{L}t{tt}"
        p.dma(x_tm[:, :].rearrange("p (tc d) -> p tc d", tc=8),
              x_dram_in[tt * TT:(tt + 1) * TT, :].rearrange("(tc p) d -> p tc d", p=128),
              q="sp", name=f"{Lt}_ldx")
        hbF = hbuf[:, :].bitcast(F32)
        x3v = x_tm[:, :].rearrange("p (tc d) -> p tc d", tc=8)
        mu8, var8, rs8, m28 = st8[:, 0:8], st8[:, 8:16], st8[:, 16:24], st8[:, 24:32]
        xnv = hbuf[:, 16384:24576].rearrange("p (tc d) -> p tc d", tc=8)
        if layer == 0:
            p.red(mu8, x3v, ALU.add, name=f"{Lt}lsum")
            p.ts(mu8, mu8, 1.0 / D, name=f"{Lt}lmu")
            p.act(hbF[:, 0:8192], x_tm[:, :], AF.Square, name=f"{Lt}lsq")
            p.red(var8, hbF[:, 0:8192].rearrange("p (tc d) -> p tc d", tc=8), ALU.add,
                  name=f"{Lt}lvs")
            p.ts(var8, var8, 1.0 / D, 1e-5, op0=ALU.mult, op1=ALU.add, name=f"{Lt}lvar")
            p.tt(m28, mu8, mu8, ALU.mult, name=f"{Lt}lm2")
            p.tt(var8, var8, m28, ALU.subtract, name=f"{Lt}lvc")
            p.recip(rs8, var8, name=f"{Lt}lrc")
            p.act(rs8, rs8, AF.Sqrt, name=f"{Lt}lrs")
            p.tt(xnv, x3v, mu8.unsqueeze(2).broadcast_to([128, 8, D]), ALU.subtract,
                 name=f"{Lt}lap")
            p.tt(xnv, xnv, rs8.unsqueeze(2).broadcast_to([128, 8, D]), ALU.mult,
                 name=f"{Lt}lam")
        else:
            p.copy(hbuf[:, 16384:24576], x_tm[:, :], name=f"{Lt}cast")
        pTb = pD[:, :].bitcast(BF16)
        for tc in range(8):
            for dc in range(DC):
                p.trans(pTb[:, dc * 128:(dc + 1) * 128],
                        hbuf[:, 16384 + tc * D + dc * 128: 16384 + tc * D + (dc + 1) * 128],
                        ident_bf[:, :], name=f"{Lt}tr{tc}_{dc}")
            dstv = xnfm[:, 0:8192].rearrange("p (dc t) -> p dc t", dc=8)[:, :, tc * 128:(tc + 1) * 128]
            srcv = pTb[:, 0:1024].rearrange("p (dc t) -> p dc t", dc=8)
            if layer == 0:
                p.tt(dstv, srcv,
                     gb_sb[:, 0:DC].unsqueeze(2).broadcast_to([128, DC, 128]),
                     ALU.mult, name=f"{Lt}gbm{tc}")
                p.tt(dstv, dstv,
                     gb_sb[:, DC:2 * DC].unsqueeze(2).broadcast_to([128, DC, 128]),
                     ALU.add, name=f"{Lt}gba{tc}")
            else:
                p.copy(dstv, srcv, name=f"{Lt}sc{tc}")
        for tc in range(8):
            ppr = pA[:, tc * E:(tc + 1) * E]
            pairs = [(xnfm[:, dc * TT + tc * 128: dc * TT + tc * 128 + 128],
                      wgr_sb[:, dc * E:(dc + 1) * E]) for dc in range(DC)]
            pairs.append((scr["ones1"][0:1, :], bgr_sb[0:1, :]))
            p.mm(ppr, pairs, name=f"{Lt}rlog{tc}")
        rm8, rsum8 = st8[:, 32:40], st8[:, 40:48]
        m18, m28r, den8 = st8[:, 48:56], st8[:, 56:64], st8[:, 64:72]
        pv, p2v = r64[:, 0:64], r64[:, 64:128]
        mkv, mk2v = r64[:, 128:192], r64[:, 192:256]
        r3 = lambda ap: ap.rearrange("p (tc e) -> p tc e", tc=8)
        bc8 = lambda ap: ap.unsqueeze(2).broadcast_to([128, 8, E])
        p.red(rm8, r3(pA[:, 0:64]), ALU.max, name=f"{Lt}rmax")
        p.tt(r3(pv), r3(pA[:, 0:64]), bc8(rm8), ALU.subtract, name=f"{Lt}rsub")
        p.act(pv, pv, AF.Exp, name=f"{Lt}rexp")
        p.red(rsum8, r3(pv), ALU.add, name=f"{Lt}rsum")
        p.recip(rsum8, rsum8, name=f"{Lt}rrcp")
        p.tt(r3(pv), r3(pv), bc8(rsum8), ALU.mult, name=f"{Lt}rnrm")
        p.red(m18, r3(pv), ALU.max, name=f"{Lt}m1")
        p.tt(r3(mkv), r3(pv), bc8(m18), ALU.is_ge, name=f"{Lt}mk1")
        p.tt(p2v, pv, mkv, ALU.mult, name=f"{Lt}pm")
        p.tt(p2v, pv, p2v, ALU.subtract, name=f"{Lt}p2")
        p.red(m28r, r3(p2v), ALU.max, name=f"{Lt}m2")
        p.tt(r3(mk2v), r3(p2v), bc8(m28r), ALU.is_ge, name=f"{Lt}mk2")
        p.tt(den8, m18, m28r, ALU.add, name=f"{Lt}dsum")
        p.ts(den8, den8, eps_rt, op0=ALU.add, name=f"{Lt}deps")
        p.recip(den8, den8, name=f"{Lt}drcp")
        p.tt(mkv, mkv, mk2v, ALU.add, name=f"{Lt}msum")
        p.tt(w_tm[:, 0:64], pv, mkv, ALU.mult, name=f"{Lt}pw")
        p.tt(r3(w_tm[:, 0:64]), r3(w_tm[:, 0:64]), bc8(den8), ALU.mult, name=f"{Lt}w")
        for tc in range(8):
            p.trans(pB[0:E, tc * 128:(tc + 1) * 128], w_tm[:, tc * E:(tc + 1) * E],
                    ident_f[:, :], name=f"{Lt}wT{tc}")
        p.copy(wT_sb[0:E, 0:1024], pB[0:E, 0:1024], eng="act", name=f"{Lt}wTe")
        if layer == 0:
            p.ts(x_tm[:, :], x_tm[:, :], 2.0, name=f"{Lt}x2")
        for tc in range(8):
            for dh in range(2):
                ps_ = pQ[tc % 2][:, dh * 512:(dh + 1) * 512]
                pairs = [(wT_sb[0:E, tc * 128:(tc + 1) * 128], b2sb[0:E, dh * 512:(dh + 1) * 512])]
                if has_shared:
                    pairs.append((scr["ones1"][0:1, :], b2shs[0:1, dh * 512:(dh + 1) * 512]))
                p.mm(ps_, pairs, name=f"{Lt}b2mm{tc}_{dh}")
                xa = x_tm[:, tc * D + dh * 512: tc * D + (dh + 1) * 512]
                p.tt(xa, xa, ps_, ALU.add, name=f"{Lt}b2add{tc}_{dh}")
        for u in range(n_units):
            Ue = f"{Lt}e{u}"
            shared = has_shared and (u == n_exp)
            for fc in range(nfc):
                sl = w1sl[fc % 3]
                p.dma(sl[:, :], w1b[u, fc, :, :], q="pool", name=f"{Ue}w1d{fc}")
                ph = pQ[fc % 2]
                for t5 in range(2):
                    pairs = [(sl[:, dc * 128:(dc + 1) * 128],
                              xnfm[:, dc * TT + t5 * 512: dc * TT + t5 * 512 + 512]) for dc in range(DC)]
                    p.mm(ph[:, t5 * 512:(t5 + 1) * 512], pairs, name=f"{Ue}p1_{fc}_{t5}")
                bidx = u * nfc + fc
                p.act(hbuf[:, fc * TT:(fc + 1) * TT], ph[:, :], AF.Gelu,
                      bias=b1sb[:, bidx:bidx + 1], name=f"{Ue}gelu{fc}")
            for g in range(n_grp):
                gfc = list(range(g * SG, min((g + 1) * SG, nfc)))
                sl2 = w2sl[g % 2]
                p.dma(sl2[:, 0:len(gfc) * D].rearrange("p (f d) -> p f d", f=len(gfc)),
                      w2b[u, gfc[0]:gfc[-1] + 1, :, :].rearrange("f p d -> p f d"),
                      q="pool", name=f"{Ue}w2d{g}")
                for tq in range(2):
                    for tci in range(4):
                        tc = tq * 4 + tci
                        groups = []
                        for i, fc in enumerate(gfc):
                            lhsT = hbuf[:, fc * TT + tc * 128: fc * TT + tc * 128 + 128]
                            for dh in range(2):
                                groups.append((pQ[tci][:, dh * 512:(dh + 1) * 512], lhsT,
                                               sl2[:, i * D + dh * 512: i * D + (dh + 1) * 512],
                                               i == 0, i == len(gfc) - 1))
                        p.mmx(groups, reads=[hbuf[:, :], sl2[:, :]], writes=[pQ[tci][:, :]],
                              name=f"{Ue}p2_{g}_{tc}")
                    for tci in range(4):
                        tc = tq * 4 + tci
                        xa = x_tm[:, tc * D:(tc + 1) * D]
                        if shared:
                            p.tt(xa, xa, pQ[tci][:, :], ALU.add, name=f"{Ue}fl{g}_{tc}")
                        else:
                            w_ap = w_tm[:, tc * E + u: tc * E + u + 1]
                            p.stt(xa, pQ[tci][:, :], w_ap, xa, op0=ALU.mult, op1=ALU.add,
                                  name=f"{Ue}fl{g}_{tc}")
        p.red(mu8, x3v, ALU.add, name=f"{Lt}psum")
        p.ts(mu8, mu8, 1.0 / D, name=f"{Lt}pmu")
        p.act(hbF[:, 0:8192], x_tm[:, :], AF.Square, name=f"{Lt}psq")
        p.red(var8, hbF[:, 0:8192].rearrange("p (tc d) -> p tc d", tc=8), ALU.add,
              name=f"{Lt}pvs")
        p.ts(var8, var8, 1.0 / D, 1e-5, op0=ALU.mult, op1=ALU.add, name=f"{Lt}pvar")
        p.tt(m28, mu8, mu8, ALU.mult, name=f"{Lt}pm2")
        p.tt(var8, var8, m28, ALU.subtract, name=f"{Lt}pvc")
        p.recip(rs8, var8, name=f"{Lt}prc")
        p.act(rs8, rs8, AF.Sqrt, name=f"{Lt}prs")
        p.tt(x3v, x3v, mu8.unsqueeze(2).broadcast_to([128, 8, D]), ALU.subtract,
             name=f"{Lt}pap")
        p.tt(x3v, x3v, rs8.unsqueeze(2).broadcast_to([128, 8, D]), ALU.mult,
             name=f"{Lt}pam")
        p.dma(x_dram_out[tt * TT:(tt + 1) * TT, :].rearrange("(tc p) d -> p tc d", p=128),
              x_tm[:, :].rearrange("p (tc d) -> p tc d", tc=8), q="sp", name=f"{Lt}_stx")



def host_pack_ffn(layer, inputs, n_exp=E, nfc=FC):
    d = {}
    L = f"L{layer}f"
    if layer == 0:
        W1 = np.asarray(inputs["L0_ffn_W1"], np.float32)
        W2 = np.asarray(inputs["L0_ffn_W2"], np.float32)
        d[f"{L}_w1b"] = np.stack([pack_w1_from_df_d(W1[e], nfc) for e in range(n_exp)])
        d[f"{L}_w2b"] = np.stack([pack_w2_from_d_df(W2[e], nfc) for e in range(n_exp)])
        b1r = np.asarray(inputs["L0_ffn_b1"], np.float32)[:n_exp].reshape(n_exp, FC, 128)[:, :nfc]
        d[f"{L}_b1b"] = np.ascontiguousarray(b1r.transpose(2, 0, 1).reshape(128, n_exp * nfc))
        d[f"{L}_b2b"] = np.asarray(inputs["L0_ffn_b2"], np.float32)
        d[f"{L}_b2sh"] = np.zeros((1, D), np.float32)
        d[f"{L}_wgr"] = np.ascontiguousarray(
            np.asarray(inputs["L0_ffn_Wg"], np.float32).T.reshape(DC, 128, E)
            .transpose(1, 0, 2).reshape(128, DC * E))
        d[f"{L}_bgr"] = np.asarray(inputs["L0_ffn_bg"], np.float32).reshape(1, E)
        gbp = np.empty((128, 2 * DC), np.float32)
        gbp[:, 0:DC] = pack_dvec(np.asarray(inputs["L0_ffn_gamma"], np.float32))
        gbp[:, DC:] = pack_dvec(np.asarray(inputs["L0_ffn_beta"], np.float32))
        d[f"{L}_gb"] = gbp
    else:
        W1 = np.asarray(inputs["L1_ffn_exp_w1"], np.float32)
        W2 = np.asarray(inputs["L1_ffn_exp_w2"], np.float32)
        w1l = [pack_w1_from_d_df(W1[e], nfc) for e in range(n_exp)]
        w1l.append(pack_w1_from_d_df(np.asarray(inputs["L1_ffn_shared_w1"], np.float32), nfc))
        d[f"{L}_w1b"] = np.stack(w1l)
        w2l = [pack_w2_from_df_d(W2[e], nfc) for e in range(n_exp)]
        w2l.append(pack_w2_from_df_d(np.asarray(inputs["L1_ffn_shared_w2"], np.float32), nfc))
        d[f"{L}_w2b"] = np.stack(w2l)
        b1l = np.asarray(inputs["L1_ffn_exp_b1"], np.float32)[:n_exp].reshape(n_exp, FC, 128)
        b1sh = np.asarray(inputs["L1_ffn_shared_b1"], np.float32).reshape(1, FC, 128)
        b1r = np.concatenate([b1l, b1sh], 0)[:, :nfc]
        d[f"{L}_b1b"] = np.ascontiguousarray(b1r.transpose(2, 0, 1).reshape(128, (n_exp + 1) * nfc))
        d[f"{L}_b2b"] = np.asarray(inputs["L1_ffn_exp_b2"], np.float32)
        d[f"{L}_b2sh"] = np.asarray(inputs["L1_ffn_shared_w2"], np.float32)[0:0]
        d[f"{L}_b2sh"] = np.asarray(inputs["L1_ffn_shared_b2"], np.float32).reshape(1, D)
        d[f"{L}_wgr"] = np.ascontiguousarray(
            np.asarray(inputs["L1_ffn_router_w"], np.float32).reshape(DC, 128, E)
            .transpose(1, 0, 2).reshape(128, DC * E))
        d[f"{L}_bgr"] = np.asarray(inputs["L1_ffn_router_b"], np.float32).reshape(1, E)
    d["identc"] = np.eye(128, dtype=np.float32)
    return d


def _attn_core_meta(c):
    half = c % 2
    return dict(half=half, pos0=half * TPC, pscale=float(half))


def kernel(**inputs):
    x = np.ascontiguousarray(np.asarray(inputs["L0_attn_x"], np.float32).reshape(16384, D))
    zeros_prev = np.zeros((TPC, D), np.float32)

    p = Prog()
    xin = p.dram_in("xin", [TPC, D], F32)
    xprev = p.dram_in("xprev", [TPC, D], F32)
    xmid = p.dram_tmp("xmid", [2 * TPC, D], F32)
    x2f = p.dram_tmp("x2f", [2 * TPC, D], F32)
    x3d = p.dram_tmp("x3d", [TPC, D], F32)
    xout = p.dram_out("xout", [TPC, D], F32)

    build_attn0(p, xin, xprev, xmid[TPC:2 * TPC, :], xprev_out=xmid[0:TPC, :])
    p.epoch()
    build_ffn_layer(p, 0, xmid[:, :], x2f[:, :], n_tt=4)
    p.epoch()
    build_attn1(p, x2f[TPC:2 * TPC, :], x2f[0:TPC, :], x3d[:, :])
    p.epoch()
    build_ffn_layer(p, 1, x3d[:, :], xout[:, :], n_tt=2)
    nc = p.finalize()

    pk_f0 = host_pack_ffn(0, inputs)
    pk_f1 = host_pack_ffn(1, inputs)
    pk_a1 = host_pack_attn1(inputs)
    in_maps = []
    for c in range(NCORES):
        m = _attn_core_meta(c)
        xin_c = x[c * TPC:(c + 1) * TPC]
        xprev_c = x[(c - 1) * TPC:c * TPC] if m["half"] == 1 else zeros_prev
        halo = x[c * TPC - 3:c * TPC] if m["half"] == 1 else None
        pk_a0 = host_pack_attn0(inputs, core_pos0=m["pos0"], halo_main=halo,
                                pscale=m["pscale"])
        mp = dict(pk_f0)
        mp.update(pk_f1)
        mp.update(pk_a1)
        mp.update(pk_a0)
        mp["L1a_ps"] = np.full((128, 1), m["pscale"], np.float32)
        mp["xin"] = xin_c
        mp["xprev"] = xprev_c
        in_maps.append(mp)

    res = run_bass_kernel_spmd(nc, in_maps, core_ids=list(range(NCORES)))
    out = np.concatenate([res.results[c]["xout"] for c in range(NCORES)], axis=0)
    return out.reshape(4, 4096, D)



def pack_proj_oc(w_d_o):
    dlen, olen = w_d_o.shape
    w = w_d_o.reshape(dlen // 128, 128, olen // 128, 128)
    return np.ascontiguousarray(w.transpose(2, 1, 0, 3).reshape(olen // 128, 128, dlen // 128 * 128))


def pack_proj_dc(w_d_o):
    dlen, olen = w_d_o.shape
    return np.ascontiguousarray(w_d_o.reshape(dlen // 128, 128, olen))


def emit_cast_transpose_fm(p, src_tm_ap, dst_fm, tcs, Ttot, scr_bf, pT_ps, ident_bf, tag,
                           src_is_f32=True):
    pTb = pT_ps[:, :].bitcast(BF16)
    for tc in tcs:
        s = src_tm_ap(tc)
        if src_is_f32:
            p.copy(scr_bf[:, :], s, name=f"{tag}c{tc}")
            s = scr_bf[:, :]
        for dc in range(DC):
            p.trans(pTb[:, dc * 128:(dc + 1) * 128], s[:, dc * 128:(dc + 1) * 128],
                    ident_bf[:, :], name=f"{tag}t{tc}_{dc}")
        p.copy(dst_fm[:, 0:DC * Ttot].rearrange("p (dc t) -> p dc t", dc=DC)
               [:, :, tc * 128:(tc + 1) * 128],
               pTb[:, 0:1024].rearrange("p (dc t) -> p dc t", dc=DC),
               name=f"{tag}e{tc}")


def emit_softplus(p, out_ap, in_ap, scr1, scr2, tag):
    p.act(scr1[:, :], in_ap, AF.Abs, name=f"{tag}a")
    p.ts(out_ap, in_ap, 0.0, op0=ALU.max, name=f"{tag}r")
    p.act(scr1[:, :], scr1[:, :], AF.Exp, scale=-1.0, name=f"{tag}e")
    p.act(scr1[:, :], scr1[:, :], AF.Ln, bias=1.0, name=f"{tag}l")
    p.tt(out_ap, out_ap, scr1[:, :], ALU.add, name=f"{tag}s")


def build_attn1(p, x_in, xprev_in, x_out, n_tc=16):
    L = "L1a"
    Ttot = n_tc * 128

    wq = p.dram_in(f"{L}_wq", [8, 128, DC * 128], F32)
    wk = p.dram_in(f"{L}_wk", [8, 128, DC * 128], F32)
    wg = p.dram_in(f"{L}_wg", [8, 128, DC * 128], F32)
    wv = p.dram_in(f"{L}_wv", [DC, 128, D], F32)
    wo = p.dram_in(f"{L}_wo", [8, 128, D], F32)
    bgc = p.dram_in(f"{L}_bg", [128, 8], F32)
    pscale_d = p.dram_in(f"{L}_ps", [128, 1], F32)
    maskst_d = p.dram_in("maskst", [128, 128], F32)
    identc = p.dram_in("identc", [128, 128], F32)

    x_scr = p.sb(f"{L}_xscr", [128, D], F32)
    x_fm = p.sb(f"{L}_xfm", [128, DC * Ttot], BF16)
    qf_fm = p.sb(f"{L}_qf", [128, 8 * Ttot], BF16)
    kf_fm = p.sb(f"{L}_kf", [128, 8 * Ttot], BF16)
    kf_tm = p.sb(f"{L}_kftm", [128, n_tc * D], BF16)
    v65 = p.sb(f"{L}_v65", [128, n_tc * H * 65], BF16)
    S_acc = p.sb(f"{L}_S", [128, 8 * 65], F32)
    S_bf = p.sb(f"{L}_Sbf", [128, 8 * 65], BF16)
    sc_sb = p.sb(f"{L}_sc", [128, H * 128], BF16)
    o_scr = p.sb(f"{L}_oscr", [128, D], BF16)
    ofm_scr = p.sb(f"{L}_ofm", [128, D], BF16)
    den_sb = p.sb(f"{L}_den", [128, H], F32)
    wsl = [p.sb(f"{L}_wsl{i}", [128, DC * 128], BF16) for i in range(3)]
    wsl2 = [p.sb(f"{L}_wsl2{i}", [128, DC * 128], BF16) for i in range(2)]
    wv_sb = p.sb(f"{L}_wvsb", [128, DC * D], BF16)
    wo_sb = x_fm
    bg_sb = p.sb(f"{L}_bgsb", [128, 8], F32)
    ps_sb = p.sb(f"{L}_pssb", [128, 1], F32)
    mask_sb = p.sb("mask_sb", [128, 128], BF16)
    ident_bf = p.sb("identbf", [128, 128], BF16)
    s1 = p.sb(f"{L}_s1", [128, 512], F32)
    s2 = None
    cast_scr = p.sb(f"{L}_cscr", [128, D], BF16)
    lnscr = dict(junk=cast_scr, st=p.sb(f"{L}_st", [128, 16], F32))

    pAB = p.ps("pAB", [128, 2048], F32)
    pCD = p.ps("pCD", [128, 2048], F32)
    pQ = [pAB[:, 0:1024], pAB[:, 1024:2048], pCD[:, 0:1024], pCD[:, 1024:2048]]
    pA, pB, pC, pD = pQ

    p.dma(ident_bf[:, :], identc[:, :], q="pool", name="ld_identbf")
    p.dma(mask_sb[:, :], maskst_d[:, :], q="pool", name="ld_mask")
    p.dma(wv_sb[:, :].rearrange("p (a b) -> p a b", a=DC), wv[:, :, :].rearrange("a p b -> p a b"),
          q="pool", name="ld_wv")
    p.dma(bg_sb[:, :], bgc[:, :], q="pool", name="ld_bg")
    p.dma(ps_sb[:, :], pscale_d[:, :], q="pool", name="ld_ps")

    def proj_fm_pass(xfm, wblob, slabs, evac, tag):
        nt5 = Ttot // 512
        for oc in range(8):
            sl = slabs[oc % len(slabs)]
            p.dma(sl[:, :], wblob[oc, :, :], q="pool", name=f"{tag}d{oc}")
            for t5 in range(nt5):
                ph = pQ[(oc * nt5 + t5) % 2]
                half = ((oc * nt5 + t5) // 2) % 2
                ps_ = ph[:, half * 512:(half + 1) * 512]
                pairs = [(sl[:, dc * 128:(dc + 1) * 128],
                          xfm[:, dc * Ttot + t5 * 512: dc * Ttot + t5 * 512 + 512])
                         for dc in range(DC)]
                p.mm(ps_, pairs, name=f"{tag}mm{oc}_{t5}")
                evac(oc, t5, ps_)

    def state_accum(tc, tag):
        for h in range(H):
            px = pQ[h // 8]
            slot = (h % 8) * 128
            p.mm(px[0:64, slot:slot + 65],
                 [(kf_tm[:, tc * D + h * 64: tc * D + h * 64 + 64],
                   v65[:, tc * H * 65 + h * 65: tc * H * 65 + (h + 1) * 65])],
                 name=f"{tag}st{h}")
        for h in range(H):
            px = pQ[h // 8]
            slot = (h % 8) * 128
            r0 = (h % 2) * 64
            p.tt(S_acc[r0:r0 + 64, (h // 2) * 65:(h // 2 + 1) * 65],
                 S_acc[r0:r0 + 64, (h // 2) * 65:(h // 2 + 1) * 65],
                 px[0:64, slot:slot + 65], ALU.add, name=f"{tag}sa{h}")

    def l1_half(x_dram, do_out, tag):
        def src(tc):
            p.dma(x_scr[:, :], x_dram[tc * 128:(tc + 1) * 128, :], q="sp", name=f"{tag}lx{tc}")
            return x_scr[:, :]
        emit_cast_transpose_fm(p, src, x_fm, range(n_tc), Ttot, cast_scr, pD, ident_bf, f"{tag}xt")
        def evk(oc, t5, ps_):
            emit_softplus(p, kf_fm[:, oc * Ttot + t5 * 512: oc * Ttot + t5 * 512 + 512],
                          ps_, s1, s2, f"{tag}spk{oc}_{t5}")
        proj_fm_pass(x_fm, wk, wsl, evk, f"{tag}pk")
        pTb = pD[:, :].bitcast(BF16)
        for tc in range(n_tc):
            for oc in range(8):
                p.trans(pTb[:, oc * 128:(oc + 1) * 128],
                        kf_fm[:, oc * Ttot + tc * 128: oc * Ttot + tc * 128 + 128],
                        ident_bf[:, :], name=f"{tag}ktt{tc}_{oc}")
            p.copy(kf_tm[:, tc * D:(tc + 1) * D], pTb[:, 0:1024], name=f"{tag}kte{tc}")
        for tc in range(n_tc):
            for oh in range(2):
                ph = pQ[oh]
                ps_ = ph[:, 0:512]
                pairs = [(x_fm[:, dc * Ttot + tc * 128: dc * Ttot + tc * 128 + 128],
                          wv_sb[:, dc * D + oh * 512: dc * D + oh * 512 + 512])
                         for dc in range(DC)]
                p.mm(ps_, pairs, name=f"{tag}pv{tc}_{oh}")
                dst = v65[:, :].rearrange("p (tc h j) -> p tc h j", tc=n_tc, h=H)[
                    :, tc, oh * 8:(oh + 1) * 8, 0:64]
                p.copy(dst, ps_.rearrange("p (h j) -> p h j", h=8), name=f"{tag}ve{tc}_{oh}")
            p.memset(v65[:, :].rearrange("p (tc h j) -> p tc h j", tc=n_tc, h=H)[
                :, tc, :, 64:65], 1.0, name=f"{tag}vo{tc}")
        if do_out:
            nt5 = Ttot // 512
            for oc in range(8):
                slq = wsl[oc % 3]
                slg = wsl2[oc % 2]
                p.dma(slq[:, :], wq[oc, :, :], q="pool", name=f"{tag}qd{oc}")
                p.dma(slg[:, :], wg[oc, :, :], q="pool", name=f"{tag}gd{oc}")
                for t5 in range(nt5):
                    psq = pA[:, (t5 % 2) * 512:(t5 % 2) * 512 + 512]
                    psg = pB[:, (t5 % 2) * 512:(t5 % 2) * 512 + 512]
                    p.mm(psq, [(slq[:, dc * 128:(dc + 1) * 128],
                                x_fm[:, dc * Ttot + t5 * 512: dc * Ttot + t5 * 512 + 512])
                               for dc in range(DC)], name=f"{tag}qmm{oc}_{t5}")
                    p.mm(psg, [(slg[:, dc * 128:(dc + 1) * 128],
                                x_fm[:, dc * Ttot + t5 * 512: dc * Ttot + t5 * 512 + 512])
                               for dc in range(DC)], name=f"{tag}gmm{oc}_{t5}")
                    emit_softplus(p, o_scr[:, 0:512], psq, s1, s2, f"{tag}spq{oc}_{t5}")
                    p.act(s1[:, :], psg, AF.Sigmoid, bias=bg_sb[:, oc:oc + 1], name=f"{tag}sig{oc}_{t5}")
                    p.stt(qf_fm[:, oc * Ttot + t5 * 512: oc * Ttot + t5 * 512 + 512],
                          s1[:, :], 1e-6, o_scr[:, 0:512], op0=ALU.add, op1=ALU.mult, name=f"{tag}qf{oc}_{t5}")

    p.memset(S_acc[:, :], 0.0, name="S0")
    l1_half(xprev_in, False, f"{L}p")
    for tc in range(n_tc):
        state_accum(tc, f"{L}ps{tc}")
    p.ts(S_acc[:, :], S_acc[:, :], ps_sb[:, 0:1], op0=ALU.mult, name=f"{L}psc")
    l1_half(x_in, True, f"{L}m")
    p.dma(wo_sb[:, 0:8 * D].rearrange("p (a b) -> p a b", a=8),
          wo[:, :, :].rearrange("a p b -> p a b"), q="pool", name="ld_wo")
    p.copy(S_bf[:, :], S_acc[:, :], name=f"{L}sb0")
    for tc in range(n_tc):
        T = f"{L}c{tc}"
        for h in range(H):
            ph = pQ[h % 2]
            slot = (h // 2) * 128
            r0 = (h % 2) * 64
            cs = (h // 2) * Ttot + tc * 128
            p.mm(ph[:, slot:slot + 128],
                 [(kf_fm[r0:r0 + 64, cs:cs + 128], qf_fm[r0:r0 + 64, cs:cs + 128])],
                 name=f"{T}sc{h}")
        for half in range(2):
            ph = pQ[half]
            p.tt(sc_sb[:, half * 1024:(half + 1) * 1024].rearrange("p (h t) -> p h t", h=8),
                 ph[:, 0:1024].rearrange("p (h t) -> p h t", h=8),
                 mask_sb[:, :].unsqueeze(1).broadcast_to([128, 8, 128]),
                 ALU.mult, name=f"{T}mk{half}")
        for h in range(H):
            pn = pQ[2 + h % 2]
            slot = (h // 2) * 128
            scc = ((h % 2) * 8 + h // 2) * 128
            r0 = (h % 2) * 64
            cs = (h // 2) * Ttot + tc * 128
            p.mmx([(pn[:, slot:slot + 65], sc_sb[:, scc:scc + 128],
                    v65[:, tc * H * 65 + h * 65: tc * H * 65 + (h + 1) * 65], True, False),
                   (pn[:, slot:slot + 65], qf_fm[r0:r0 + 64, cs:cs + 128],
                    S_bf[r0:r0 + 64, (h // 2) * 65:(h // 2 + 1) * 65], False, True)],
                  reads=[sc_sb[:, :], v65[:, :], qf_fm[:, :], S_bf[:, :]],
                  writes=[pn[:, slot:slot + 65]], name=f"{T}nm{h}")
        state_accum(tc, T)
        p.copy(S_bf[:, :], S_acc[:, :], name=f"{T}sbf")
        for half in range(2):
            pn = pQ[2 + half]
            p.copy(den_sb[:, half * 8:(half + 1) * 8],
                   pn[:, 0:1024].rearrange("p (h t) -> p h t", h=8)[:, :, 64], name=f"{T}dg{half}")
        p.ts(den_sb[:, :], den_sb[:, :], 1e-6, op0=ALU.add, name=f"{T}de")
        p.recip(den_sb[:, :], den_sb[:, :], name=f"{T}dr")
        for h in range(H):
            pn = pQ[2 + h % 2]
            slot = (h // 2) * 128
            p.ts(o_scr[:, h * 64:(h + 1) * 64], pn[:, slot:slot + 64],
                 den_sb[:, (h % 2) * 8 + h // 2:(h % 2) * 8 + h // 2 + 1],
                 op0=ALU.mult, name=f"{T}oe{h}")
        pTb2 = pA[:, :].bitcast(BF16)
        for oc in range(8):
            p.trans(pTb2[:, oc * 128:(oc + 1) * 128], o_scr[:, oc * 128:(oc + 1) * 128],
                    ident_bf[:, :], name=f"{T}ot{oc}")
        for oc in range(8):
            p.copy(ofm_scr[:, oc * 128:(oc + 1) * 128], pTb2[:, oc * 128:(oc + 1) * 128],
                   name=f"{T}oc{oc}")
        p.dma(x_scr[:, :], x_in[tc * 128:(tc + 1) * 128, :], q="sp", name=f"{T}rx")
        for dh in range(2):
            pso = pB[:, dh * 512:(dh + 1) * 512]
            p.mm(pso, [(ofm_scr[:, oc * 128:(oc + 1) * 128],
                        wo_sb[:, oc * D + dh * 512: oc * D + dh * 512 + 512])
                       for oc in range(8)], name=f"{T}op{dh}")
            p.tt(x_scr[:, dh * 512:(dh + 1) * 512], x_scr[:, dh * 512:(dh + 1) * 512],
                 pso, ALU.add, name=f"{T}ra{dh}")
        ln_stats_apply(p, x_scr[:, :], x_scr[:, :], lnscr, f"{T}ln")
        p.dma(x_out[tc * 128:(tc + 1) * 128, :], x_scr[:, :], q="sp", name=f"{T}sx")


def host_pack_attn1(inputs):
    d = {}
    Wq = np.asarray(inputs["L1_attn_Wq"], np.float32)
    Wk = np.asarray(inputs["L1_attn_Wk"], np.float32)
    Wv = np.asarray(inputs["L1_attn_Wv"], np.float32)
    Wg = np.asarray(inputs["L1_attn_Wg"], np.float32)
    Wo = np.asarray(inputs["L1_attn_Wo"], np.float32)
    d["L1a_wq"] = pack_proj_oc(Wq.T)
    d["L1a_wk"] = pack_proj_oc(Wk.T)
    d["L1a_wg"] = pack_proj_oc(Wg.T)
    d["L1a_wv"] = pack_proj_dc(Wv.T)
    d["L1a_wo"] = pack_proj_dc(Wo.T)
    d["L1a_bg"] = np.ascontiguousarray(
        np.asarray(inputs["L1_attn_bg"], np.float32).reshape(8, 128).T)
    d["maskst"] = np.triu(np.ones((128, 128), np.float32))
    d["identc"] = np.eye(128, dtype=np.float32)
    return d



def build_attn0(p, x_in, xprev_in, x_out, xprev_out=None, n_tc=16):
    L = "L0a"
    Ttot = n_tc * 128
    W3 = Ttot + 4
    HOF = 1
    DOF = HOF + 3

    wq = p.dram_in(f"{L}_wq", [8, 128, DC * 128], F32)
    wk = p.dram_in(f"{L}_wk", [8, 128, DC * 128], F32)
    wv = p.dram_in(f"{L}_wv", [8, 128, DC * 128], F32)
    wg = p.dram_in(f"{L}_wg", [DC, 128, D], F32)
    wo = p.dram_in(f"{L}_wo", [8, 128, D], F32)
    wgkb = p.dram_in(f"{L}_wgk", [128, DC * H], F32)
    wbb = p.dram_in(f"{L}_wb", [128, DC * H], F32)
    convw = p.dram_in(f"{L}_cw", [128, 3 * 8 * 4], F32)
    dtb_d = p.dram_in(f"{L}_dtb", [H, 1], F32)
    alog_d = p.dram_in(f"{L}_alog", [H, 1], F32)
    bb_d = p.dram_in(f"{L}_bb", [1, H], F32)
    gnorm_d = p.dram_in(f"{L}_gn", [128, 1], F32)
    ropec_d = p.dram_in(f"{L}_ropec", [128, n_tc * 16], F32)
    ropes_d = p.dram_in(f"{L}_ropes", [128, n_tc * 16], F32)
    ropecp_d = p.dram_in(f"{L}_ropecp", [128, n_tc * 16], F32)
    ropesp_d = p.dram_in(f"{L}_ropesp", [128, n_tc * 16], F32)
    xhalo_p = p.dram_in(f"{L}_xhp", [128, DC * 3], F32)
    xhalo_m = p.dram_in(f"{L}_xhm", [128, DC * 3], F32)
    pscale_d = p.dram_in(f"{L}_ps", [128, 1], F32)
    identc = p.dram_in("identc", [128, 128], F32)
    maskI_d = p.dram_in(f"{L}_mI", [128, 128], F32)
    maskSn_d = p.dram_in(f"{L}_mSn", [128, 128], F32)
    maskLn_d = p.dram_in(f"{L}_mLn", [128, 128], F32)
    pairsel_d = p.dram_in(f"{L}_psel", [H, 128], F32)
    pairdg_d = p.dram_in(f"{L}_pdg", [H, 8], F32)
    sel16_d = p.dram_in(f"{L}_sel", [H, H * 128], F32)
    identE_d = p.dram_in(f"{L}_idE", [H, H], F32)

    x_fm = p.sb(f"{L}_xfm", [128, max(DC * Ttot, 15360)], BF16)
    pre = p.sb(f"{L}_pre", [128, max(8 * W3, 16384)], BF16)
    q_tm = p.sb(f"{L}_qtm", [128, n_tc * D], BF16)
    k_tm = p.sb(f"{L}_ktm", [128, n_tc * D], BF16)
    v_tm = p.sb(f"{L}_vtm", [128, n_tc * D], BF16)
    gk_tm = p.sb(f"{L}_gktm", [128, n_tc * H], F32)
    beta_tm = p.sb(f"{L}_btm", [128, n_tc * H], F32)
    S_acc = p.sb(f"{L}_S", [128, 8 * DKH], F32)
    S_bf = p.sb(f"{L}_Sbf", [128, 8 * DKH], BF16)
    wsl = [p.sb(f"{L}_wsl{i}", [128, DC * 128], BF16) for i in range(2)]
    wosb = p.sb(f"{L}_wosb", [128, 8 * D], BF16)
    wgsl = None
    wgk_sb = p.sb(f"{L}_wgks", [128, DC * H], BF16)
    wb_sb = p.sb(f"{L}_wbs", [128, DC * H], BF16)
    cw_sb = p.sb(f"{L}_cws", [128, 96], F32)
    ropec = p.sb(f"{L}_rc", [128, n_tc * 16], F32)
    ropes = p.sb(f"{L}_rs", [128, n_tc * 16], F32)
    xhalo_sb = p.sb(f"{L}_xh", [128, DC * 3], BF16)
    gn_sb = p.sb(f"{L}_gns", [128, 1], F32)
    ps_sb = p.sb(f"{L}_pss", [128, 1], F32)
    dtb_sb = p.sb(f"{L}_dtbs", [H, 1], F32)
    nega_sb = p.sb(f"{L}_negas", [H, 1], F32)
    bb_sb = p.sb(f"{L}_bbs", [1, H], BF16)
    ident_bf = p.sb("identbf", [128, 128], BF16)
    ident_f = p.sb("identf", [128, 128], F32)
    maskI = p.sb(f"{L}_mIs", [128, 128], BF16)
    maskSn = p.sb(f"{L}_mSns", [128, 128], BF16)
    maskLn = p.sb(f"{L}_mLns", [128, 128], BF16)
    pairsel = p.sb(f"{L}_psels", [H, 128], F32)
    pairdg = p.sb(f"{L}_pdgs", [H, 8], F32)
    sel16 = p.sb(f"{L}_sels", [H, H * 128], F32)
    identE = p.sb(f"{L}_idEs", [H, H], F32)
    ones1 = p.sb("ones1", [1, 128], BF16)
    xfm_c = p.sb(f"{L}_xfmc", [128, D], BF16)
    mIf_sb = p.sb(f"{L}_mIfs", [128, 128], F32)
    Gfm_c = p.sb(f"{L}_Gfm", [H, 128], F32)
    bfm_c = p.sb(f"{L}_bfm", [H, 128], F32)
    st16 = p.sb(f"{L}_st16", [128, 16], F32)
    st16b = p.sb(f"{L}_st16b", [128, 16], F32)
    st16c = p.sb(f"{L}_st16c", [128, 16], F32)
    hcsc = p.sb(f"{L}_hcsc", [128, 32], F32)
    ebcp = p.sb(f"{L}_ebcp", [128, 8], F32)
    eda = p.sb(f"{L}_eda", [H, 16], F32)
    tsc = p.sb(f"{L}_tsc", [128, 512], BF16)
    x_scr = p.sb(f"{L}_xscr", [128, D], F32)
    s1 = x_scr[:, 512:1024]
    cscr = xfm_c[:, :].bitcast(F32)
    lnscr = dict(junk=xfm_c, st=p.sb(f"{L}_lst", [128, 16], F32))

    AR = x_fm[:, :]
    sc_sb = AR[:, 0:2048]
    MB_sb = AR[:, 2048:4096]
    Dst_sb = AR[:, 4096:6144]
    DmI_sb = AR[:, 6144:8192]
    DmS_sb = AR[:, 8192:10240]
    B_sb = AR[:, 10240:11264]
    U_sb = AR[:, 11264:12288]
    o_tmc = AR[:, 12288:13312]
    ofm_c = AR[:, 13312:14336]
    qgfm_c = AR[:, 14336:15360]
    gs_d = p.dram_tmp(f"{L}_gsd", [Ttot, D], BF16)
    AR2 = pre[:, :]
    Mst = AR2[:, 0:2048]
    Mmath = AR2[:, 2048:4096]
    M2st = AR2[:, 4096:6144]
    M2math = AR2[:, 6144:8192]
    bbc_sb = AR2[:, 8192:10240]
    kgfm_c = AR2[:, 10240:11264]
    tmp_tm = AR2[:, 11264:12288]
    ktb_c = AR2[:, 12288:13312]
    kfm_c = AR2[:, 13312:14336]
    qfm_c = AR2[:, 14336:15360]
    gs_c = AR2[:, 15360:16384]
    gs2_c = AR[:, 15360:16384]

    wgsl = wsl
    pAB = p.ps("pAB", [128, 2048], F32)
    pCD = p.ps("pCD", [128, 2048], F32)
    pQ = [pAB[:, 0:1024], pAB[:, 1024:2048], pCD[:, 0:1024], pCD[:, 1024:2048]]
    pA, pB, pC, pD = pQ

    p.dma(ident_bf[:, :], identc[:, :], q="pool", name="ld_ib")
    p.dma(ident_f[:, :], identc[:, :], q="sp", name="ld_if")
    p.dma(maskI[:, :], maskI_d[:, :], q="pool", name="ld_mI")
    p.dma(mIf_sb[:, :], maskI_d[:, :], q="sp", name="ld_mIf")
    p.dma(maskSn[:, :], maskSn_d[:, :], q="pool", name="ld_mSn")
    p.dma(maskLn[:, :], maskLn_d[:, :], q="pool", name="ld_mLn")
    p.dma(pairsel[:, :], pairsel_d[:, :], q="pool", name="ld_psel")
    p.dma(pairdg[:, :], pairdg_d[:, :], q="sp", name="ld_pdg")
    p.dma(sel16[:, :], sel16_d[:, :], q="pool", name="ld_sel")
    p.dma(identE[:, :], identE_d[:, :], q="sp", name="ld_idE")
    p.dma(wgk_sb[:, :], wgkb[:, :], q="pool", name="ld_wgk")
    p.dma(wb_sb[:, :], wbb[:, :], q="pool", name="ld_wb")
    p.dma(cw_sb[:, :], convw[:, :], q="sp", name="ld_cw")
    p.dma(gn_sb[:, :], gnorm_d[:, :], q="sp", name="ld_gn")
    p.dma(ps_sb[:, :], pscale_d[:, :], q="sp", name="ld_ps")
    p.dma(dtb_sb[:, :], dtb_d[:, :], q="sp", name="ld_dtb")
    p.dma(nega_sb[:, :], alog_d[:, :], q="sp", name="ld_alog")
    p.dma(bb_sb[:, :], bb_d[:, :], q="pool", name="ld_bb")
    p.memset(ones1[0:1, :], 1.0, name="ones")
    p.act(nega_sb[:, :], nega_sb[:, :], AF.Exp, name="negA_e")
    p.ts(nega_sb[:, :], nega_sb[:, :], -1.0, name="negA_n")
    p.memset(S_acc[:, :], 0.0, name="S0")
    p.memset(S_bf[:, :], 0.0, name="Sbf0")

    def cast_transpose_half(x_dram, tag):
        def src(tc):
            p.dma(x_scr[:, :], x_dram[tc * 128:(tc + 1) * 128, :], q="sp", name=f"{tag}lx{tc}")
            return x_scr[:, :]
        emit_cast_transpose_fm(p, src, x_fm, range(n_tc), Ttot, xfm_c,
                               pD, ident_bf, f"{tag}xt")

    def proj_conv_tensor(which, wblob, cw_off, dst_tm, halo_sb, tag):
        nt5 = Ttot // 512
        for oc in range(8):
            sl = wsl[oc % 2]
            p.dma(sl[:, :], wblob[oc, :, :], q="pool", name=f"{tag}d{oc}")
            phh = pA[:, 0:3]
            p.mm(phh, [(sl[:, dc * 128:(dc + 1) * 128], halo_sb[:, dc * 3:(dc + 1) * 3])
                       for dc in range(DC)], name=f"{tag}h{oc}")
            p.copy(pre[:, oc * W3 + HOF: oc * W3 + HOF + 3], phh, name=f"{tag}he{oc}")
            for t5 in range(nt5):
                ph = pQ[1 + t5 % 2]
                ps_ = ph[:, 0:512]
                p.mm(ps_, [(sl[:, dc * 128:(dc + 1) * 128],
                            x_fm[:, dc * Ttot + t5 * 512: dc * Ttot + t5 * 512 + 512])
                           for dc in range(DC)], name=f"{tag}mm{oc}_{t5}")
                p.copy(pre[:, oc * W3 + DOF + t5 * 512: oc * W3 + DOF + t5 * 512 + 512],
                       ps_, name=f"{tag}pe{oc}_{t5}")
            for t5 in range(nt5):
                base = oc * W3 + DOF + t5 * 512
                p.ts(cscr[:, :], pre[:, base - 3: base - 3 + 512],
                     cw_sb[:, cw_off + oc * 4: cw_off + oc * 4 + 1], op0=ALU.mult,
                     name=f"{tag}c0_{oc}_{t5}")
                for kk in range(1, 4):
                    p.stt(cscr[:, :], pre[:, base - 3 + kk: base - 3 + kk + 512],
                          cw_sb[:, cw_off + oc * 4 + kk: cw_off + oc * 4 + kk + 1],
                          cscr[:, :], op0=ALU.mult, op1=ALU.add, name=f"{tag}c{kk}_{oc}_{t5}")
                p.act(tsc[:, :], cscr[:, :], AF.Silu, name=f"{tag}si{oc}_{t5}")
                pTb = pD[:, :].bitcast(BF16)
                for i in range(4):
                    p.trans(pTb[:, i * 128:(i + 1) * 128], tsc[:, i * 128:(i + 1) * 128],
                            ident_bf[:, :], name=f"{tag}tr{oc}_{t5}_{i}")
                p.copy(dst_tm[:, 0:n_tc * D].rearrange("p (tc d) -> p tc d", tc=n_tc)
                       [:, t5 * 4:(t5 + 1) * 4, oc * 128:(oc + 1) * 128],
                       pTb[:, 0:512].rearrange("p (i t) -> p i t", i=4),
                       name=f"{tag}te{oc}_{t5}")

    def gkbeta_pass(tag):
        nt5 = Ttot // 512
        for t5 in range(nt5):
            psg = pA[0:H, 0:512]
            pairs = [(wgk_sb[:, dc * H:(dc + 1) * H],
                      x_fm[:, dc * Ttot + t5 * 512: dc * Ttot + t5 * 512 + 512])
                     for dc in range(DC)]
            p.mm(psg, pairs, name=f"{tag}gkmm{t5}")
            p.act(s1[0:H, :], psg, AF.Abs, bias=dtb_sb[0:H, 0:1], name=f"{tag}gka{t5}")
            p.act(s1[0:H, :], s1[0:H, :], AF.Exp, scale=-1.0, name=f"{tag}gke{t5}")
            p.act(s1[0:H, :], s1[0:H, :], AF.Ln, bias=1.0, name=f"{tag}gkl{t5}")
            p.act(cscr[0:H, :], psg, AF.Relu, bias=dtb_sb[0:H, 0:1], name=f"{tag}gkr{t5}")
            p.tt(cscr[0:H, :], cscr[0:H, :], s1[0:H, :], ALU.add, name=f"{tag}gks{t5}")
            p.ts(cscr[0:H, :], cscr[0:H, :], nega_sb[0:H, 0:1], op0=ALU.mult, name=f"{tag}gkn{t5}")
            for i in range(4):
                ptt = pB[:, 0:16]
                p.trans(ptt, cscr[0:H, i * 128:(i + 1) * 128], identE[:, :], name=f"{tag}gkt{t5}_{i}")
                tc = t5 * 4 + i
                p.copy(gk_tm[:, tc * H:(tc + 1) * H], ptt, name=f"{tag}gkte{t5}_{i}")
        for tc in range(n_tc):
            psb_ = pC[:, 0:H]
            pairs = [(x_fm[:, dc * Ttot + tc * 128: dc * Ttot + tc * 128 + 128],
                      wb_sb[:, dc * H:(dc + 1) * H]) for dc in range(DC)]
            pairs.append((ones1[0:1, :], bb_sb[0:1, :]))
            p.mm(psb_, pairs, name=f"{tag}bmm{tc}")
            p.act(beta_tm[:, tc * H:(tc + 1) * H], psb_, AF.Sigmoid, name=f"{tag}bsig{tc}")

    def l2_rope(dst_tm, tag, rc=None, rs=None):
        rc = ropec if rc is None else rc
        rs = ropes if rs is None else rs
        """l2norm (per head) + rope(32) in TM, in place, per tc."""
        for tc in range(n_tc):
            t3 = dst_tm[:, tc * D:(tc + 1) * D].rearrange("p (h d) -> p h d", h=H)
            p.act(xfm_c[:, :], dst_tm[:, tc * D:(tc + 1) * D], AF.Square, name=f"{tag}sq{tc}")
            p.red(st16[:, :], xfm_c[:, :].rearrange("p (h d) -> p h d", h=H), ALU.add,
                  name=f"{tag}nrm{tc}")
            p.ts(st16[:, :], st16[:, :], 1e-6, op0=ALU.add, name=f"{tag}ne{tc}")
            p.recip(st16[:, :], st16[:, :], name=f"{tag}nr{tc}")
            p.act(st16[:, :], st16[:, :], AF.Sqrt, name=f"{tag}ns{tc}")
            p.tt(t3, t3, st16[:, :].unsqueeze(2).broadcast_to([128, H, 64]), ALU.mult,
                 name=f"{tag}nm{tc}")
            x1 = t3[:, :, 0:16]
            x2 = t3[:, :, 16:32]
            cosb = rc[:, tc * 16:(tc + 1) * 16].unsqueeze(1).broadcast_to([128, H, 16])
            sinb = rs[:, tc * 16:(tc + 1) * 16].unsqueeze(1).broadcast_to([128, H, 16])
            sA = s1[:, 0:256].rearrange("p (h d) -> p h d", h=H)
            sB = cscr[:, 0:256].rearrange("p (h d) -> p h d", h=H)
            p.tt(sA, x1, sinb, ALU.mult, name=f"{tag}r1{tc}")
            p.tt(sB, x1, cosb, ALU.mult, name=f"{tag}r2{tc}")
            p.tt(x1, x2, sinb, ALU.mult, name=f"{tag}r3{tc}")
            p.tt(x1, sB, x1, ALU.subtract, name=f"{tag}r4{tc}")
            p.tt(x2, x2, cosb, ALU.mult, name=f"{tag}r5{tc}")
            p.tt(x2, x2, sA, ALU.add, name=f"{tag}r6{tc}")

    def chunk_pass(tc, do_out, tag, x_dram=None, x_outd=None):
        T = tag
        pG3 = pB[:, 0:16]
        p.mm(pG3, [(mIf_sb[:, :], gk_tm[:, tc * H:(tc + 1) * H])], name=f"{T}Gmm")
        p.copy(st16[:, :], pG3, name=f"{T}Gtm")
        pG2 = pA[0:H, 0:128]
        p.trans(pG2, st16[:, :], ident_f[:, :], name=f"{T}Gt")
        p.copy(Gfm_c[:, :], pG2, name=f"{T}Gf")
        p.ts(eda[:, 0:1], Gfm_c[:, 127:128], 1.0, op0=ALU.mult, name=f"{T}GL")
        p.ts(bfm_c[:, :], Gfm_c[:, :], eda[:, 0:1], op0=ALU.subtract, name=f"{T}E2a")
        p.act(bfm_c[:, :], bfm_c[:, :], AF.Exp, scale=-1.0, name=f"{T}E2e")
        pG4 = pC[:, 0:16]
        p.trans(pG4, bfm_c[:, :], identE[:, :], name=f"{T}E2t")
        p.tt(lnscr["st"][:, 0:16], beta_tm[:, tc * H:(tc + 1) * H], pG4, ALU.mult,
             name=f"{T}fb")
        if (not do_out) and tc == 0:
            p.dbg("pfb0", lnscr["st"][:, 0:16])
            p.dbg("pGtm0", st16[:, :])
        p.act(st16c[:, :], st16[:, :], AF.Exp, name=f"{T}eG")
        p.act(eda[:, 0:1], eda[:, 0:1], AF.Exp, name=f"{T}eL")
        p.tt(eda[:, 8:16], pairdg[:, :], eda[:, 0:1].broadcast_to([H, 8]), ALU.mult,
             name=f"{T}eD2")
        pEB = pD[:, 0:8]
        p.mm(pEB, [(pairsel[:, :], eda[:, 8:16])], name=f"{T}ebm")
        p.copy(ebcp[:, :], pEB, name=f"{T}ebe")
        pTb = pD[:, :].bitcast(BF16)
        for oc in range(8):
            p.trans(pTb[:, oc * 128:(oc + 1) * 128],
                    k_tm[:, tc * D + oc * 128: tc * D + oc * 128 + 128], ident_bf[:, :],
                    name=f"{T}ktr{oc}")
        p.copy(kfm_c[:, 0:1024], pTb[:, 0:1024], eng="act", name=f"{T}kte")
        eGb = st16c[:, :].unsqueeze(2).broadcast_to([128, H, 64])
        fbb = lnscr["st"][:, 0:16].unsqueeze(2).broadcast_to([128, H, 64])
        k3 = k_tm[:, tc * D:(tc + 1) * D].rearrange("p (h d) -> p h d", h=H)
        p.tt(tmp_tm[:, :].rearrange("p (h d) -> p h d", h=H), k3, eGb, ALU.mult,
             name=f"{T}kg")
        p.ts(tmp_tm[:, :], tmp_tm[:, :], -1.0, name=f"{T}kgn")
        p.tt(ktb_c[:, :].rearrange("p (h d) -> p h d", h=H), k3, fbb, ALU.mult,
             name=f"{T}ktb")
        if (not do_out) and tc == 0:
            p.dbg("pkg0", tmp_tm[:, :])
            p.dbg("peG0", st16c[:, :])
            p.dbg("pktmc0", k_tm[:, 0:D])
        for oc in range(8):
            p.trans(pTb[:, oc * 128:(oc + 1) * 128],
                    tmp_tm[:, oc * 128:(oc + 1) * 128], ident_bf[:, :], name=f"{T}kgt{oc}")
        p.copy(kgfm_c[:, 0:1024], pTb[:, 0:1024], eng="act", name=f"{T}kge")
        if do_out:
            for oc in range(8):
                p.trans(pTb[:, oc * 128:(oc + 1) * 128],
                        q_tm[:, tc * D + oc * 128: tc * D + oc * 128 + 128], ident_bf[:, :],
                        name=f"{T}qtr{oc}")
            p.copy(qfm_c[:, 0:1024], pTb[:, 0:1024], eng="act", name=f"{T}qte")
            q3 = q_tm[:, tc * D:(tc + 1) * D].rearrange("p (h d) -> p h d", h=H)
            p.tt(tmp_tm[:, :].rearrange("p (h d) -> p h d", h=H), q3, eGb, ALU.mult,
                 name=f"{T}qg")
            for oc in range(8):
                p.trans(pTb[:, oc * 128:(oc + 1) * 128],
                        tmp_tm[:, oc * 128:(oc + 1) * 128], ident_bf[:, :], name=f"{T}qgt{oc}")
            p.copy(qgfm_c[:, 0:1024], pTb[:, 0:1024], eng="act", name=f"{T}qge")
        pG5 = pB[0:H, 0:128]
        p.trans(pG5, beta_tm[:, tc * H:(tc + 1) * H], ident_f[:, :], name=f"{T}bt")
        p.copy(bfm_c[:, :], pG5, name=f"{T}bte")
        for h in range(H):
            px = pQ[h % 2]
            slot = (h // 2) * 128
            r0 = (h % 2) * 64
            cs = (h // 2) * 128
            p.mm(px[:, slot:slot + 128],
                 [(kfm_c[r0:r0 + 64, cs:cs + 128], kfm_c[r0:r0 + 64, cs:cs + 128])],
                 name=f"{T}skk{h}")
        p.copy(sc_sb[:, 0:2048], pAB[:, 0:2048], eng="act", name=f"{T}skke")
        if do_out:
            for h in range(H):
                px = pQ[2 + h % 2]
                slot = (h // 2) * 128
                r0 = (h % 2) * 64
                cs = (h // 2) * 128
                p.mm(px[:, slot:slot + 128],
                     [(kfm_c[r0:r0 + 64, cs:cs + 128], qfm_c[r0:r0 + 64, cs:cs + 128])],
                     name=f"{T}skq{h}")
        p.copy(hcsc[:, 0:16].rearrange("p (q a) -> p q a", q=2),
               st16[:, :].rearrange("p (a q) -> p q a", q=2), name=f"{T}ghc")
        p.copy(hcsc[:, 16:32].rearrange("p (q a) -> p q a", q=2),
               beta_tm[:, tc * H:(tc + 1) * H].rearrange("p (a q) -> p q a", q=2),
               name=f"{T}bhc")
        g_bc = hcsc[:, 0:16].unsqueeze(2).broadcast_to([128, 16, 128])
        b_bc = hcsc[:, 16:32].unsqueeze(2).broadcast_to([128, 16, 128])
        for h in range(H):
            px = pQ[h % 2]
            slot = (h // 2) * 128
            p.mm(px[:, slot:slot + 128], [(sel16[:, h * 128:(h + 1) * 128], Gfm_c[:, :])],
                 name=f"{T}cbG{h}")
        p.tt(Dst_sb[:, 0:2048].rearrange("p (hc t) -> p hc t", hc=16),
             pAB[:, 0:2048].rearrange("p (hc t) -> p hc t", hc=16), g_bc,
             ALU.subtract, name=f"{T}dsc")
        p.ts(Dst_sb[:, 0:2048], Dst_sb[:, 0:2048], 0.0, op0=ALU.min, name=f"{T}dmn")
        p.act(Dst_sb[:, 0:2048], Dst_sb[:, 0:2048], AF.Exp, name=f"{T}dst")
        mi3 = maskI[:, :].unsqueeze(1).broadcast_to([128, 16, 128])
        ms3 = maskSn[:, :].unsqueeze(1).broadcast_to([128, 16, 128])
        dsv = Dst_sb[:, 0:2048].rearrange("p (hc t) -> p hc t", hc=16)
        p.tt(DmI_sb[:, 0:2048].rearrange("p (hc t) -> p hc t", hc=16), dsv, mi3,
             ALU.mult, name=f"{T}dmi")
        p.tt(DmS_sb[:, 0:2048].rearrange("p (hc t) -> p hc t", hc=16), dsv, ms3,
             ALU.mult, name=f"{T}dms")
        p.tt(Mst[:, 0:2048], sc_sb[:, 0:2048], DmS_sb[:, 0:2048], ALU.mult,
             name=f"{T}msa")
        p.tt(Mst[:, 0:2048].rearrange("p (hc t) -> p hc t", hc=16),
             Mst[:, 0:2048].rearrange("p (hc t) -> p hc t", hc=16), b_bc,
             ALU.mult, name=f"{T}msb")
        if do_out:
            p.tt(MB_sb[:, 0:2048], pCD[:, 0:2048], DmI_sb[:, 0:2048],
                 ALU.mult, name=f"{T}mba")
            p.tt(MB_sb[:, 0:2048].rearrange("p (hc t) -> p hc t", hc=16),
                 MB_sb[:, 0:2048].rearrange("p (hc t) -> p hc t", hc=16), b_bc,
                 ALU.mult, name=f"{T}mbb")
        for h in range(H):
            px = pQ[2 + h % 2]
            slot = (h // 2) * 128
            p.mm(px[:, slot:slot + 128], [(sel16[:, h * 128:(h + 1) * 128], Gfm_c[:, :])],
                 name=f"{T}cbG2{h}")
        p.tt(Dst_sb[:, 0:2048].rearrange("p (hc t) -> p hc t", hc=16),
             pCD[:, 0:2048].rearrange("p (hc t) -> p hc t", hc=16), g_bc,
             ALU.subtract, name=f"{T}dtc")
        p.ts(Dst_sb[:, 0:2048], Dst_sb[:, 0:2048], 0.0, op0=ALU.max, name=f"{T}dmx")
        p.act(Dst_sb[:, 0:2048], Dst_sb[:, 0:2048], AF.Exp, scale=-1.0, name=f"{T}dts")
        for h in range(H):
            px = pQ[h % 2]
            slot = (h // 2) * 128
            p.mm(px[:, slot:slot + 128],
                 [(sel16[:, h * 128:(h + 1) * 128], bfm_c[:, :])], name=f"{T}cbB{h}")
        p.copy(bbc_sb[:, 0:2048], pAB[:, 0:2048], eng="act", name=f"{T}cbBe")
        ml3 = maskLn[:, :].unsqueeze(1).broadcast_to([128, 16, 128])
        p.tt(Mmath[:, 0:2048].rearrange("p (hc t) -> p hc t", hc=16),
             Dst_sb[:, 0:2048].rearrange("p (hc t) -> p hc t", hc=16), ml3,
             ALU.mult, name=f"{T}mm1")
        p.tt(Mmath[:, 0:2048], Mmath[:, 0:2048], sc_sb[:, 0:2048], ALU.mult, name=f"{T}mm2")
        p.tt(Mmath[:, 0:2048], Mmath[:, 0:2048], bbc_sb[:, 0:2048], ALU.mult, name=f"{T}mm3")
        for h in range(H):
            px = pQ[h % 2]
            r0 = (h % 2) * 64
            sl_ = px[:, (h // 2) * 64:(h // 2) * 64 + 64]
            p.mmx([(sl_, ident_bf[:, :], v_tm[:, tc * D + h * 64: tc * D + h * 64 + 64],
                    True, False),
                   (sl_, kgfm_c[r0:r0 + 64, (h // 2) * 128:(h // 2) * 128 + 128],
                    S_bf[r0:r0 + 64, (h // 2) * 64:(h // 2) * 64 + 64], False, True)],
                  reads=[ident_bf[:, :], v_tm[:, :], kgfm_c[:, :], S_bf[:, :]],
                  writes=[sl_], name=f"{T}B{h}")
        p.copy(B_sb[:, 0:1024].rearrange("p (q w) -> p q w", q=2),
               pAB[:, 0:2048].rearrange("p (q w) -> p q w", q=2)[:, :, 0:512],
               eng="act", name=f"{T}Be")
        cm, cs_, nm_, ns_ = Mmath, Mst, M2math, M2st
        src_x, dst_x = B_sb, U_sb
        for lev in range(7):
            for h in range(H):
                px = pQ[h % 2]
                hc = ((h % 2) * 8 + h // 2)
                p.mm1(px[:, (h // 2) * 64:(h // 2) * 64 + 64],
                      cs_[:, hc * 128:hc * 128 + 128],
                      src_x[:, hc * 64:hc * 64 + 64], True, True, name=f"{T}ap{lev}_{h}")
            p.tt(dst_x[:, 0:1024].rearrange("p (q w) -> p q w", q=2),
                 src_x[:, 0:1024].rearrange("p (q w) -> p q w", q=2),
                 pAB[:, 0:2048].rearrange("p (q w) -> p q w", q=2)[:, :, 0:512],
                 ALU.add, name=f"{T}ape{lev}")
            src_x, dst_x = dst_x, src_x
            if lev < 6:
                for h in range(H):
                    hc = ((h % 2) * 8 + h // 2)
                    px = pQ[2 + h % 2]
                    p.mm(px[:, (h // 2) * 128:(h // 2) * 128 + 128],
                         [(cs_[:, hc * 128:hc * 128 + 128], cm[:, hc * 128:hc * 128 + 128])],
                         name=f"{T}sqm{lev}_{h}")
                p.copy(nm_[:, 0:2048], pCD[:, 0:2048], eng="act", name=f"{T}sqme{lev}")
                for h in range(H):
                    hc = ((h % 2) * 8 + h // 2)
                    px = pQ[2 + h % 2]
                    p.mm(px[:, (h // 2) * 128:(h // 2) * 128 + 128],
                         [(cm[:, hc * 128:hc * 128 + 128], cs_[:, hc * 128:hc * 128 + 128])],
                         name=f"{T}sqs{lev}_{h}")
                p.copy(ns_[:, 0:2048], pCD[:, 0:2048], eng="act", name=f"{T}sqse{lev}")
                cm, cs_, nm_, ns_ = nm_, ns_, cm, cs_
        U_fin = src_x
        if (not do_out) and tc == 0:
            p.dbg("pU0", U_fin[:, 0:1024])
            p.dbg("pMst0", Mst[:, 0:2048])
            p.dbg("pMmath0", Mmath[:, 0:2048])
            p.dbg("pB0", B_sb[:, 0:1024])
            p.dbg("pktb0", ktb_c[:, :])
        if do_out and tc == 0:
            p.dbg("U0", U_fin[:, 0:1024])
            p.dbg("G0", Gfm_c[:, :])
            p.dbg("Mst0", Mst[:, 0:2048])
            p.dbg("MB0", MB_sb[:, 0:2048])
            p.dbg("sc0", sc_sb[:, 0:2048])
            p.dbg("B0", B_sb[:, 0:1024])
        if do_out:
            for h in range(H):
                px = pQ[2 + h % 2]
                hc = ((h % 2) * 8 + h // 2)
                r0 = (h % 2) * 64
                p.mmx([(px[:, (h // 2) * 64:(h // 2) * 64 + 64],
                        MB_sb[:, hc * 128:hc * 128 + 128], U_fin[:, hc * 64:hc * 64 + 64],
                        True, False),
                       (px[:, (h // 2) * 64:(h // 2) * 64 + 64],
                        qgfm_c[r0:r0 + 64, (h // 2) * 128:(h // 2) * 128 + 128],
                        S_bf[r0:r0 + 64, (h // 2) * 64:(h // 2) * 64 + 64], False, True)],
                      reads=[MB_sb, U_fin, qgfm_c[:, :], S_bf[:, :]],
                      writes=[px[:, (h // 2) * 64:(h // 2) * 64 + 64]], name=f"{T}O{h}")
        for h in range(H):
            px = pQ[h % 2]
            p.mm1(px[0:64, (h // 2) * 64:(h // 2) * 64 + 64],
                  ktb_c[:, h * 64:(h + 1) * 64],
                  U_fin[:, ((h % 2) * 8 + h // 2) * 64:((h % 2) * 8 + h // 2) * 64 + 64],
                  True, True, name=f"{T}su{h}")
        for q in range(2):
            r0 = q * 64
            Sg = S_acc[r0:r0 + 64, :].rearrange("p (a d) -> p a d", a=8)
            eb = ebcp[r0:r0 + 64, :].unsqueeze(2).broadcast_to([64, 8, 64])
            p.tt(Sg, Sg, eb, ALU.mult, name=f"{T}ssm{q}")
            p.tt(S_acc[r0:r0 + 64, :], S_acc[r0:r0 + 64, :], pQ[q][0:64, 0:512],
                 ALU.add, name=f"{T}ssa{q}")
        p.copy(S_bf[:, :], S_acc[:, :], name=f"{T}sbf")
        if (not do_out) and tc == 0:
            p.dbg("pS1", S_acc[:, :])
        if do_out:
            for hh in range(2):
                px = pQ[2 + hh]
                dst3 = o_tmc[:, :].rearrange("p (oc two d) -> p oc two d", oc=8, two=2)[:, :, hh, :]
                p.copy(dst3, px[:, 0:512].rearrange("p (a b) -> p a b", a=8),
                       name=f"{T}oe{hh}")
            p.act(xfm_c[:, :], o_tmc[:, :], AF.Square, name=f"{T}osq")
            p.red(st16b[:, :], xfm_c[:, :].rearrange("p (h d) -> p h d", h=H), ALU.add,
                  name=f"{T}orn")
            p.ts(st16b[:, :], st16b[:, :], 1.0 / 64.0, 1e-5, op0=ALU.mult, op1=ALU.add,
                 name=f"{T}ore")
            p.recip(st16b[:, :], st16b[:, :], name=f"{T}orr")
            p.act(st16b[:, :], st16b[:, :], AF.Sqrt, name=f"{T}ors")
            p.tt(o_tmc[:, :].rearrange("p (h d) -> p h d", h=H),
                 o_tmc[:, :].rearrange("p (h d) -> p h d", h=H),
                 st16b[:, :].unsqueeze(2).broadcast_to([128, H, 64]), ALU.mult,
                 name=f"{T}orm")
            p.dma(x_scr[:, :], x_dram[tc * 128:(tc + 1) * 128, :], q="sp", name=f"{T}gx")
            gbuf = gs_c if tc % 2 == 0 else gs2_c
            p.dma(gbuf[:, 0:1024], gs_d[tc * 128:(tc + 1) * 128, :], q="act",
                  name=f"{T}gl")
            pTb2 = pD[:, :].bitcast(BF16)
            if tc == 0:
                p.dbg("orms0", o_tmc[:, :])
            p.tt(o_tmc[:, :], o_tmc[:, :], gbuf[:, 0:1024], ALU.mult, name=f"{T}og")
            if tc == 0:
                p.dbg("o0", o_tmc[:, :])
            for oc in range(8):
                p.trans(pTb2[:, oc * 128:(oc + 1) * 128], o_tmc[:, oc * 128:(oc + 1) * 128],
                        ident_bf[:, :], name=f"{T}ot{oc}")
            p.copy(ofm_c[:, 0:1024], pTb2[:, 0:1024], eng="act", name=f"{T}ote")
            for dh in range(2):
                pso = pB[:, dh * 512:(dh + 1) * 512]
                p.mm(pso, [(ofm_c[:, oc * 128:(oc + 1) * 128],
                            wosb[:, oc * D + dh * 512: oc * D + dh * 512 + 512])
                           for oc in range(8)], name=f"{T}op{dh}")
                p.tt(x_scr[:, dh * 512:(dh + 1) * 512], x_scr[:, dh * 512:(dh + 1) * 512],
                     pso, ALU.add, name=f"{T}ra{dh}")
            ln_stats_apply(p, x_scr[:, :], x_scr[:, :], lnscr, f"{T}ln")
            p.dma(x_outd[tc * 128:(tc + 1) * 128, :], x_scr[:, :], q="sp", name=f"{T}sx")

    def half_pass(x_dram, halo_sb_src, do_out, tag, is_prefix=False, x_outd=None):
        p.dma(ropec[:, :], ropecp_d[:, :] if is_prefix else ropec_d[:, :], q="sp", name=f"{tag}lrc")
        p.dma(ropes[:, :], ropesp_d[:, :] if is_prefix else ropes_d[:, :], q="sp", name=f"{tag}lrs")
        p.dma(xhalo_sb[:, :], halo_sb_src[:, :], q="pool", name=f"{tag}lh")
        cast_transpose_half(x_dram, tag)
        proj_conv_tensor("q", wq, 0, q_tm, xhalo_sb, f"{tag}q") if do_out else None
        proj_conv_tensor("k", wk, 32, k_tm, xhalo_sb, f"{tag}k")
        proj_conv_tensor("v", wv, 64, v_tm, xhalo_sb, f"{tag}v")
        gkbeta_pass(tag)
        if do_out:
            l2_rope(q_tm, f"{tag}lrq")
        l2_rope(k_tm, f"{tag}lrk")
        if do_out:
            p.dbg("qtm", q_tm[:, :])
            p.dbg("ktm", k_tm[:, :])
            p.dbg("vtm", v_tm[:, :])
            p.dbg("gktm", gk_tm[:, :])
            p.dbg("btm", beta_tm[:, :])
        if do_out:
            p.dma(wosb[:, :].rearrange("p (a b) -> p a b", a=8),
                  wo[:, :, :].rearrange("a p b -> p a b"), q="pool", name=f"{tag}lwo")
            p.ts(wosb[:, :], wosb[:, :], gn_sb[:, 0:1], op0=ALU.mult, name=f"{tag}gn")
            for dc in range(DC):
                p.dma(AR2[:, dc * 1024:(dc + 1) * 1024], wg[dc, :, :], q="pool",
                      name=f"{tag}gw{dc}")
            for tc in range(n_tc):
                gstage = gs_c if tc % 2 == 0 else AR2[:, 8192:9216]
                for oh in range(2):
                    psg = pQ[oh][:, 0:512]
                    pairs = [(x_fm[:, dc * Ttot + tc * 128: dc * Ttot + tc * 128 + 128],
                              AR2[:, dc * 1024 + oh * 512: dc * 1024 + oh * 512 + 512])
                             for dc in range(DC)]
                    p.mm(psg, pairs, name=f"{tag}gp{tc}_{oh}")
                    p.act(gstage[:, oh * 512:(oh + 1) * 512], psg, AF.Silu,
                          name=f"{tag}gsv{tc}_{oh}")
                p.dma(gs_d[tc * 128:(tc + 1) * 128, :], gstage[:, 0:1024], q="act",
                      name=f"{tag}gst{tc}")
        for tc in range(n_tc):
            chunk_pass(tc, do_out, f"{tag}c{tc}", x_dram=x_dram, x_outd=x_outd)

    half_pass(xprev_in, xhalo_p, xprev_out is not None, f"{L}p",
              is_prefix=True, x_outd=xprev_out)
    p.ts(S_acc[:, :], S_acc[:, :], ps_sb[:, 0:1], op0=ALU.mult, name=f"{L}psc")
    p.copy(S_bf[:, :], S_acc[:, :], name=f"{L}psbf")
    p.dbg("Spre", S_acc[:, :])
    half_pass(x_in, xhalo_m, True, f"{L}m", is_prefix=False, x_outd=x_out)


def host_pack_attn0(inputs, core_pos0=0, halo_main=None, pscale=0.0, n_tc=16):
    d = {}
    Wq = np.asarray(inputs["L0_attn_Wq"], np.float32)
    Wk = np.asarray(inputs["L0_attn_Wk"], np.float32)
    Wv = np.asarray(inputs["L0_attn_Wv"], np.float32)
    Wg = np.asarray(inputs["L0_attn_Wg"], np.float32)
    Wo = np.asarray(inputs["L0_attn_Wo"], np.float32)
    d["L0a_wq"] = pack_proj_oc(Wq)
    d["L0a_wk"] = pack_proj_oc(Wk)
    d["L0a_wv"] = pack_proj_oc(Wv)
    d["L0a_wg"] = pack_proj_dc(Wg)
    d["L0a_wo"] = pack_proj_dc(Wo)
    d["L0a_wgk"] = np.ascontiguousarray(
        np.asarray(inputs["L0_attn_Wgk"], np.float32).reshape(DC, 128, H)
        .transpose(1, 0, 2).reshape(128, DC * H))
    d["L0a_wb"] = np.ascontiguousarray(
        np.asarray(inputs["L0_attn_Wb"], np.float32).reshape(DC, 128, H)
        .transpose(1, 0, 2).reshape(128, DC * H))
    cw = np.zeros((128, 96), np.float32)
    for i, nm in enumerate(("conv_q_w", "conv_k_w", "conv_v_w")):
        w = np.asarray(inputs[f"L0_attn_{nm}"], np.float32)
        cw[:, i * 32:(i + 1) * 32] = w.reshape(8, 128, 4).transpose(1, 0, 2).reshape(128, 32)
    d["L0a_cw"] = cw
    d["L0a_dtb"] = np.asarray(inputs["L0_attn_dt_bias"], np.float32).reshape(H, 1)
    d["L0a_alog"] = np.asarray(inputs["L0_attn_A_log"], np.float32).reshape(H, 1)
    d["L0a_bb"] = np.asarray(inputs["L0_attn_bb"], np.float32).reshape(1, H)
    d["L0a_gn"] = np.tile(np.asarray(inputs["L0_attn_gnorm_w"], np.float32), 2).reshape(128, 1)
    Ttot = n_tc * 128
    inv = 10000.0 ** (-np.arange(16) / 16.0)

    def _tab(p0):
        ang = (p0 + np.arange(Ttot))[:, None] * inv[None, :]
        c = np.ascontiguousarray(np.cos(ang).astype(np.float32)
                                 .reshape(n_tc, 128, 16).transpose(1, 0, 2).reshape(128, n_tc * 16))
        s_ = np.ascontiguousarray(np.sin(ang).astype(np.float32)
                                  .reshape(n_tc, 128, 16).transpose(1, 0, 2).reshape(128, n_tc * 16))
        return c, s_
    d["L0a_ropec"], d["L0a_ropes"] = _tab(core_pos0)
    d["L0a_ropecp"], d["L0a_ropesp"] = _tab(max(core_pos0 - Ttot, 0))
    d["L0a_xhp"] = np.zeros((128, DC * 3), np.float32)
    if halo_main is None:
        d["L0a_xhm"] = np.zeros((128, DC * 3), np.float32)
    else:
        d["L0a_xhm"] = np.ascontiguousarray(
            halo_main.T.reshape(DC, 128, 3).transpose(1, 0, 2).reshape(128, DC * 3)).astype(np.float32)
    d["L0a_ps"] = np.full((128, 1), pscale, np.float32)
    s = np.arange(128)
    d["L0a_mI"] = (s[:, None] <= s[None, :]).astype(np.float32)
    d["L0a_mSn"] = -(s[:, None] < s[None, :]).astype(np.float32)
    d["L0a_mLn"] = -(s[:, None] > s[None, :]).astype(np.float32)
    ps_ = np.zeros((H, 128), np.float32)
    for h in range(H):
        ps_[h, :] = ((np.arange(128) >= 64) == (h % 2 == 1)).astype(np.float32)
    d["L0a_psel"] = ps_
    pd_ = np.zeros((H, 8), np.float32)
    for h in range(H):
        pd_[h, h // 2] = 1.0
    d["L0a_pdg"] = pd_
    sel = np.zeros((H, H * 128), np.float32)
    for h in range(H):
        sel[h, h * 128:(h + 1) * 128] = 1.0
    d["L0a_sel"] = sel
    d["L0a_idE"] = np.eye(H, dtype=np.float32)
    d["identc"] = np.eye(128, dtype=np.float32)
    return d


# version 10 (final)
# speedup vs baseline: 1.0903x; 1.0166x over previous
import numpy as np
import ml_dtypes
from contextlib import ExitStack

import concourse.bass as bass
import concourse.mybir as mybir
from concourse.bass_utils import run_bass_kernel_spmd

F32 = mybir.dt.float32
BF16 = mybir.dt.bfloat16
AF = mybir.ActivationFunctionType
ALU = mybir.AluOpType
AX = mybir.AxisListType
BBF16 = ml_dtypes.bfloat16

NCORES = 8
TPC = 2048
D = 1024
H = 16
DKH = 64
E = 8
DF = 4096
FC = DF // 128
DC = D // 128
TT = 1024


ENGS = ("pe", "act", "dve", "pool", "sp")
DMA_QUEUES = {"sp": "sp", "act": "act", "pool": "pool"}
N_DMA_SEMS = {"sp": 6, "act": 4, "pool": 10}


def _akey(ap):
    try:
        return ap.name
    except Exception:
        return str(ap)


class Prog:
    def __init__(self, tag="k"):
        self.nc = bass.Bass("TRN2", target_bir_lowering=False, debug=False)
        self.ctx = ExitStack()
        self.tag = tag
        self.ops = []
        self.sems = {}
        self.count = {}
        for e in ENGS:
            self._mksem("e_" + e)
        self.dmapool = {}
        self.dmaidx = {q: 0 for q in DMA_QUEUES}
        for q, n in N_DMA_SEMS.items():
            self.dmapool[q] = [self._mksem(f"d_{q}{i}") for i in range(n)]
        self.writers = {}
        self.readers = {}
        self.waited = {e: {} for e in ENGS}
        self.sbufs = {}
        self.psums = {}
        self.drams = {}
        self._finalized = False
        for _nm, _shape, _dt in (("identbf", [128, 128], BF16),
                                 ("identf", [128, 128], F32),
                                 ("ones1", [1, 128], BF16)):
            self.sbufs[_nm] = self.ctx.enter_context(self.nc.sbuf_tensor(_nm, _shape, _dt))
        self.PERSIST = set(self.sbufs)
        self.arena_base = (self.nc.sbuf_base + 31) // 32 * 32
        self.arena_top = self.nc.sbuf_top
        self.arena_cur = 0
        self.arena_peak = 0

    def _mksem(self, name):
        h = self.ctx.enter_context(self.nc.semaphore(name))
        self.sems[name] = h
        self.count[name] = 0
        return name

    def sb(self, name, shape, dt):
        if name not in self.sbufs:
            per = 1
            for s in shape[1:]:
                per *= s
            per = (per * mybir.dt.size(dt) + 31) // 32 * 32
            off = self.arena_base + self.arena_cur
            assert off + per <= self.arena_top, (
                f"SBUF arena overflow: {name} at {off}+{per} > {self.arena_top}")
            self.sbufs[name] = self.nc.alloc_sbuf_tensor_at(name, list(shape), dt, offset=off)
            self.arena_cur += per
            self.arena_peak = max(self.arena_peak, self.arena_cur)
        return self.sbufs[name]

    def epoch(self):
        self.barrier()
        self.arena_cur = 0
        for k in list(self.sbufs):
            if k not in self.PERSIST:
                del self.sbufs[k]

    def ps(self, name, shape, dt=F32):
        if name not in self.psums:
            self.psums[name] = self.ctx.enter_context(self.nc.psum_tensor(name, list(shape), dt))
        return self.psums[name]

    def dram_in(self, name, shape, dt=F32):
        if name not in self.drams:
            self.drams[name] = self.nc.dram_tensor(name, list(shape), dt, kind="ExternalInput").ap()
        return self.drams[name]

    def dram_out(self, name, shape, dt=F32):
        if name not in self.drams:
            self.drams[name] = self.nc.dram_tensor(name, list(shape), dt, kind="ExternalOutput").ap()
        return self.drams[name]

    def dram_tmp(self, name, shape, dt=F32):
        if name not in self.drams:
            self.drams[name] = self.nc.dram_tensor(name, list(shape), dt, kind="Internal").ap()
        return self.drams[name]

    def _keys(self, ap):
        nm = _akey(ap)
        try:
            t = ap.tensor
            shape = list(t.shape)
            if len(shape) != 2:
                return [nm]
            rowbytes = shape[1] * mybir.dt.size(t.dtype)
            if rowbytes < 4096:
                return [nm]
        except Exception:
            return [nm]
        nb = (rowbytes + 2047) // 2048
        try:
            es = mybir.dt.size(ap.dtype)
            col0 = (ap.offset * es) % rowbytes
            span = es
            for stride, n in ap.ap:
                sb = stride * es
                if sb == rowbytes:
                    continue
                span += abs(sb) * (n - 1)
            b0 = col0 // 2048
            b1 = (col0 + span - 1) // 2048
            if b1 >= nb or (b1 - b0) > 31:
                b0, b1 = 0, nb - 1
        except Exception:
            b0, b1 = 0, nb - 1
        return [f"{nm}.{b}" for b in range(b0, b1 + 1)]

    def _record(self, eng, emit, reads, writes, opname="", dma_q=None):
        waits = {}

        def need(tok):
            if tok is None:
                return
            s, v = tok
            if eng == "pe" and s == "e_pe":
                return
            if self.waited[eng].get(s, 0) >= v:
                return
            waits[s] = max(waits.get(s, 0), v)

        rkeys = [k for a in reads if a is not None for k in self._keys(a)]
        wkeys = [k for a in writes if a is not None for k in self._keys(a)]
        psum_rkeys = [k for a in reads
                      if a is not None and _akey(a) in self.psums
                      for k in self._keys(a)]
        for k in rkeys:
            need(self.writers.get(k))
        own = "e_" + eng
        for k in psum_rkeys:
            for s, v in self.readers.get(k, {}).items():
                if s != own:
                    need((s, v))
        for k in wkeys:
            need(self.writers.get(k))
            for s, v in self.readers.get(k, {}).items():
                need((s, v))
        if dma_q is not None:
            pool = self.dmapool[dma_q]
            sname = pool[self.dmaidx[dma_q] % len(pool)]
            self.dmaidx[dma_q] += 1
            need((sname, self.count[sname]))
            self.count[sname] += 16
            tok = (sname, self.count[sname])
            inc = (sname, 16)
        else:
            sname = "e_" + eng
            self.count[sname] += 1
            tok = (sname, self.count[sname])
            inc = (sname, 1)
        for k in rkeys:
            self.readers.setdefault(k, {})[tok[0]] = tok[1]
        for k in wkeys:
            self.writers[k] = tok
            self.readers[k] = {tok[0]: tok[1]}
        for s, v in waits.items():
            self.waited[eng][s] = max(self.waited[eng].get(s, 0), v)
        self.ops.append(dict(eng=eng, waits=sorted(waits.items()), emit=emit, inc=inc, name=opname))
        return tok

    def barrier(self):
        cur = [(s, v) for s, v in self.count.items() if v > 0]
        for e in ENGS:
            ws = [(s, v) for s, v in cur if self.waited[e].get(s, 0) < v and s != "e_" + e]
            for s, v in ws:
                self.waited[e][s] = v
            if ws:
                self.ops.append(dict(eng=e, waits=sorted(ws), emit=None, inc=None, name="barrier"))

    def dma(self, dst, src, q="pool", name="dma"):
        eng = DMA_QUEUES[q]
        return self._record(eng, lambda E_: E_.dma_start(out=dst, in_=src),
                            reads=[src], writes=[dst], opname=name, dma_q=q)

    def mm(self, out, pairs, name="mm"):
        reads = [a for pr in pairs for a in pr]
        n = len(pairs)

        def emit(E_):
            last = None
            for i, (l, r) in enumerate(pairs):
                last = E_.matmul(out, l, r, start=(i == 0), stop=(i == n - 1))
            return last
        return self._record("pe", emit, reads=reads, writes=[out], opname=name)

    def mmx(self, groups, reads, writes, name="mmx"):
        def emit(E_):
            last = None
            for (o, l, r, st, sp) in groups:
                last = E_.matmul(o, l, r, start=st, stop=sp)
            return last
        return self._record("pe", emit, reads=reads, writes=writes, opname=name)

    def mm1(self, out, lhsT, rhs, start, stop, name="mm1"):
        return self._record("pe", lambda E_: E_.matmul(out, lhsT, rhs, start=start, stop=stop),
                            reads=[lhsT, rhs], writes=[out], opname=name)

    def trans(self, out, in_, ident, name="tr"):
        return self._record("pe", lambda E_: E_.transpose(out, in_, ident),
                            reads=[in_, ident], writes=[out], opname=name)

    def act(self, out, in_, func, bias=0.0, scale=1.0, accum_out=None, name="act"):
        reads = [in_] + [x for x in (bias, scale) if isinstance(x, bass.AP)]
        writes = [out] + ([accum_out] if accum_out is not None else [])

        def emit(E_):
            return E_.activation(out, in_, func, bias=bias, scale=scale, accum_out=accum_out)
        return self._record("act", emit, reads=reads, writes=writes, opname=name)

    def tt(self, out, a, b, op, eng="dve", name="tt"):
        return self._record(eng, lambda E_: E_.tensor_tensor(out, a, b, op=op),
                            reads=[a, b], writes=[out], opname=name)

    def ts(self, out, in_, s1, s2=None, op0=ALU.mult, op1=None, eng="dve", name="ts"):
        reads = [in_] + [s for s in (s1, s2) if isinstance(s, bass.AP)]

        def emit(E_):
            if op1 is None:
                return E_.tensor_single_scalar(out, in_, s1, op=op0)
            return E_.tensor_scalar(out, in_, s1, s2, op0=op0, op1=op1)
        return self._record(eng, emit, reads=reads, writes=[out], opname=name)

    def stt(self, out, in0, scalar, in1, op0, op1, eng="dve", name="stt"):
        reads = [in0, in1] + ([scalar] if isinstance(scalar, bass.AP) else [])
        return self._record(eng, lambda E_: E_.scalar_tensor_tensor(out, in0, scalar, in1, op0=op0, op1=op1),
                            reads=reads, writes=[out], opname=name)

    def red(self, out, in_, op=ALU.add, name="red"):
        if op == ALU.add:
            return self._record("dve", lambda E_: E_.reduce_sum(out, in_, axis=AX.X),
                                reads=[in_], writes=[out], opname=name)
        if op == ALU.max:
            return self._record("dve", lambda E_: E_.reduce_max(out, in_, axis=AX.X),
                                reads=[in_], writes=[out], opname=name)
        return self._record("dve", lambda E_: E_.tensor_reduce(out, in_, axis=AX.X, op=op),
                            reads=[in_], writes=[out], opname=name)

    def copy(self, out, in_, eng="dve", name="cp"):
        if eng == "act":
            return self._record("act", lambda E_: E_.copy(out=out, in_=in_),
                                reads=[in_], writes=[out], opname=name)
        return self._record(eng, lambda E_: E_.tensor_copy(out, in_),
                            reads=[in_], writes=[out], opname=name)

    def recip(self, out, in_, name="recip"):
        return self._record("dve", lambda E_: E_.reciprocal(out, in_),
                            reads=[in_], writes=[out], opname=name)

    def memset(self, ap, v, eng="dve", name="memset"):
        return self._record(eng, lambda E_: E_.memset(ap, v), reads=[], writes=[ap], opname=name)

    def dbg(self, name, ap, q="sp"):
        import os as _os
        if not _os.environ.get("KDBG"):
            return
        shp = list(ap.shape)
        d = self.dram_out("dbg_" + name, shp, ap.dtype)
        self.dma(d[tuple(slice(0, s) for s in shp)], ap, q=q, name="dbg_" + name)

    def collective(self, kind, op, ins, outs, replica_groups, name="cc"):
        def emit(E_):
            return E_.collective_compute(kind, op, ins=ins, outs=outs, replica_groups=replica_groups)
        return self._record("pool", emit, reads=ins, writes=outs, opname=name, dma_q="pool")

    def truncate(self, n):
        self.ops = self.ops[:n]
        cnt = {s: 0 for s in self.count}
        for op in self.ops:
            if op["inc"] is not None:
                s, amt = op["inc"]
                cnt[s] += amt
        self.count = cnt

    def finalize(self):
        assert not self._finalized
        self._finalized = True
        import os as _os
        if _os.environ.get("TRUNC_OPS"):
            self.truncate(int(_os.environ["TRUNC_OPS"]))
        final = [(s, v) for s, v in self.count.items() if v > 0]
        nc = self.nc
        byeng = {e: [] for e in ENGS}
        for op in self.ops:
            byeng[op["eng"]].append(op)
        block = self.ctx.enter_context(nc.Block())

        def run_section(eng_name, handle):
            for op in byeng[eng_name]:
                for s, v in op["waits"]:
                    handle.wait_ge(self.sems[s], v)
                if op["emit"] is not None:
                    inst = op["emit"](handle)
                    if op["inc"] is not None:
                        s, amt = op["inc"]
                        inst.then_inc(self.sems[s], amt)

        @block.sync
        def _(sp):
            run_section("sp", sp)

        @block.tensor
        def _(pe):
            run_section("pe", pe)

        @block.vector
        def _(dve):
            run_section("dve", dve)

        @block.scalar
        def _(act):
            run_section("act", act)

        @block.gpsimd
        def _(pool):
            run_section("pool", pool)
            for s, v in final:
                pool.wait_ge(self.sems[s], v)

        self.ctx.close()
        return nc



def pack_w1_from_df_d(w1_df_d, nfc=FC):
    w = w1_df_d.reshape(FC, 128, DC, 128)
    return np.ascontiguousarray(w.transpose(0, 3, 2, 1)[:nfc])


def pack_w1_from_d_df(w1_d_df, nfc=FC):
    w = w1_d_df.reshape(DC, 128, FC, 128)
    return np.ascontiguousarray(w.transpose(2, 1, 0, 3)[:nfc])


def pack_w2_from_d_df(w2_d_df, nfc=FC):
    w = w2_d_df.reshape(D, FC, 128)
    return np.ascontiguousarray(w.transpose(1, 2, 0)[:nfc])


def pack_w2_from_df_d(w2_df_d, nfc=FC):
    return np.ascontiguousarray(w2_df_d.reshape(FC, 128, D)[:nfc])


def pack_dvec(v):
    return np.ascontiguousarray(v.reshape(DC, 128).T)


def pack_proj(w_d_o):
    dlen, olen = w_d_o.shape
    w = w_d_o.reshape(dlen // 128, 128, olen)
    return np.ascontiguousarray(w.transpose(1, 0, 2))



def ln_stats_apply(p, x_ap, out_ap, scr, tag):
    st = scr["st"]
    mu, var, rs, m2 = st[:, 0:1], st[:, 1:2], st[:, 2:3], st[:, 3:4]
    p.red(mu, x_ap, ALU.add, name=f"{tag}_sum")
    p.ts(mu, mu, 1.0 / D, name=f"{tag}_mu")
    p.act(scr["junk"][:, :], x_ap, AF.Square, accum_out=var, name=f"{tag}_sq")
    p.ts(var, var, 1.0 / D, 1e-5, op0=ALU.mult, op1=ALU.add, name=f"{tag}_var")
    p.tt(m2, mu, mu, ALU.mult, name=f"{tag}_m2")
    p.tt(var, var, m2, ALU.subtract, name=f"{tag}_vc")
    p.recip(rs, var, name=f"{tag}_rcp")
    p.act(rs, rs, AF.Sqrt, name=f"{tag}_rstd")
    p.ts(out_ap, x_ap, mu, rs, op0=ALU.subtract, op1=ALU.mult, name=f"{tag}_app")


def build_ffn_layer(p, layer, x_dram_in, x_dram_out, n_tt=2, n_exp=E, nfc=FC):
    L = f"L{layer}f"
    has_shared = (layer == 1)
    eps_rt = 1e-8 if layer == 0 else 1e-9
    n_units = n_exp + (1 if has_shared else 0)

    w1b = p.dram_in(f"{L}_w1b", [n_units, nfc, 128, DC * 128], F32)
    w2b = p.dram_in(f"{L}_w2b", [n_units, nfc, 128, D], F32)
    b1b = p.dram_in(f"{L}_b1b", [128, n_units * nfc], F32)
    b2b = p.dram_in(f"{L}_b2b", [E, D], F32)
    b2sh = p.dram_in(f"{L}_b2sh", [1, D], F32)
    wgr = p.dram_in(f"{L}_wgr", [128, DC * E], F32)
    bgr = p.dram_in(f"{L}_bgr", [1, E], F32)
    if layer == 0:
        gb = p.dram_in(f"{L}_gb", [128, 2 * DC], F32)
    identc = p.dram_in("identc", [128, 128], F32)

    x_tm = p.sb(f"{L}_x", [128, 8 * D], F32)
    xnfm = p.sb(f"{L}_xnfm", [128, DC * TT], BF16)
    hbuf = p.sb(f"{L}_h", [128, nfc * TT], BF16)
    w1sl = [p.sb(f"{L}_w1s{i}", [128, DC * 128], BF16) for i in range(3)]
    SG = 8
    n_grp = (nfc + SG - 1) // SG
    w2sl = [p.sb(f"{L}_w2s{i}", [128, SG * D], BF16) for i in range(2)]
    b1sb = p.sb(f"{L}_b1sb", [128, n_units * nfc], F32)
    b2sb = p.sb(f"{L}_b2sb", [E, D], BF16)
    b2shs = p.sb(f"{L}_b2shs", [1, D], BF16)
    wgr_sb = p.sb(f"{L}_wgrs", [128, DC * E], BF16)
    bgr_sb = p.sb(f"{L}_bgrs", [1, E], BF16)
    w_tm = p.sb(f"{L}_wtm", [128, 8 * E], F32)
    wT_sb = p.sb(f"{L}_wT", [E, 8 * 128], BF16)
    ident_bf = p.sb("identbf", [128, 128], BF16)
    ident_f = p.sb("identf", [128, 128], F32)
    scr = dict(
        junk=p.sb(f"{L}_junk", [128, D], BF16),
        st=p.sb(f"{L}_st", [128, 16], F32),
        prob=p.sb(f"{L}_prob", [128, E], F32),
        prob2=p.sb(f"{L}_prob2", [128, E], F32),
        mask=p.sb(f"{L}_mask", [128, E], F32),
        mask2=p.sb(f"{L}_mask2", [128, E], F32),
        ones1=p.sb("ones1", [1, 128], BF16),
    )
    xn_scr = p.sb(f"{L}_xnscr", [128, D], BF16)
    st8 = p.sb(f"{L}_st8", [128, 80], F32)
    r64 = p.sb(f"{L}_r64", [128, 256], F32)
    if layer == 0:
        gb_sb = p.sb(f"{L}_gbsb", [128, 2 * DC], F32)

    pAB = p.ps("pAB", [128, 2048], F32)
    pCD = p.ps("pCD", [128, 2048], F32)
    pQ = [pAB[:, 0:1024], pAB[:, 1024:2048], pCD[:, 0:1024], pCD[:, 1024:2048]]
    pA, pB, pC, pD = pQ

    p.dma(ident_bf[:, :], identc[:, :], q="pool", name="ld_identbf")
    p.dma(ident_f[:, :], identc[:, :], q="sp", name="ld_identf")
    p.memset(scr["ones1"][0:1, :], 1.0, eng="dve")
    p.dma(wgr_sb[:, :], wgr[:, :], q="pool", name="ld_wgr")
    p.dma(bgr_sb[0:1, :], bgr[:, :], q="pool", name="ld_bgr")
    p.dma(b2sb[0:E, :], b2b[:, :], q="pool", name="ld_b2")
    p.dma(b2shs[0:1, :], b2sh[:, :], q="pool", name="ld_b2sh")
    p.dma(b1sb[:, :], b1b[:, :], q="pool", name="ld_b1")
    if layer == 0:
        p.dma(gb_sb[:, :], gb[:, :], q="sp", name="ld_gb")

    for tt in range(n_tt):
        Lt = f"{L}t{tt}"
        p.dma(x_tm[:, :].rearrange("p (tc d) -> p tc d", tc=8),
              x_dram_in[tt * TT:(tt + 1) * TT, :].rearrange("(tc p) d -> p tc d", p=128),
              q="sp", name=f"{Lt}_ldx")
        hbF = hbuf[:, :].bitcast(F32)
        x3v = x_tm[:, :].rearrange("p (tc d) -> p tc d", tc=8)
        mu8, var8, rs8, m28 = st8[:, 0:8], st8[:, 8:16], st8[:, 16:24], st8[:, 24:32]
        xnv = hbuf[:, 16384:24576].rearrange("p (tc d) -> p tc d", tc=8)
        if layer == 0:
            p.red(mu8, x3v, ALU.add, name=f"{Lt}lsum")
            p.ts(mu8, mu8, 1.0 / D, name=f"{Lt}lmu")
            p.act(hbF[:, 0:8192], x_tm[:, :], AF.Square, name=f"{Lt}lsq")
            p.red(var8, hbF[:, 0:8192].rearrange("p (tc d) -> p tc d", tc=8), ALU.add,
                  name=f"{Lt}lvs")
            p.ts(var8, var8, 1.0 / D, 1e-5, op0=ALU.mult, op1=ALU.add, name=f"{Lt}lvar")
            p.tt(m28, mu8, mu8, ALU.mult, name=f"{Lt}lm2")
            p.tt(var8, var8, m28, ALU.subtract, name=f"{Lt}lvc")
            p.recip(rs8, var8, name=f"{Lt}lrc")
            p.act(rs8, rs8, AF.Sqrt, name=f"{Lt}lrs")
            p.tt(xnv, x3v, mu8.unsqueeze(2).broadcast_to([128, 8, D]), ALU.subtract,
                 name=f"{Lt}lap")
            p.tt(xnv, xnv, rs8.unsqueeze(2).broadcast_to([128, 8, D]), ALU.mult,
                 name=f"{Lt}lam")
        else:
            p.copy(hbuf[:, 16384:24576], x_tm[:, :], name=f"{Lt}cast")
        pTb = pD[:, :].bitcast(BF16)
        for tc in range(8):
            for dc in range(DC):
                p.trans(pTb[:, dc * 128:(dc + 1) * 128],
                        hbuf[:, 16384 + tc * D + dc * 128: 16384 + tc * D + (dc + 1) * 128],
                        ident_bf[:, :], name=f"{Lt}tr{tc}_{dc}")
            dstv = xnfm[:, 0:8192].rearrange("p (dc t) -> p dc t", dc=8)[:, :, tc * 128:(tc + 1) * 128]
            srcv = pTb[:, 0:1024].rearrange("p (dc t) -> p dc t", dc=8)
            if layer == 0:
                p.tt(dstv, srcv,
                     gb_sb[:, 0:DC].unsqueeze(2).broadcast_to([128, DC, 128]),
                     ALU.mult, name=f"{Lt}gbm{tc}")
                p.tt(dstv, dstv,
                     gb_sb[:, DC:2 * DC].unsqueeze(2).broadcast_to([128, DC, 128]),
                     ALU.add, name=f"{Lt}gba{tc}")
            else:
                p.copy(dstv, srcv, name=f"{Lt}sc{tc}")
        for tc in range(8):
            ppr = pA[:, tc * E:(tc + 1) * E]
            pairs = [(xnfm[:, dc * TT + tc * 128: dc * TT + tc * 128 + 128],
                      wgr_sb[:, dc * E:(dc + 1) * E]) for dc in range(DC)]
            pairs.append((scr["ones1"][0:1, :], bgr_sb[0:1, :]))
            p.mm(ppr, pairs, name=f"{Lt}rlog{tc}")
        rm8, rsum8 = st8[:, 32:40], st8[:, 40:48]
        m18, m28r, den8 = st8[:, 48:56], st8[:, 56:64], st8[:, 64:72]
        pv, p2v = r64[:, 0:64], r64[:, 64:128]
        mkv, mk2v = r64[:, 128:192], r64[:, 192:256]
        r3 = lambda ap: ap.rearrange("p (tc e) -> p tc e", tc=8)
        bc8 = lambda ap: ap.unsqueeze(2).broadcast_to([128, 8, E])
        p.red(rm8, r3(pA[:, 0:64]), ALU.max, name=f"{Lt}rmax")
        p.tt(r3(pv), r3(pA[:, 0:64]), bc8(rm8), ALU.subtract, name=f"{Lt}rsub")
        p.act(pv, pv, AF.Exp, name=f"{Lt}rexp")
        p.red(rsum8, r3(pv), ALU.add, name=f"{Lt}rsum")
        p.recip(rsum8, rsum8, name=f"{Lt}rrcp")
        p.tt(r3(pv), r3(pv), bc8(rsum8), ALU.mult, name=f"{Lt}rnrm")
        p.red(m18, r3(pv), ALU.max, name=f"{Lt}m1")
        p.tt(r3(mkv), r3(pv), bc8(m18), ALU.is_ge, name=f"{Lt}mk1")
        p.tt(p2v, pv, mkv, ALU.mult, name=f"{Lt}pm")
        p.tt(p2v, pv, p2v, ALU.subtract, name=f"{Lt}p2")
        p.red(m28r, r3(p2v), ALU.max, name=f"{Lt}m2")
        p.tt(r3(mk2v), r3(p2v), bc8(m28r), ALU.is_ge, name=f"{Lt}mk2")
        p.tt(den8, m18, m28r, ALU.add, name=f"{Lt}dsum")
        p.ts(den8, den8, eps_rt, op0=ALU.add, name=f"{Lt}deps")
        p.recip(den8, den8, name=f"{Lt}drcp")
        p.tt(mkv, mkv, mk2v, ALU.add, name=f"{Lt}msum")
        p.tt(w_tm[:, 0:64], pv, mkv, ALU.mult, name=f"{Lt}pw")
        p.tt(r3(w_tm[:, 0:64]), r3(w_tm[:, 0:64]), bc8(den8), ALU.mult, name=f"{Lt}w")
        for tc in range(8):
            p.trans(pB[0:E, tc * 128:(tc + 1) * 128], w_tm[:, tc * E:(tc + 1) * E],
                    ident_f[:, :], name=f"{Lt}wT{tc}")
        p.copy(wT_sb[0:E, 0:1024], pB[0:E, 0:1024], eng="act", name=f"{Lt}wTe")
        if layer == 0:
            p.ts(x_tm[:, :], x_tm[:, :], 2.0, name=f"{Lt}x2")
        for tc in range(8):
            for dh in range(2):
                ps_ = pQ[tc % 2][:, dh * 512:(dh + 1) * 512]
                pairs = [(wT_sb[0:E, tc * 128:(tc + 1) * 128], b2sb[0:E, dh * 512:(dh + 1) * 512])]
                if has_shared:
                    pairs.append((scr["ones1"][0:1, :], b2shs[0:1, dh * 512:(dh + 1) * 512]))
                p.mm(ps_, pairs, name=f"{Lt}b2mm{tc}_{dh}")
                xa = x_tm[:, tc * D + dh * 512: tc * D + (dh + 1) * 512]
                p.tt(xa, xa, ps_, ALU.add, name=f"{Lt}b2add{tc}_{dh}")
        for u in range(n_units):
            Ue = f"{Lt}e{u}"
            shared = has_shared and (u == n_exp)
            for fc in range(nfc):
                sl = w1sl[fc % 3]
                p.dma(sl[:, :], w1b[u, fc, :, :], q="pool", name=f"{Ue}w1d{fc}")
                ph = pQ[fc % 2]
                for t5 in range(2):
                    pairs = [(sl[:, dc * 128:(dc + 1) * 128],
                              xnfm[:, dc * TT + t5 * 512: dc * TT + t5 * 512 + 512]) for dc in range(DC)]
                    p.mm(ph[:, t5 * 512:(t5 + 1) * 512], pairs, name=f"{Ue}p1_{fc}_{t5}")
                bidx = u * nfc + fc
                p.act(hbuf[:, fc * TT:(fc + 1) * TT], ph[:, :], AF.Gelu,
                      bias=b1sb[:, bidx:bidx + 1], name=f"{Ue}gelu{fc}")
            for g in range(n_grp):
                gfc = list(range(g * SG, min((g + 1) * SG, nfc)))
                sl2 = w2sl[g % 2]
                p.dma(sl2[:, 0:len(gfc) * D].rearrange("p (f d) -> p f d", f=len(gfc)),
                      w2b[u, gfc[0]:gfc[-1] + 1, :, :].rearrange("f p d -> p f d"),
                      q="pool", name=f"{Ue}w2d{g}")
                for tq in range(2):
                    for tci in range(4):
                        tc = tq * 4 + tci
                        groups = []
                        for i, fc in enumerate(gfc):
                            lhsT = hbuf[:, fc * TT + tc * 128: fc * TT + tc * 128 + 128]
                            for dh in range(2):
                                groups.append((pQ[tci][:, dh * 512:(dh + 1) * 512], lhsT,
                                               sl2[:, i * D + dh * 512: i * D + (dh + 1) * 512],
                                               i == 0, i == len(gfc) - 1))
                        p.mmx(groups, reads=[hbuf[:, :], sl2[:, :]], writes=[pQ[tci][:, :]],
                              name=f"{Ue}p2_{g}_{tc}")
                    for tci in range(4):
                        tc = tq * 4 + tci
                        xa = x_tm[:, tc * D:(tc + 1) * D]
                        if shared:
                            p.tt(xa, xa, pQ[tci][:, :], ALU.add, name=f"{Ue}fl{g}_{tc}")
                        else:
                            w_ap = w_tm[:, tc * E + u: tc * E + u + 1]
                            p.stt(xa, pQ[tci][:, :], w_ap, xa, op0=ALU.mult, op1=ALU.add,
                                  name=f"{Ue}fl{g}_{tc}")
        p.red(mu8, x3v, ALU.add, name=f"{Lt}psum")
        p.ts(mu8, mu8, 1.0 / D, name=f"{Lt}pmu")
        p.act(hbF[:, 0:8192], x_tm[:, :], AF.Square, name=f"{Lt}psq")
        p.red(var8, hbF[:, 0:8192].rearrange("p (tc d) -> p tc d", tc=8), ALU.add,
              name=f"{Lt}pvs")
        p.ts(var8, var8, 1.0 / D, 1e-5, op0=ALU.mult, op1=ALU.add, name=f"{Lt}pvar")
        p.tt(m28, mu8, mu8, ALU.mult, name=f"{Lt}pm2")
        p.tt(var8, var8, m28, ALU.subtract, name=f"{Lt}pvc")
        p.recip(rs8, var8, name=f"{Lt}prc")
        p.act(rs8, rs8, AF.Sqrt, name=f"{Lt}prs")
        p.tt(x3v, x3v, mu8.unsqueeze(2).broadcast_to([128, 8, D]), ALU.subtract,
             name=f"{Lt}pap")
        p.tt(x3v, x3v, rs8.unsqueeze(2).broadcast_to([128, 8, D]), ALU.mult,
             name=f"{Lt}pam")
        p.dma(x_dram_out[tt * TT:(tt + 1) * TT, :].rearrange("(tc p) d -> p tc d", p=128),
              x_tm[:, :].rearrange("p (tc d) -> p tc d", tc=8), q="sp", name=f"{Lt}_stx")



def host_pack_ffn(layer, inputs, n_exp=E, nfc=FC):
    d = {}
    L = f"L{layer}f"
    if layer == 0:
        W1 = np.asarray(inputs["L0_ffn_W1"], np.float32)
        W2 = np.asarray(inputs["L0_ffn_W2"], np.float32)
        d[f"{L}_w1b"] = np.stack([pack_w1_from_df_d(W1[e], nfc) for e in range(n_exp)])
        d[f"{L}_w2b"] = np.stack([pack_w2_from_d_df(W2[e], nfc) for e in range(n_exp)])
        b1r = np.asarray(inputs["L0_ffn_b1"], np.float32)[:n_exp].reshape(n_exp, FC, 128)[:, :nfc]
        d[f"{L}_b1b"] = np.ascontiguousarray(b1r.transpose(2, 0, 1).reshape(128, n_exp * nfc))
        d[f"{L}_b2b"] = np.asarray(inputs["L0_ffn_b2"], np.float32)
        d[f"{L}_b2sh"] = np.zeros((1, D), np.float32)
        d[f"{L}_wgr"] = np.ascontiguousarray(
            np.asarray(inputs["L0_ffn_Wg"], np.float32).T.reshape(DC, 128, E)
            .transpose(1, 0, 2).reshape(128, DC * E))
        d[f"{L}_bgr"] = np.asarray(inputs["L0_ffn_bg"], np.float32).reshape(1, E)
        gbp = np.empty((128, 2 * DC), np.float32)
        gbp[:, 0:DC] = pack_dvec(np.asarray(inputs["L0_ffn_gamma"], np.float32))
        gbp[:, DC:] = pack_dvec(np.asarray(inputs["L0_ffn_beta"], np.float32))
        d[f"{L}_gb"] = gbp
    else:
        W1 = np.asarray(inputs["L1_ffn_exp_w1"], np.float32)
        W2 = np.asarray(inputs["L1_ffn_exp_w2"], np.float32)
        w1l = [pack_w1_from_d_df(W1[e], nfc) for e in range(n_exp)]
        w1l.append(pack_w1_from_d_df(np.asarray(inputs["L1_ffn_shared_w1"], np.float32), nfc))
        d[f"{L}_w1b"] = np.stack(w1l)
        w2l = [pack_w2_from_df_d(W2[e], nfc) for e in range(n_exp)]
        w2l.append(pack_w2_from_df_d(np.asarray(inputs["L1_ffn_shared_w2"], np.float32), nfc))
        d[f"{L}_w2b"] = np.stack(w2l)
        b1l = np.asarray(inputs["L1_ffn_exp_b1"], np.float32)[:n_exp].reshape(n_exp, FC, 128)
        b1sh = np.asarray(inputs["L1_ffn_shared_b1"], np.float32).reshape(1, FC, 128)
        b1r = np.concatenate([b1l, b1sh], 0)[:, :nfc]
        d[f"{L}_b1b"] = np.ascontiguousarray(b1r.transpose(2, 0, 1).reshape(128, (n_exp + 1) * nfc))
        d[f"{L}_b2b"] = np.asarray(inputs["L1_ffn_exp_b2"], np.float32)
        d[f"{L}_b2sh"] = np.asarray(inputs["L1_ffn_shared_w2"], np.float32)[0:0]
        d[f"{L}_b2sh"] = np.asarray(inputs["L1_ffn_shared_b2"], np.float32).reshape(1, D)
        d[f"{L}_wgr"] = np.ascontiguousarray(
            np.asarray(inputs["L1_ffn_router_w"], np.float32).reshape(DC, 128, E)
            .transpose(1, 0, 2).reshape(128, DC * E))
        d[f"{L}_bgr"] = np.asarray(inputs["L1_ffn_router_b"], np.float32).reshape(1, E)
    d["identc"] = np.eye(128, dtype=np.float32)
    return d


def _attn_core_meta(c):
    half = c % 2
    return dict(half=half, pos0=half * TPC, pscale=float(half))


def kernel(**inputs):
    x = np.ascontiguousarray(np.asarray(inputs["L0_attn_x"], np.float32).reshape(16384, D))
    zeros_prev = np.zeros((TPC, D), np.float32)

    p = Prog()
    xin = p.dram_in("xin", [TPC, D], F32)
    xprev = p.dram_in("xprev", [TPC, D], F32)
    xmid = p.dram_tmp("xmid", [2 * TPC, D], F32)
    x2f = p.dram_tmp("x2f", [2 * TPC, D], F32)
    x3d = p.dram_tmp("x3d", [TPC, D], F32)
    xout = p.dram_out("xout", [TPC, D], F32)

    build_attn0(p, xin, xprev, xmid[TPC:2 * TPC, :], xprev_out=xmid[0:TPC, :])
    p.epoch()
    build_ffn_layer(p, 0, xmid[:, :], x2f[:, :], n_tt=4)
    p.epoch()
    build_attn1(p, x2f[TPC:2 * TPC, :], x2f[0:TPC, :], x3d[:, :])
    p.epoch()
    build_ffn_layer(p, 1, x3d[:, :], xout[:, :], n_tt=2)
    nc = p.finalize()

    pk_f0 = host_pack_ffn(0, inputs)
    pk_f1 = host_pack_ffn(1, inputs)
    pk_a1 = host_pack_attn1(inputs)
    in_maps = []
    for c in range(NCORES):
        m = _attn_core_meta(c)
        xin_c = x[c * TPC:(c + 1) * TPC]
        xprev_c = x[(c - 1) * TPC:c * TPC] if m["half"] == 1 else zeros_prev
        halo = x[c * TPC - 3:c * TPC] if m["half"] == 1 else None
        pk_a0 = host_pack_attn0(inputs, core_pos0=m["pos0"], halo_main=halo,
                                pscale=m["pscale"])
        mp = dict(pk_f0)
        mp.update(pk_f1)
        mp.update(pk_a1)
        mp.update(pk_a0)
        mp["L1a_ps"] = np.full((128, 1), m["pscale"], np.float32)
        mp["xin"] = xin_c
        mp["xprev"] = xprev_c
        in_maps.append(mp)

    res = run_bass_kernel_spmd(nc, in_maps, core_ids=list(range(NCORES)))
    out = np.concatenate([res.results[c]["xout"] for c in range(NCORES)], axis=0)
    return out.reshape(4, 4096, D)



def pack_proj_oc(w_d_o):
    dlen, olen = w_d_o.shape
    w = w_d_o.reshape(dlen // 128, 128, olen // 128, 128)
    return np.ascontiguousarray(w.transpose(2, 1, 0, 3).reshape(olen // 128, 128, dlen // 128 * 128))


def pack_proj_dc(w_d_o):
    dlen, olen = w_d_o.shape
    return np.ascontiguousarray(w_d_o.reshape(dlen // 128, 128, olen))


def emit_cast_transpose_fm(p, src_tm_ap, dst_fm, tcs, Ttot, scr_bf, pT_ps, ident_bf, tag,
                           src_is_f32=True):
    pTb = pT_ps[:, :].bitcast(BF16)
    for tc in tcs:
        s = src_tm_ap(tc)
        if src_is_f32:
            p.copy(scr_bf[:, :], s, name=f"{tag}c{tc}")
            s = scr_bf[:, :]
        for dc in range(DC):
            p.trans(pTb[:, dc * 128:(dc + 1) * 128], s[:, dc * 128:(dc + 1) * 128],
                    ident_bf[:, :], name=f"{tag}t{tc}_{dc}")
        p.copy(dst_fm[:, 0:DC * Ttot].rearrange("p (dc t) -> p dc t", dc=DC)
               [:, :, tc * 128:(tc + 1) * 128],
               pTb[:, 0:1024].rearrange("p (dc t) -> p dc t", dc=DC),
               name=f"{tag}e{tc}")


def emit_softplus(p, out_ap, in_ap, scr1, scr2, tag):
    p.act(scr1[:, :], in_ap, AF.Abs, name=f"{tag}a")
    p.ts(out_ap, in_ap, 0.0, op0=ALU.max, name=f"{tag}r")
    p.act(scr1[:, :], scr1[:, :], AF.Exp, scale=-1.0, name=f"{tag}e")
    p.act(scr1[:, :], scr1[:, :], AF.Ln, bias=1.0, name=f"{tag}l")
    p.tt(out_ap, out_ap, scr1[:, :], ALU.add, name=f"{tag}s")


def build_attn1(p, x_in, xprev_in, x_out, n_tc=16):
    L = "L1a"
    Ttot = n_tc * 128

    wq = p.dram_in(f"{L}_wq", [8, 128, DC * 128], F32)
    wk = p.dram_in(f"{L}_wk", [8, 128, DC * 128], F32)
    wg = p.dram_in(f"{L}_wg", [8, 128, DC * 128], F32)
    wv = p.dram_in(f"{L}_wv", [DC, 128, D], F32)
    wo = p.dram_in(f"{L}_wo", [8, 128, D], F32)
    bgc = p.dram_in(f"{L}_bg", [128, 8], F32)
    pscale_d = p.dram_in(f"{L}_ps", [128, 1], F32)
    maskst_d = p.dram_in("maskst", [128, 128], F32)
    identc = p.dram_in("identc", [128, 128], F32)

    x_scr = p.sb(f"{L}_xscr", [128, D], F32)
    x_fm = p.sb(f"{L}_xfm", [128, DC * Ttot], BF16)
    qf_fm = p.sb(f"{L}_qf", [128, 8 * Ttot], BF16)
    kf_fm = p.sb(f"{L}_kf", [128, 8 * Ttot], BF16)
    kf_tm = p.sb(f"{L}_kftm", [128, n_tc * D], BF16)
    v65 = p.sb(f"{L}_v65", [128, n_tc * H * 65], BF16)
    S_acc = p.sb(f"{L}_S", [128, 8 * 65], F32)
    S_bf = p.sb(f"{L}_Sbf", [128, 8 * 65], BF16)
    sc_sb = p.sb(f"{L}_sc", [128, H * 128], BF16)
    o_scr = p.sb(f"{L}_oscr", [128, D], BF16)
    ofm_scr = p.sb(f"{L}_ofm", [128, D], BF16)
    den_sb = p.sb(f"{L}_den", [128, H], F32)
    wsl = [p.sb(f"{L}_wsl{i}", [128, DC * 128], BF16) for i in range(3)]
    wsl2 = [p.sb(f"{L}_wsl2{i}", [128, DC * 128], BF16) for i in range(2)]
    wv_sb = p.sb(f"{L}_wvsb", [128, DC * D], BF16)
    wo_sb = x_fm
    bg_sb = p.sb(f"{L}_bgsb", [128, 8], F32)
    ps_sb = p.sb(f"{L}_pssb", [128, 1], F32)
    mask_sb = p.sb("mask_sb", [128, 128], BF16)
    ident_bf = p.sb("identbf", [128, 128], BF16)
    s1 = p.sb(f"{L}_s1", [128, 512], F32)
    s2 = None
    cast_scr = p.sb(f"{L}_cscr", [128, D], BF16)
    lnscr = dict(junk=cast_scr, st=p.sb(f"{L}_st", [128, 16], F32))

    pAB = p.ps("pAB", [128, 2048], F32)
    pCD = p.ps("pCD", [128, 2048], F32)
    pQ = [pAB[:, 0:1024], pAB[:, 1024:2048], pCD[:, 0:1024], pCD[:, 1024:2048]]
    pA, pB, pC, pD = pQ

    p.dma(ident_bf[:, :], identc[:, :], q="pool", name="ld_identbf")
    p.dma(mask_sb[:, :], maskst_d[:, :], q="pool", name="ld_mask")
    p.dma(wv_sb[:, :].rearrange("p (a b) -> p a b", a=DC), wv[:, :, :].rearrange("a p b -> p a b"),
          q="pool", name="ld_wv")
    p.dma(bg_sb[:, :], bgc[:, :], q="pool", name="ld_bg")
    p.dma(ps_sb[:, :], pscale_d[:, :], q="pool", name="ld_ps")

    def proj_fm_pass(xfm, wblob, slabs, evac, tag):
        nt5 = Ttot // 512
        for oc in range(8):
            sl = slabs[oc % len(slabs)]
            p.dma(sl[:, :], wblob[oc, :, :], q="pool", name=f"{tag}d{oc}")
            for t5 in range(nt5):
                ph = pQ[(oc * nt5 + t5) % 2]
                half = ((oc * nt5 + t5) // 2) % 2
                ps_ = ph[:, half * 512:(half + 1) * 512]
                pairs = [(sl[:, dc * 128:(dc + 1) * 128],
                          xfm[:, dc * Ttot + t5 * 512: dc * Ttot + t5 * 512 + 512])
                         for dc in range(DC)]
                p.mm(ps_, pairs, name=f"{tag}mm{oc}_{t5}")
                evac(oc, t5, ps_)

    def state_accum(tc, tag):
        for h in range(H):
            px = pQ[h // 8]
            slot = (h % 8) * 128
            p.mm(px[0:64, slot:slot + 65],
                 [(kf_tm[:, tc * D + h * 64: tc * D + h * 64 + 64],
                   v65[:, tc * H * 65 + h * 65: tc * H * 65 + (h + 1) * 65])],
                 name=f"{tag}st{h}")
        for h in range(H):
            px = pQ[h // 8]
            slot = (h % 8) * 128
            r0 = (h % 2) * 64
            p.tt(S_acc[r0:r0 + 64, (h // 2) * 65:(h // 2 + 1) * 65],
                 S_acc[r0:r0 + 64, (h // 2) * 65:(h // 2 + 1) * 65],
                 px[0:64, slot:slot + 65], ALU.add, name=f"{tag}sa{h}")

    def l1_half(x_dram, do_out, tag):
        def src(tc):
            p.dma(x_scr[:, :], x_dram[tc * 128:(tc + 1) * 128, :], q="sp", name=f"{tag}lx{tc}")
            return x_scr[:, :]
        emit_cast_transpose_fm(p, src, x_fm, range(n_tc), Ttot, cast_scr, pD, ident_bf, f"{tag}xt")
        def evk(oc, t5, ps_):
            emit_softplus(p, kf_fm[:, oc * Ttot + t5 * 512: oc * Ttot + t5 * 512 + 512],
                          ps_, s1, s2, f"{tag}spk{oc}_{t5}")
        proj_fm_pass(x_fm, wk, wsl, evk, f"{tag}pk")
        pTb = pD[:, :].bitcast(BF16)
        for tc in range(n_tc):
            for oc in range(8):
                p.trans(pTb[:, oc * 128:(oc + 1) * 128],
                        kf_fm[:, oc * Ttot + tc * 128: oc * Ttot + tc * 128 + 128],
                        ident_bf[:, :], name=f"{tag}ktt{tc}_{oc}")
            p.copy(kf_tm[:, tc * D:(tc + 1) * D], pTb[:, 0:1024], name=f"{tag}kte{tc}")
        for tc in range(n_tc):
            for oh in range(2):
                ph = pQ[oh]
                ps_ = ph[:, 0:512]
                pairs = [(x_fm[:, dc * Ttot + tc * 128: dc * Ttot + tc * 128 + 128],
                          wv_sb[:, dc * D + oh * 512: dc * D + oh * 512 + 512])
                         for dc in range(DC)]
                p.mm(ps_, pairs, name=f"{tag}pv{tc}_{oh}")
                dst = v65[:, :].rearrange("p (tc h j) -> p tc h j", tc=n_tc, h=H)[
                    :, tc, oh * 8:(oh + 1) * 8, 0:64]
                p.copy(dst, ps_.rearrange("p (h j) -> p h j", h=8), name=f"{tag}ve{tc}_{oh}")
            p.memset(v65[:, :].rearrange("p (tc h j) -> p tc h j", tc=n_tc, h=H)[
                :, tc, :, 64:65], 1.0, name=f"{tag}vo{tc}")
        if do_out:
            nt5 = Ttot // 512
            for oc in range(8):
                slq = wsl[oc % 3]
                slg = wsl2[oc % 2]
                p.dma(slq[:, :], wq[oc, :, :], q="pool", name=f"{tag}qd{oc}")
                p.dma(slg[:, :], wg[oc, :, :], q="pool", name=f"{tag}gd{oc}")
                for t5 in range(nt5):
                    psq = pA[:, (t5 % 2) * 512:(t5 % 2) * 512 + 512]
                    psg = pB[:, (t5 % 2) * 512:(t5 % 2) * 512 + 512]
                    p.mm(psq, [(slq[:, dc * 128:(dc + 1) * 128],
                                x_fm[:, dc * Ttot + t5 * 512: dc * Ttot + t5 * 512 + 512])
                               for dc in range(DC)], name=f"{tag}qmm{oc}_{t5}")
                    p.mm(psg, [(slg[:, dc * 128:(dc + 1) * 128],
                                x_fm[:, dc * Ttot + t5 * 512: dc * Ttot + t5 * 512 + 512])
                               for dc in range(DC)], name=f"{tag}gmm{oc}_{t5}")
                    emit_softplus(p, o_scr[:, 0:512], psq, s1, s2, f"{tag}spq{oc}_{t5}")
                    p.act(s1[:, :], psg, AF.Sigmoid, bias=bg_sb[:, oc:oc + 1], name=f"{tag}sig{oc}_{t5}")
                    p.stt(qf_fm[:, oc * Ttot + t5 * 512: oc * Ttot + t5 * 512 + 512],
                          s1[:, :], 1e-6, o_scr[:, 0:512], op0=ALU.add, op1=ALU.mult, name=f"{tag}qf{oc}_{t5}")

    p.memset(S_acc[:, :], 0.0, name="S0")
    l1_half(xprev_in, False, f"{L}p")
    for tc in range(n_tc):
        state_accum(tc, f"{L}ps{tc}")
    p.ts(S_acc[:, :], S_acc[:, :], ps_sb[:, 0:1], op0=ALU.mult, name=f"{L}psc")
    l1_half(x_in, True, f"{L}m")
    p.dma(wo_sb[:, 0:8 * D].rearrange("p (a b) -> p a b", a=8),
          wo[:, :, :].rearrange("a p b -> p a b"), q="pool", name="ld_wo")
    p.copy(S_bf[:, :], S_acc[:, :], name=f"{L}sb0")
    for tc in range(n_tc):
        T = f"{L}c{tc}"
        for h in range(H):
            ph = pQ[h % 2]
            slot = (h // 2) * 128
            r0 = (h % 2) * 64
            cs = (h // 2) * Ttot + tc * 128
            p.mm(ph[:, slot:slot + 128],
                 [(kf_fm[r0:r0 + 64, cs:cs + 128], qf_fm[r0:r0 + 64, cs:cs + 128])],
                 name=f"{T}sc{h}")
        for half in range(2):
            ph = pQ[half]
            p.tt(sc_sb[:, half * 1024:(half + 1) * 1024].rearrange("p (h t) -> p h t", h=8),
                 ph[:, 0:1024].rearrange("p (h t) -> p h t", h=8),
                 mask_sb[:, :].unsqueeze(1).broadcast_to([128, 8, 128]),
                 ALU.mult, name=f"{T}mk{half}")
        for h in range(H):
            pn = pQ[2 + h % 2]
            slot = (h // 2) * 128
            scc = ((h % 2) * 8 + h // 2) * 128
            r0 = (h % 2) * 64
            cs = (h // 2) * Ttot + tc * 128
            p.mmx([(pn[:, slot:slot + 65], sc_sb[:, scc:scc + 128],
                    v65[:, tc * H * 65 + h * 65: tc * H * 65 + (h + 1) * 65], True, False),
                   (pn[:, slot:slot + 65], qf_fm[r0:r0 + 64, cs:cs + 128],
                    S_bf[r0:r0 + 64, (h // 2) * 65:(h // 2 + 1) * 65], False, True)],
                  reads=[sc_sb[:, :], v65[:, :], qf_fm[:, :], S_bf[:, :]],
                  writes=[pn[:, slot:slot + 65]], name=f"{T}nm{h}")
        state_accum(tc, T)
        p.copy(S_bf[:, :], S_acc[:, :], name=f"{T}sbf")
        for half in range(2):
            pn = pQ[2 + half]
            p.copy(den_sb[:, half * 8:(half + 1) * 8],
                   pn[:, 0:1024].rearrange("p (h t) -> p h t", h=8)[:, :, 64], name=f"{T}dg{half}")
        p.ts(den_sb[:, :], den_sb[:, :], 1e-6, op0=ALU.add, name=f"{T}de")
        p.recip(den_sb[:, :], den_sb[:, :], name=f"{T}dr")
        for h in range(H):
            pn = pQ[2 + h % 2]
            slot = (h // 2) * 128
            p.ts(o_scr[:, h * 64:(h + 1) * 64], pn[:, slot:slot + 64],
                 den_sb[:, (h % 2) * 8 + h // 2:(h % 2) * 8 + h // 2 + 1],
                 op0=ALU.mult, name=f"{T}oe{h}")
        pTb2 = pA[:, :].bitcast(BF16)
        for oc in range(8):
            p.trans(pTb2[:, oc * 128:(oc + 1) * 128], o_scr[:, oc * 128:(oc + 1) * 128],
                    ident_bf[:, :], name=f"{T}ot{oc}")
        for oc in range(8):
            p.copy(ofm_scr[:, oc * 128:(oc + 1) * 128], pTb2[:, oc * 128:(oc + 1) * 128],
                   name=f"{T}oc{oc}")
        p.dma(x_scr[:, :], x_in[tc * 128:(tc + 1) * 128, :], q="sp", name=f"{T}rx")
        for dh in range(2):
            pso = pB[:, dh * 512:(dh + 1) * 512]
            p.mm(pso, [(ofm_scr[:, oc * 128:(oc + 1) * 128],
                        wo_sb[:, oc * D + dh * 512: oc * D + dh * 512 + 512])
                       for oc in range(8)], name=f"{T}op{dh}")
            p.tt(x_scr[:, dh * 512:(dh + 1) * 512], x_scr[:, dh * 512:(dh + 1) * 512],
                 pso, ALU.add, name=f"{T}ra{dh}")
        ln_stats_apply(p, x_scr[:, :], x_scr[:, :], lnscr, f"{T}ln")
        p.dma(x_out[tc * 128:(tc + 1) * 128, :], x_scr[:, :], q="sp", name=f"{T}sx")


def host_pack_attn1(inputs):
    d = {}
    Wq = np.asarray(inputs["L1_attn_Wq"], np.float32)
    Wk = np.asarray(inputs["L1_attn_Wk"], np.float32)
    Wv = np.asarray(inputs["L1_attn_Wv"], np.float32)
    Wg = np.asarray(inputs["L1_attn_Wg"], np.float32)
    Wo = np.asarray(inputs["L1_attn_Wo"], np.float32)
    d["L1a_wq"] = pack_proj_oc(Wq.T)
    d["L1a_wk"] = pack_proj_oc(Wk.T)
    d["L1a_wg"] = pack_proj_oc(Wg.T)
    d["L1a_wv"] = pack_proj_dc(Wv.T)
    d["L1a_wo"] = pack_proj_dc(Wo.T)
    d["L1a_bg"] = np.ascontiguousarray(
        np.asarray(inputs["L1_attn_bg"], np.float32).reshape(8, 128).T)
    d["maskst"] = np.triu(np.ones((128, 128), np.float32))
    d["identc"] = np.eye(128, dtype=np.float32)
    return d



def build_attn0(p, x_in, xprev_in, x_out, xprev_out=None, n_tc=16):
    L = "L0a"
    Ttot = n_tc * 128
    W3 = Ttot + 4
    HOF = 1
    DOF = HOF + 3

    wq = p.dram_in(f"{L}_wq", [8, 128, DC * 128], F32)
    wk = p.dram_in(f"{L}_wk", [8, 128, DC * 128], F32)
    wv = p.dram_in(f"{L}_wv", [8, 128, DC * 128], F32)
    wg = p.dram_in(f"{L}_wg", [DC, 128, D], F32)
    wo = p.dram_in(f"{L}_wo", [8, 128, D], F32)
    wgkb = p.dram_in(f"{L}_wgk", [128, DC * H], F32)
    wbb = p.dram_in(f"{L}_wb", [128, DC * H], F32)
    convw = p.dram_in(f"{L}_cw", [128, 3 * 8 * 4], F32)
    dtb_d = p.dram_in(f"{L}_dtb", [H, 1], F32)
    alog_d = p.dram_in(f"{L}_alog", [H, 1], F32)
    bb_d = p.dram_in(f"{L}_bb", [1, H], F32)
    gnorm_d = p.dram_in(f"{L}_gn", [128, 1], F32)
    ropec_d = p.dram_in(f"{L}_ropec", [128, n_tc * 16], F32)
    ropes_d = p.dram_in(f"{L}_ropes", [128, n_tc * 16], F32)
    ropecp_d = p.dram_in(f"{L}_ropecp", [128, n_tc * 16], F32)
    ropesp_d = p.dram_in(f"{L}_ropesp", [128, n_tc * 16], F32)
    xhalo_p = p.dram_in(f"{L}_xhp", [128, DC * 3], F32)
    xhalo_m = p.dram_in(f"{L}_xhm", [128, DC * 3], F32)
    pscale_d = p.dram_in(f"{L}_ps", [128, 1], F32)
    identc = p.dram_in("identc", [128, 128], F32)
    maskI_d = p.dram_in(f"{L}_mI", [128, 128], F32)
    maskSn_d = p.dram_in(f"{L}_mSn", [128, 128], F32)
    maskLn_d = p.dram_in(f"{L}_mLn", [128, 128], F32)
    pairsel_d = p.dram_in(f"{L}_psel", [H, 128], F32)
    pairdg_d = p.dram_in(f"{L}_pdg", [H, 8], F32)
    sel16_d = p.dram_in(f"{L}_sel", [H, H * 128], F32)
    identE_d = p.dram_in(f"{L}_idE", [H, H], F32)

    x_fm = p.sb(f"{L}_xfm", [128, max(DC * Ttot, 15360)], BF16)
    pre = p.sb(f"{L}_pre", [128, max(8 * W3, 16384)], BF16)
    q_tm = p.sb(f"{L}_qtm", [128, n_tc * D], BF16)
    k_tm = p.sb(f"{L}_ktm", [128, n_tc * D], BF16)
    v_tm = p.sb(f"{L}_vtm", [128, n_tc * D], BF16)
    gk_tm = p.sb(f"{L}_gktm", [128, n_tc * H], F32)
    beta_tm = p.sb(f"{L}_btm", [128, n_tc * H], F32)
    S_acc = p.sb(f"{L}_S", [128, 8 * DKH], F32)
    S_bf = p.sb(f"{L}_Sbf", [128, 8 * DKH], BF16)
    wsl = [p.sb(f"{L}_wsl{i}", [128, DC * 128], BF16) for i in range(2)]
    wosb = p.sb(f"{L}_wosb", [128, 8 * D], BF16)
    wgsl = None
    wgk_sb = p.sb(f"{L}_wgks", [128, DC * H], BF16)
    wb_sb = p.sb(f"{L}_wbs", [128, DC * H], BF16)
    cw_sb = p.sb(f"{L}_cws", [128, 96], F32)
    ropec = p.sb(f"{L}_rc", [128, n_tc * 16], F32)
    ropes = p.sb(f"{L}_rs", [128, n_tc * 16], F32)
    xhalo_sb = p.sb(f"{L}_xh", [128, DC * 3], BF16)
    gn_sb = p.sb(f"{L}_gns", [128, 1], F32)
    ps_sb = p.sb(f"{L}_pss", [128, 1], F32)
    dtb_sb = p.sb(f"{L}_dtbs", [H, 1], F32)
    nega_sb = p.sb(f"{L}_negas", [H, 1], F32)
    bb_sb = p.sb(f"{L}_bbs", [1, H], BF16)
    ident_bf = p.sb("identbf", [128, 128], BF16)
    ident_f = p.sb("identf", [128, 128], F32)
    maskI = p.sb(f"{L}_mIs", [128, 128], BF16)
    maskSn = p.sb(f"{L}_mSns", [128, 128], BF16)
    maskLn = p.sb(f"{L}_mLns", [128, 128], BF16)
    pairsel = p.sb(f"{L}_psels", [H, 128], F32)
    pairdg = p.sb(f"{L}_pdgs", [H, 8], F32)
    sel16 = p.sb(f"{L}_sels", [H, H * 128], F32)
    identE = p.sb(f"{L}_idEs", [H, H], F32)
    ones1 = p.sb("ones1", [1, 128], BF16)
    xfm_c = p.sb(f"{L}_xfmc", [128, D], BF16)
    mIf_sb = p.sb(f"{L}_mIfs", [128, 128], F32)
    Gfm_c = p.sb(f"{L}_Gfm", [H, 128], F32)
    bfm_c = p.sb(f"{L}_bfm", [H, 128], F32)
    st16 = p.sb(f"{L}_st16", [128, 16], F32)
    st16b = p.sb(f"{L}_st16b", [128, 16], F32)
    st16c = p.sb(f"{L}_st16c", [128, 16], F32)
    hcsc = p.sb(f"{L}_hcsc", [128, 32], F32)
    ebcp = p.sb(f"{L}_ebcp", [128, 8], F32)
    eda = p.sb(f"{L}_eda", [H, 16], F32)
    tsc = p.sb(f"{L}_tsc", [128, 512], BF16)
    x_scr = p.sb(f"{L}_xscr", [128, D], F32)
    s1 = x_scr[:, 512:1024]
    cscr = xfm_c[:, :].bitcast(F32)
    lnscr = dict(junk=xfm_c, st=p.sb(f"{L}_lst", [128, 16], F32))

    AR = x_fm[:, :]
    sc_sb = AR[:, 0:2048]
    MB_sb = AR[:, 2048:4096]
    Dst_sb = AR[:, 4096:6144]
    DmI_sb = AR[:, 6144:8192]
    DmS_sb = AR[:, 8192:10240]
    B_sb = AR[:, 10240:11264]
    U_sb = AR[:, 11264:12288]
    o_tmc = AR[:, 12288:13312]
    ofm_c = AR[:, 13312:14336]
    qgfm_c = AR[:, 14336:15360]
    gs_d = p.dram_tmp(f"{L}_gsd", [Ttot, D], BF16)
    AR2 = pre[:, :]
    Mst = AR2[:, 0:2048]
    Mmath = AR2[:, 2048:4096]
    M2st = AR2[:, 4096:6144]
    M2math = AR2[:, 6144:8192]
    bbc_sb = AR2[:, 8192:10240]
    kgfm_c = AR2[:, 10240:11264]
    tmp_tm = AR2[:, 11264:12288]
    ktb_c = AR2[:, 12288:13312]
    kfm_c = AR2[:, 13312:14336]
    qfm_c = AR2[:, 14336:15360]
    gs_c = AR2[:, 15360:16384]
    gs2_c = AR[:, 15360:16384]

    wgsl = wsl
    pAB = p.ps("pAB", [128, 2048], F32)
    pCD = p.ps("pCD", [128, 2048], F32)
    pQ = [pAB[:, 0:1024], pAB[:, 1024:2048], pCD[:, 0:1024], pCD[:, 1024:2048]]
    pA, pB, pC, pD = pQ

    p.dma(ident_bf[:, :], identc[:, :], q="pool", name="ld_ib")
    p.dma(ident_f[:, :], identc[:, :], q="sp", name="ld_if")
    p.dma(maskI[:, :], maskI_d[:, :], q="pool", name="ld_mI")
    p.dma(mIf_sb[:, :], maskI_d[:, :], q="sp", name="ld_mIf")
    p.dma(maskSn[:, :], maskSn_d[:, :], q="pool", name="ld_mSn")
    p.dma(maskLn[:, :], maskLn_d[:, :], q="pool", name="ld_mLn")
    p.dma(pairsel[:, :], pairsel_d[:, :], q="pool", name="ld_psel")
    p.dma(pairdg[:, :], pairdg_d[:, :], q="sp", name="ld_pdg")
    p.dma(sel16[:, :], sel16_d[:, :], q="pool", name="ld_sel")
    p.dma(identE[:, :], identE_d[:, :], q="sp", name="ld_idE")
    p.dma(wgk_sb[:, :], wgkb[:, :], q="pool", name="ld_wgk")
    p.dma(wb_sb[:, :], wbb[:, :], q="pool", name="ld_wb")
    p.dma(cw_sb[:, :], convw[:, :], q="sp", name="ld_cw")
    p.dma(gn_sb[:, :], gnorm_d[:, :], q="sp", name="ld_gn")
    p.dma(ps_sb[:, :], pscale_d[:, :], q="sp", name="ld_ps")
    p.dma(dtb_sb[:, :], dtb_d[:, :], q="sp", name="ld_dtb")
    p.dma(nega_sb[:, :], alog_d[:, :], q="sp", name="ld_alog")
    p.dma(bb_sb[:, :], bb_d[:, :], q="pool", name="ld_bb")
    p.memset(ones1[0:1, :], 1.0, name="ones")
    p.act(nega_sb[:, :], nega_sb[:, :], AF.Exp, name="negA_e")
    p.ts(nega_sb[:, :], nega_sb[:, :], -1.0, name="negA_n")
    p.memset(S_acc[:, :], 0.0, name="S0")
    p.memset(S_bf[:, :], 0.0, name="Sbf0")

    def cast_transpose_half(x_dram, tag):
        def src(tc):
            p.dma(x_scr[:, :], x_dram[tc * 128:(tc + 1) * 128, :], q="sp", name=f"{tag}lx{tc}")
            return x_scr[:, :]
        emit_cast_transpose_fm(p, src, x_fm, range(n_tc), Ttot, xfm_c,
                               pD, ident_bf, f"{tag}xt")

    def proj_conv_tensor(which, wblob, cw_off, dst_tm, halo_sb, tag):
        nt5 = Ttot // 512
        for oc in range(8):
            sl = wsl[oc % 2]
            p.dma(sl[:, :], wblob[oc, :, :], q="pool", name=f"{tag}d{oc}")
            phh = pA[:, 0:3]
            p.mm(phh, [(sl[:, dc * 128:(dc + 1) * 128], halo_sb[:, dc * 3:(dc + 1) * 3])
                       for dc in range(DC)], name=f"{tag}h{oc}")
            p.copy(pre[:, oc * W3 + HOF: oc * W3 + HOF + 3], phh, eng="act", name=f"{tag}he{oc}")
            for t5 in range(nt5):
                ph = pQ[1 + t5 % 2]
                ps_ = ph[:, 0:512]
                p.mm(ps_, [(sl[:, dc * 128:(dc + 1) * 128],
                            x_fm[:, dc * Ttot + t5 * 512: dc * Ttot + t5 * 512 + 512])
                           for dc in range(DC)], name=f"{tag}mm{oc}_{t5}")
                p.copy(pre[:, oc * W3 + DOF + t5 * 512: oc * W3 + DOF + t5 * 512 + 512],
                       ps_, eng="act", name=f"{tag}pe{oc}_{t5}")
            for t5 in range(nt5):
                base = oc * W3 + DOF + t5 * 512
                p.ts(cscr[:, :], pre[:, base - 3: base - 3 + 512],
                     cw_sb[:, cw_off + oc * 4: cw_off + oc * 4 + 1], op0=ALU.mult,
                     name=f"{tag}c0_{oc}_{t5}")
                for kk in range(1, 4):
                    p.stt(cscr[:, :], pre[:, base - 3 + kk: base - 3 + kk + 512],
                          cw_sb[:, cw_off + oc * 4 + kk: cw_off + oc * 4 + kk + 1],
                          cscr[:, :], op0=ALU.mult, op1=ALU.add, name=f"{tag}c{kk}_{oc}_{t5}")
                p.act(tsc[:, :], cscr[:, :], AF.Silu, name=f"{tag}si{oc}_{t5}")
                pTb = pD[:, :].bitcast(BF16)
                for i in range(4):
                    p.trans(pTb[:, i * 128:(i + 1) * 128], tsc[:, i * 128:(i + 1) * 128],
                            ident_bf[:, :], name=f"{tag}tr{oc}_{t5}_{i}")
                p.copy(dst_tm[:, 0:n_tc * D].rearrange("p (tc d) -> p tc d", tc=n_tc)
                       [:, t5 * 4:(t5 + 1) * 4, oc * 128:(oc + 1) * 128],
                       pTb[:, 0:512].rearrange("p (i t) -> p i t", i=4),
                       name=f"{tag}te{oc}_{t5}")

    def gkbeta_pass(tag):
        nt5 = Ttot // 512
        for t5 in range(nt5):
            psg = pA[0:H, 0:512]
            pairs = [(wgk_sb[:, dc * H:(dc + 1) * H],
                      x_fm[:, dc * Ttot + t5 * 512: dc * Ttot + t5 * 512 + 512])
                     for dc in range(DC)]
            p.mm(psg, pairs, name=f"{tag}gkmm{t5}")
            p.act(s1[0:H, :], psg, AF.Abs, bias=dtb_sb[0:H, 0:1], name=f"{tag}gka{t5}")
            p.act(s1[0:H, :], s1[0:H, :], AF.Exp, scale=-1.0, name=f"{tag}gke{t5}")
            p.act(s1[0:H, :], s1[0:H, :], AF.Ln, bias=1.0, name=f"{tag}gkl{t5}")
            p.act(cscr[0:H, :], psg, AF.Relu, bias=dtb_sb[0:H, 0:1], name=f"{tag}gkr{t5}")
            p.tt(cscr[0:H, :], cscr[0:H, :], s1[0:H, :], ALU.add, name=f"{tag}gks{t5}")
            p.ts(cscr[0:H, :], cscr[0:H, :], nega_sb[0:H, 0:1], op0=ALU.mult, name=f"{tag}gkn{t5}")
            for i in range(4):
                ptt = pB[:, 0:16]
                p.trans(ptt, cscr[0:H, i * 128:(i + 1) * 128], identE[:, :], name=f"{tag}gkt{t5}_{i}")
                tc = t5 * 4 + i
                p.copy(gk_tm[:, tc * H:(tc + 1) * H], ptt, name=f"{tag}gkte{t5}_{i}")
        for tc in range(n_tc):
            psb_ = pC[:, 0:H]
            pairs = [(x_fm[:, dc * Ttot + tc * 128: dc * Ttot + tc * 128 + 128],
                      wb_sb[:, dc * H:(dc + 1) * H]) for dc in range(DC)]
            pairs.append((ones1[0:1, :], bb_sb[0:1, :]))
            p.mm(psb_, pairs, name=f"{tag}bmm{tc}")
            p.act(beta_tm[:, tc * H:(tc + 1) * H], psb_, AF.Sigmoid, name=f"{tag}bsig{tc}")

    def l2_rope(dst_tm, tag, rc=None, rs=None):
        rc = ropec if rc is None else rc
        rs = ropes if rs is None else rs
        """l2norm (per head) + rope(32) in TM, in place, per tc."""
        for tc in range(n_tc):
            t3 = dst_tm[:, tc * D:(tc + 1) * D].rearrange("p (h d) -> p h d", h=H)
            p.act(xfm_c[:, :], dst_tm[:, tc * D:(tc + 1) * D], AF.Square, name=f"{tag}sq{tc}")
            p.red(st16[:, :], xfm_c[:, :].rearrange("p (h d) -> p h d", h=H), ALU.add,
                  name=f"{tag}nrm{tc}")
            p.ts(st16[:, :], st16[:, :], 1e-6, op0=ALU.add, name=f"{tag}ne{tc}")
            p.recip(st16[:, :], st16[:, :], name=f"{tag}nr{tc}")
            p.act(st16[:, :], st16[:, :], AF.Sqrt, name=f"{tag}ns{tc}")
            p.tt(t3, t3, st16[:, :].unsqueeze(2).broadcast_to([128, H, 64]), ALU.mult,
                 name=f"{tag}nm{tc}")
            x1 = t3[:, :, 0:16]
            x2 = t3[:, :, 16:32]
            cosb = rc[:, tc * 16:(tc + 1) * 16].unsqueeze(1).broadcast_to([128, H, 16])
            sinb = rs[:, tc * 16:(tc + 1) * 16].unsqueeze(1).broadcast_to([128, H, 16])
            sA = s1[:, 0:256].rearrange("p (h d) -> p h d", h=H)
            sB = cscr[:, 0:256].rearrange("p (h d) -> p h d", h=H)
            p.tt(sA, x1, sinb, ALU.mult, name=f"{tag}r1{tc}")
            p.tt(sB, x1, cosb, ALU.mult, name=f"{tag}r2{tc}")
            p.tt(x1, x2, sinb, ALU.mult, name=f"{tag}r3{tc}")
            p.tt(x1, sB, x1, ALU.subtract, name=f"{tag}r4{tc}")
            p.tt(x2, x2, cosb, ALU.mult, name=f"{tag}r5{tc}")
            p.tt(x2, x2, sA, ALU.add, name=f"{tag}r6{tc}")

    def chunk_pass(tc, do_out, tag, x_dram=None, x_outd=None):
        T = tag
        pG3 = pB[:, 0:16]
        p.mm(pG3, [(mIf_sb[:, :], gk_tm[:, tc * H:(tc + 1) * H])], name=f"{T}Gmm")
        p.copy(st16[:, :], pG3, name=f"{T}Gtm")
        pG2 = pA[0:H, 0:128]
        p.trans(pG2, st16[:, :], ident_f[:, :], name=f"{T}Gt")
        p.copy(Gfm_c[:, :], pG2, name=f"{T}Gf")
        p.ts(eda[:, 0:1], Gfm_c[:, 127:128], 1.0, op0=ALU.mult, name=f"{T}GL")
        p.ts(bfm_c[:, :], Gfm_c[:, :], eda[:, 0:1], op0=ALU.subtract, name=f"{T}E2a")
        p.act(bfm_c[:, :], bfm_c[:, :], AF.Exp, scale=-1.0, name=f"{T}E2e")
        pG4 = pC[:, 0:16]
        p.trans(pG4, bfm_c[:, :], identE[:, :], name=f"{T}E2t")
        p.tt(lnscr["st"][:, 0:16], beta_tm[:, tc * H:(tc + 1) * H], pG4, ALU.mult,
             name=f"{T}fb")
        if (not do_out) and tc == 0:
            p.dbg("pfb0", lnscr["st"][:, 0:16])
            p.dbg("pGtm0", st16[:, :])
        p.act(st16c[:, :], st16[:, :], AF.Exp, name=f"{T}eG")
        p.act(eda[:, 0:1], eda[:, 0:1], AF.Exp, name=f"{T}eL")
        p.tt(eda[:, 8:16], pairdg[:, :], eda[:, 0:1].broadcast_to([H, 8]), ALU.mult,
             name=f"{T}eD2")
        pEB = pD[:, 0:8]
        p.mm(pEB, [(pairsel[:, :], eda[:, 8:16])], name=f"{T}ebm")
        p.copy(ebcp[:, :], pEB, name=f"{T}ebe")
        pTb = pD[:, :].bitcast(BF16)
        for oc in range(8):
            p.trans(pTb[:, oc * 128:(oc + 1) * 128],
                    k_tm[:, tc * D + oc * 128: tc * D + oc * 128 + 128], ident_bf[:, :],
                    name=f"{T}ktr{oc}")
        p.copy(kfm_c[:, 0:1024], pTb[:, 0:1024], eng="act", name=f"{T}kte")
        eGb = st16c[:, :].unsqueeze(2).broadcast_to([128, H, 64])
        fbb = lnscr["st"][:, 0:16].unsqueeze(2).broadcast_to([128, H, 64])
        k3 = k_tm[:, tc * D:(tc + 1) * D].rearrange("p (h d) -> p h d", h=H)
        p.tt(tmp_tm[:, :].rearrange("p (h d) -> p h d", h=H), k3, eGb, ALU.mult,
             name=f"{T}kg")
        p.ts(tmp_tm[:, :], tmp_tm[:, :], -1.0, name=f"{T}kgn")
        p.tt(ktb_c[:, :].rearrange("p (h d) -> p h d", h=H), k3, fbb, ALU.mult,
             name=f"{T}ktb")
        if (not do_out) and tc == 0:
            p.dbg("pkg0", tmp_tm[:, :])
            p.dbg("peG0", st16c[:, :])
            p.dbg("pktmc0", k_tm[:, 0:D])
        for oc in range(8):
            p.trans(pTb[:, oc * 128:(oc + 1) * 128],
                    tmp_tm[:, oc * 128:(oc + 1) * 128], ident_bf[:, :], name=f"{T}kgt{oc}")
        p.copy(kgfm_c[:, 0:1024], pTb[:, 0:1024], eng="act", name=f"{T}kge")
        if do_out:
            for oc in range(8):
                p.trans(pTb[:, oc * 128:(oc + 1) * 128],
                        q_tm[:, tc * D + oc * 128: tc * D + oc * 128 + 128], ident_bf[:, :],
                        name=f"{T}qtr{oc}")
            p.copy(qfm_c[:, 0:1024], pTb[:, 0:1024], eng="act", name=f"{T}qte")
            q3 = q_tm[:, tc * D:(tc + 1) * D].rearrange("p (h d) -> p h d", h=H)
            p.tt(tmp_tm[:, :].rearrange("p (h d) -> p h d", h=H), q3, eGb, ALU.mult,
                 name=f"{T}qg")
            for oc in range(8):
                p.trans(pTb[:, oc * 128:(oc + 1) * 128],
                        tmp_tm[:, oc * 128:(oc + 1) * 128], ident_bf[:, :], name=f"{T}qgt{oc}")
            p.copy(qgfm_c[:, 0:1024], pTb[:, 0:1024], eng="act", name=f"{T}qge")
        pG5 = pB[0:H, 0:128]
        p.trans(pG5, beta_tm[:, tc * H:(tc + 1) * H], ident_f[:, :], name=f"{T}bt")
        p.copy(bfm_c[:, :], pG5, name=f"{T}bte")
        for h in range(H):
            px = pQ[h % 2]
            slot = (h // 2) * 128
            r0 = (h % 2) * 64
            cs = (h // 2) * 128
            p.mm(px[:, slot:slot + 128],
                 [(kfm_c[r0:r0 + 64, cs:cs + 128], kfm_c[r0:r0 + 64, cs:cs + 128])],
                 name=f"{T}skk{h}")
        p.copy(sc_sb[:, 0:2048], pAB[:, 0:2048], eng="act", name=f"{T}skke")
        if do_out:
            for h in range(H):
                px = pQ[2 + h % 2]
                slot = (h // 2) * 128
                r0 = (h % 2) * 64
                cs = (h // 2) * 128
                p.mm(px[:, slot:slot + 128],
                     [(kfm_c[r0:r0 + 64, cs:cs + 128], qfm_c[r0:r0 + 64, cs:cs + 128])],
                     name=f"{T}skq{h}")
        p.copy(hcsc[:, 0:16].rearrange("p (q a) -> p q a", q=2),
               st16[:, :].rearrange("p (a q) -> p q a", q=2), name=f"{T}ghc")
        p.copy(hcsc[:, 16:32].rearrange("p (q a) -> p q a", q=2),
               beta_tm[:, tc * H:(tc + 1) * H].rearrange("p (a q) -> p q a", q=2),
               name=f"{T}bhc")
        g_bc = hcsc[:, 0:16].unsqueeze(2).broadcast_to([128, 16, 128])
        b_bc = hcsc[:, 16:32].unsqueeze(2).broadcast_to([128, 16, 128])
        for h in range(H):
            px = pQ[h % 2]
            slot = (h // 2) * 128
            p.mm(px[:, slot:slot + 128], [(sel16[:, h * 128:(h + 1) * 128], Gfm_c[:, :])],
                 name=f"{T}cbG{h}")
        p.tt(Dst_sb[:, 0:2048].rearrange("p (hc t) -> p hc t", hc=16),
             pAB[:, 0:2048].rearrange("p (hc t) -> p hc t", hc=16), g_bc,
             ALU.subtract, name=f"{T}dsc")
        p.ts(Dst_sb[:, 0:2048], Dst_sb[:, 0:2048], 0.0, op0=ALU.min, name=f"{T}dmn")
        p.act(Dst_sb[:, 0:2048], Dst_sb[:, 0:2048], AF.Exp, name=f"{T}dst")
        mi3 = maskI[:, :].unsqueeze(1).broadcast_to([128, 16, 128])
        ms3 = maskSn[:, :].unsqueeze(1).broadcast_to([128, 16, 128])
        dsv = Dst_sb[:, 0:2048].rearrange("p (hc t) -> p hc t", hc=16)
        p.tt(DmI_sb[:, 0:2048].rearrange("p (hc t) -> p hc t", hc=16), dsv, mi3,
             ALU.mult, name=f"{T}dmi")
        p.tt(DmS_sb[:, 0:2048].rearrange("p (hc t) -> p hc t", hc=16), dsv, ms3,
             ALU.mult, name=f"{T}dms")
        p.tt(Mst[:, 0:2048], sc_sb[:, 0:2048], DmS_sb[:, 0:2048], ALU.mult,
             name=f"{T}msa")
        p.tt(Mst[:, 0:2048].rearrange("p (hc t) -> p hc t", hc=16),
             Mst[:, 0:2048].rearrange("p (hc t) -> p hc t", hc=16), b_bc,
             ALU.mult, name=f"{T}msb")
        if do_out:
            p.tt(MB_sb[:, 0:2048], pCD[:, 0:2048], DmI_sb[:, 0:2048],
                 ALU.mult, name=f"{T}mba")
            p.tt(MB_sb[:, 0:2048].rearrange("p (hc t) -> p hc t", hc=16),
                 MB_sb[:, 0:2048].rearrange("p (hc t) -> p hc t", hc=16), b_bc,
                 ALU.mult, name=f"{T}mbb")
        for h in range(H):
            px = pQ[2 + h % 2]
            slot = (h // 2) * 128
            p.mm(px[:, slot:slot + 128], [(sel16[:, h * 128:(h + 1) * 128], Gfm_c[:, :])],
                 name=f"{T}cbG2{h}")
        p.tt(Dst_sb[:, 0:2048].rearrange("p (hc t) -> p hc t", hc=16),
             pCD[:, 0:2048].rearrange("p (hc t) -> p hc t", hc=16), g_bc,
             ALU.subtract, name=f"{T}dtc")
        p.ts(Dst_sb[:, 0:2048], Dst_sb[:, 0:2048], 0.0, op0=ALU.max, name=f"{T}dmx")
        p.act(Dst_sb[:, 0:2048], Dst_sb[:, 0:2048], AF.Exp, scale=-1.0, name=f"{T}dts")
        for h in range(H):
            px = pQ[h % 2]
            slot = (h // 2) * 128
            p.mm(px[:, slot:slot + 128],
                 [(sel16[:, h * 128:(h + 1) * 128], bfm_c[:, :])], name=f"{T}cbB{h}")
        p.copy(bbc_sb[:, 0:2048], pAB[:, 0:2048], eng="act", name=f"{T}cbBe")
        ml3 = maskLn[:, :].unsqueeze(1).broadcast_to([128, 16, 128])
        p.tt(Mmath[:, 0:2048].rearrange("p (hc t) -> p hc t", hc=16),
             Dst_sb[:, 0:2048].rearrange("p (hc t) -> p hc t", hc=16), ml3,
             ALU.mult, name=f"{T}mm1")
        p.tt(Mmath[:, 0:2048], Mmath[:, 0:2048], sc_sb[:, 0:2048], ALU.mult, name=f"{T}mm2")
        p.tt(Mmath[:, 0:2048], Mmath[:, 0:2048], bbc_sb[:, 0:2048], ALU.mult, name=f"{T}mm3")
        for h in range(H):
            px = pQ[h % 2]
            r0 = (h % 2) * 64
            sl_ = px[:, (h // 2) * 64:(h // 2) * 64 + 64]
            p.mmx([(sl_, ident_bf[:, :], v_tm[:, tc * D + h * 64: tc * D + h * 64 + 64],
                    True, False),
                   (sl_, kgfm_c[r0:r0 + 64, (h // 2) * 128:(h // 2) * 128 + 128],
                    S_bf[r0:r0 + 64, (h // 2) * 64:(h // 2) * 64 + 64], False, True)],
                  reads=[ident_bf[:, :], v_tm[:, :], kgfm_c[:, :], S_bf[:, :]],
                  writes=[sl_], name=f"{T}B{h}")
        p.copy(B_sb[:, 0:1024].rearrange("p (q w) -> p q w", q=2),
               pAB[:, 0:2048].rearrange("p (q w) -> p q w", q=2)[:, :, 0:512],
               eng="act", name=f"{T}Be")
        cm, cs_, nm_, ns_ = Mmath, Mst, M2math, M2st
        src_x, dst_x = B_sb, U_sb
        for lev in range(7):
            for h in range(H):
                px = pQ[h % 2]
                hc = ((h % 2) * 8 + h // 2)
                p.mm1(px[:, (h // 2) * 64:(h // 2) * 64 + 64],
                      cs_[:, hc * 128:hc * 128 + 128],
                      src_x[:, hc * 64:hc * 64 + 64], True, True, name=f"{T}ap{lev}_{h}")
            p.tt(dst_x[:, 0:1024].rearrange("p (q w) -> p q w", q=2),
                 src_x[:, 0:1024].rearrange("p (q w) -> p q w", q=2),
                 pAB[:, 0:2048].rearrange("p (q w) -> p q w", q=2)[:, :, 0:512],
                 ALU.add, name=f"{T}ape{lev}")
            src_x, dst_x = dst_x, src_x
            if lev < 6:
                for h in range(H):
                    hc = ((h % 2) * 8 + h // 2)
                    px = pQ[2 + h % 2]
                    p.mm(px[:, (h // 2) * 128:(h // 2) * 128 + 128],
                         [(cs_[:, hc * 128:hc * 128 + 128], cm[:, hc * 128:hc * 128 + 128])],
                         name=f"{T}sqm{lev}_{h}")
                p.copy(nm_[:, 0:2048], pCD[:, 0:2048], eng="act", name=f"{T}sqme{lev}")
                for h in range(H):
                    hc = ((h % 2) * 8 + h // 2)
                    px = pQ[2 + h % 2]
                    p.mm(px[:, (h // 2) * 128:(h // 2) * 128 + 128],
                         [(cm[:, hc * 128:hc * 128 + 128], cs_[:, hc * 128:hc * 128 + 128])],
                         name=f"{T}sqs{lev}_{h}")
                p.copy(ns_[:, 0:2048], pCD[:, 0:2048], eng="act", name=f"{T}sqse{lev}")
                cm, cs_, nm_, ns_ = nm_, ns_, cm, cs_
        U_fin = src_x
        if (not do_out) and tc == 0:
            p.dbg("pU0", U_fin[:, 0:1024])
            p.dbg("pMst0", Mst[:, 0:2048])
            p.dbg("pMmath0", Mmath[:, 0:2048])
            p.dbg("pB0", B_sb[:, 0:1024])
            p.dbg("pktb0", ktb_c[:, :])
        if do_out and tc == 0:
            p.dbg("U0", U_fin[:, 0:1024])
            p.dbg("G0", Gfm_c[:, :])
            p.dbg("Mst0", Mst[:, 0:2048])
            p.dbg("MB0", MB_sb[:, 0:2048])
            p.dbg("sc0", sc_sb[:, 0:2048])
            p.dbg("B0", B_sb[:, 0:1024])
        if do_out:
            for h in range(H):
                px = pQ[2 + h % 2]
                hc = ((h % 2) * 8 + h // 2)
                r0 = (h % 2) * 64
                p.mmx([(px[:, (h // 2) * 64:(h // 2) * 64 + 64],
                        MB_sb[:, hc * 128:hc * 128 + 128], U_fin[:, hc * 64:hc * 64 + 64],
                        True, False),
                       (px[:, (h // 2) * 64:(h // 2) * 64 + 64],
                        qgfm_c[r0:r0 + 64, (h // 2) * 128:(h // 2) * 128 + 128],
                        S_bf[r0:r0 + 64, (h // 2) * 64:(h // 2) * 64 + 64], False, True)],
                      reads=[MB_sb, U_fin, qgfm_c[:, :], S_bf[:, :]],
                      writes=[px[:, (h // 2) * 64:(h // 2) * 64 + 64]], name=f"{T}O{h}")
        for h in range(H):
            px = pQ[h % 2]
            p.mm1(px[0:64, (h // 2) * 64:(h // 2) * 64 + 64],
                  ktb_c[:, h * 64:(h + 1) * 64],
                  U_fin[:, ((h % 2) * 8 + h // 2) * 64:((h % 2) * 8 + h // 2) * 64 + 64],
                  True, True, name=f"{T}su{h}")
        for q in range(2):
            r0 = q * 64
            Sg = S_acc[r0:r0 + 64, :].rearrange("p (a d) -> p a d", a=8)
            eb = ebcp[r0:r0 + 64, :].unsqueeze(2).broadcast_to([64, 8, 64])
            p.tt(Sg, Sg, eb, ALU.mult, name=f"{T}ssm{q}")
            p.tt(S_acc[r0:r0 + 64, :], S_acc[r0:r0 + 64, :], pQ[q][0:64, 0:512],
                 ALU.add, name=f"{T}ssa{q}")
        p.copy(S_bf[:, :], S_acc[:, :], name=f"{T}sbf")
        if (not do_out) and tc == 0:
            p.dbg("pS1", S_acc[:, :])
        if do_out:
            for hh in range(2):
                px = pQ[2 + hh]
                dst3 = o_tmc[:, :].rearrange("p (oc two d) -> p oc two d", oc=8, two=2)[:, :, hh, :]
                p.copy(dst3, px[:, 0:512].rearrange("p (a b) -> p a b", a=8),
                       name=f"{T}oe{hh}")
            p.act(xfm_c[:, :], o_tmc[:, :], AF.Square, name=f"{T}osq")
            p.red(st16b[:, :], xfm_c[:, :].rearrange("p (h d) -> p h d", h=H), ALU.add,
                  name=f"{T}orn")
            p.ts(st16b[:, :], st16b[:, :], 1.0 / 64.0, 1e-5, op0=ALU.mult, op1=ALU.add,
                 name=f"{T}ore")
            p.recip(st16b[:, :], st16b[:, :], name=f"{T}orr")
            p.act(st16b[:, :], st16b[:, :], AF.Sqrt, name=f"{T}ors")
            p.tt(o_tmc[:, :].rearrange("p (h d) -> p h d", h=H),
                 o_tmc[:, :].rearrange("p (h d) -> p h d", h=H),
                 st16b[:, :].unsqueeze(2).broadcast_to([128, H, 64]), ALU.mult,
                 name=f"{T}orm")
            p.dma(x_scr[:, :], x_dram[tc * 128:(tc + 1) * 128, :], q="sp", name=f"{T}gx")
            gbuf = gs_c if tc % 2 == 0 else gs2_c
            p.dma(gbuf[:, 0:1024], gs_d[tc * 128:(tc + 1) * 128, :], q="act",
                  name=f"{T}gl")
            pTb2 = pD[:, :].bitcast(BF16)
            if tc == 0:
                p.dbg("orms0", o_tmc[:, :])
            p.tt(o_tmc[:, :], o_tmc[:, :], gbuf[:, 0:1024], ALU.mult, name=f"{T}og")
            if tc == 0:
                p.dbg("o0", o_tmc[:, :])
            for oc in range(8):
                p.trans(pTb2[:, oc * 128:(oc + 1) * 128], o_tmc[:, oc * 128:(oc + 1) * 128],
                        ident_bf[:, :], name=f"{T}ot{oc}")
            p.copy(ofm_c[:, 0:1024], pTb2[:, 0:1024], eng="act", name=f"{T}ote")
            for dh in range(2):
                pso = pB[:, dh * 512:(dh + 1) * 512]
                p.mm(pso, [(ofm_c[:, oc * 128:(oc + 1) * 128],
                            wosb[:, oc * D + dh * 512: oc * D + dh * 512 + 512])
                           for oc in range(8)], name=f"{T}op{dh}")
                p.tt(x_scr[:, dh * 512:(dh + 1) * 512], x_scr[:, dh * 512:(dh + 1) * 512],
                     pso, ALU.add, name=f"{T}ra{dh}")
            ln_stats_apply(p, x_scr[:, :], x_scr[:, :], lnscr, f"{T}ln")
            p.dma(x_outd[tc * 128:(tc + 1) * 128, :], x_scr[:, :], q="sp", name=f"{T}sx")

    def half_pass(x_dram, halo_sb_src, do_out, tag, is_prefix=False, x_outd=None):
        p.dma(ropec[:, :], ropecp_d[:, :] if is_prefix else ropec_d[:, :], q="sp", name=f"{tag}lrc")
        p.dma(ropes[:, :], ropesp_d[:, :] if is_prefix else ropes_d[:, :], q="sp", name=f"{tag}lrs")
        p.dma(xhalo_sb[:, :], halo_sb_src[:, :], q="pool", name=f"{tag}lh")
        cast_transpose_half(x_dram, tag)
        proj_conv_tensor("q", wq, 0, q_tm, xhalo_sb, f"{tag}q") if do_out else None
        proj_conv_tensor("k", wk, 32, k_tm, xhalo_sb, f"{tag}k")
        proj_conv_tensor("v", wv, 64, v_tm, xhalo_sb, f"{tag}v")
        gkbeta_pass(tag)
        if do_out:
            l2_rope(q_tm, f"{tag}lrq")
        l2_rope(k_tm, f"{tag}lrk")
        if do_out:
            p.dbg("qtm", q_tm[:, :])
            p.dbg("ktm", k_tm[:, :])
            p.dbg("vtm", v_tm[:, :])
            p.dbg("gktm", gk_tm[:, :])
            p.dbg("btm", beta_tm[:, :])
        if do_out:
            p.dma(wosb[:, :].rearrange("p (a b) -> p a b", a=8),
                  wo[:, :, :].rearrange("a p b -> p a b"), q="pool", name=f"{tag}lwo")
            p.ts(wosb[:, :], wosb[:, :], gn_sb[:, 0:1], op0=ALU.mult, name=f"{tag}gn")
            for dc in range(DC):
                p.dma(AR2[:, dc * 1024:(dc + 1) * 1024], wg[dc, :, :], q="pool",
                      name=f"{tag}gw{dc}")
            for tc in range(n_tc):
                gstage = gs_c if tc % 2 == 0 else AR2[:, 8192:9216]
                for oh in range(2):
                    psg = pQ[oh][:, 0:512]
                    pairs = [(x_fm[:, dc * Ttot + tc * 128: dc * Ttot + tc * 128 + 128],
                              AR2[:, dc * 1024 + oh * 512: dc * 1024 + oh * 512 + 512])
                             for dc in range(DC)]
                    p.mm(psg, pairs, name=f"{tag}gp{tc}_{oh}")
                    p.act(gstage[:, oh * 512:(oh + 1) * 512], psg, AF.Silu,
                          name=f"{tag}gsv{tc}_{oh}")
                p.dma(gs_d[tc * 128:(tc + 1) * 128, :], gstage[:, 0:1024], q="act",
                      name=f"{tag}gst{tc}")
        for tc in range(n_tc):
            chunk_pass(tc, do_out, f"{tag}c{tc}", x_dram=x_dram, x_outd=x_outd)

    half_pass(xprev_in, xhalo_p, xprev_out is not None, f"{L}p",
              is_prefix=True, x_outd=xprev_out)
    p.ts(S_acc[:, :], S_acc[:, :], ps_sb[:, 0:1], op0=ALU.mult, name=f"{L}psc")
    p.copy(S_bf[:, :], S_acc[:, :], name=f"{L}psbf")
    p.dbg("Spre", S_acc[:, :])
    half_pass(x_in, xhalo_m, True, f"{L}m", is_prefix=False, x_outd=x_out)


def host_pack_attn0(inputs, core_pos0=0, halo_main=None, pscale=0.0, n_tc=16):
    d = {}
    Wq = np.asarray(inputs["L0_attn_Wq"], np.float32)
    Wk = np.asarray(inputs["L0_attn_Wk"], np.float32)
    Wv = np.asarray(inputs["L0_attn_Wv"], np.float32)
    Wg = np.asarray(inputs["L0_attn_Wg"], np.float32)
    Wo = np.asarray(inputs["L0_attn_Wo"], np.float32)
    d["L0a_wq"] = pack_proj_oc(Wq)
    d["L0a_wk"] = pack_proj_oc(Wk)
    d["L0a_wv"] = pack_proj_oc(Wv)
    d["L0a_wg"] = pack_proj_dc(Wg)
    d["L0a_wo"] = pack_proj_dc(Wo)
    d["L0a_wgk"] = np.ascontiguousarray(
        np.asarray(inputs["L0_attn_Wgk"], np.float32).reshape(DC, 128, H)
        .transpose(1, 0, 2).reshape(128, DC * H))
    d["L0a_wb"] = np.ascontiguousarray(
        np.asarray(inputs["L0_attn_Wb"], np.float32).reshape(DC, 128, H)
        .transpose(1, 0, 2).reshape(128, DC * H))
    cw = np.zeros((128, 96), np.float32)
    for i, nm in enumerate(("conv_q_w", "conv_k_w", "conv_v_w")):
        w = np.asarray(inputs[f"L0_attn_{nm}"], np.float32)
        cw[:, i * 32:(i + 1) * 32] = w.reshape(8, 128, 4).transpose(1, 0, 2).reshape(128, 32)
    d["L0a_cw"] = cw
    d["L0a_dtb"] = np.asarray(inputs["L0_attn_dt_bias"], np.float32).reshape(H, 1)
    d["L0a_alog"] = np.asarray(inputs["L0_attn_A_log"], np.float32).reshape(H, 1)
    d["L0a_bb"] = np.asarray(inputs["L0_attn_bb"], np.float32).reshape(1, H)
    d["L0a_gn"] = np.tile(np.asarray(inputs["L0_attn_gnorm_w"], np.float32), 2).reshape(128, 1)
    Ttot = n_tc * 128
    inv = 10000.0 ** (-np.arange(16) / 16.0)

    def _tab(p0):
        ang = (p0 + np.arange(Ttot))[:, None] * inv[None, :]
        c = np.ascontiguousarray(np.cos(ang).astype(np.float32)
                                 .reshape(n_tc, 128, 16).transpose(1, 0, 2).reshape(128, n_tc * 16))
        s_ = np.ascontiguousarray(np.sin(ang).astype(np.float32)
                                  .reshape(n_tc, 128, 16).transpose(1, 0, 2).reshape(128, n_tc * 16))
        return c, s_
    d["L0a_ropec"], d["L0a_ropes"] = _tab(core_pos0)
    d["L0a_ropecp"], d["L0a_ropesp"] = _tab(max(core_pos0 - Ttot, 0))
    d["L0a_xhp"] = np.zeros((128, DC * 3), np.float32)
    if halo_main is None:
        d["L0a_xhm"] = np.zeros((128, DC * 3), np.float32)
    else:
        d["L0a_xhm"] = np.ascontiguousarray(
            halo_main.T.reshape(DC, 128, 3).transpose(1, 0, 2).reshape(128, DC * 3)).astype(np.float32)
    d["L0a_ps"] = np.full((128, 1), pscale, np.float32)
    s = np.arange(128)
    d["L0a_mI"] = (s[:, None] <= s[None, :]).astype(np.float32)
    d["L0a_mSn"] = -(s[:, None] < s[None, :]).astype(np.float32)
    d["L0a_mLn"] = -(s[:, None] > s[None, :]).astype(np.float32)
    ps_ = np.zeros((H, 128), np.float32)
    for h in range(H):
        ps_[h, :] = ((np.arange(128) >= 64) == (h % 2 == 1)).astype(np.float32)
    d["L0a_psel"] = ps_
    pd_ = np.zeros((H, 8), np.float32)
    for h in range(H):
        pd_[h, h // 2] = 1.0
    d["L0a_pdg"] = pd_
    sel = np.zeros((H, H * 128), np.float32)
    for h in range(H):
        sel[h, h * 128:(h + 1) * 128] = 1.0
    d["L0a_sel"] = sel
    d["L0a_idE"] = np.eye(H, dtype=np.float32)
    d["identc"] = np.eye(128, dtype=np.float32)
    return d
